# version 17
# baseline (speedup 1.0000x reference)
import numpy as np
import ml_dtypes
from contextlib import ExitStack

import concourse.bass as bass
import concourse.mybir as mybir
from concourse.bass_utils import run_bass_kernel_spmd

F32 = mybir.dt.float32
BF16 = mybir.dt.bfloat16
AF = mybir.ActivationFunctionType
ALU = mybir.AluOpType

T = 2048
E = 1024
HID = 4096
H = 16
D = 64
CH = 128
NT = 16
NE = 8
NHID = 32
NSPAN = 4
NSLAB = 2
LNEPS = 1e-5
ATEPS = 1e-6

ENGS = ("tensor", "vector", "scalar", "gpsimd", "sync")


class Sem:
    __slots__ = ("name", "n", "h")

    def __init__(self, name):
        self.name = name
        self.n = 0
        self.h = None


class Buf:
    __slots__ = ("name", "pw", "readers", "wave_open")

    def __init__(self, name):
        self.name = name
        self.pw = []
        self.readers = {}
        self.wave_open = False


class Prog:
    def __init__(self):
        self.ops = {e: [] for e in ENGS}
        self.lastw = {e: {} for e in ENGS}
        self.esem = {e: Sem("sem_" + e) for e in ENGS}
        self.qsem = {}
        self.nwaits = 0
        self.nops = 0
        self._pmr = []

    def dsem(self, name):
        if name not in self.qsem:
            self.qsem[name] = Sem(name)
        return self.qsem[name]

    def sems(self):
        return list(self.esem.values()) + list(self.qsem.values())

    def _wait(self, eng, mark):
        if mark is None:
            return
        sem, val = mark
        if val <= 0:
            return
        lw = self.lastw[eng]
        if lw.get(sem.name, 0) >= val:
            return
        lw[sem.name] = val
        self.nwaits += 1
        self.ops[eng].append(lambda e, s=sem, v=val: e.wait_ge(s.h, v))

    def op(self, eng, fn, reads=(), writes=(), wmode="full", dma=None,
           extra_waits=()):
        waits = list(extra_waits)
        for b in reads:
            waits.extend(b.pw)
        opening = []
        for b in writes:
            if wmode == "full" or not b.wave_open:
                waits.extend(b.pw)
                for s, v in b.readers.items():
                    waits.append((s, v))
                opening.append(b)
        wmax = {}
        for m in waits:
            if m is None:
                continue
            s, v = m
            if wmax.get(s.name, (None, 0))[1] < v:
                wmax[s.name] = (s, v)
        for s, v in wmax.values():
            self._wait(eng, (s, v))
        self.nops += 1
        if wmode == "mid":
            mark = None
            self.ops[eng].append(fn)
        else:
            sem = self.dsem(dma) if dma else self.esem[eng]
            inck = 16 if dma else 1
            sem.n += inck
            mark = (sem, sem.n)
            self.ops[eng].append(
                lambda e, fn=fn, s=sem, v=inck: fn(e).then_inc(s.h, v))
        for b in opening:
            b.pw = []
            b.readers = {}
            b.wave_open = True
        for b in writes:
            if mark is not None:
                if wmode == "full":
                    b.pw = [mark]
                else:
                    b.pw.append(mark)
        for b in reads:
            b.wave_open = False
            if mark is not None:
                s, v = mark
                if b.readers.get(s, 0) < v:
                    b.readers[s] = v
        if mark is None and reads:
            self._pmr.append((eng, list(reads)))
        elif mark is not None and self._pmr:
            rest = []
            for peng, bufs in self._pmr:
                if peng == eng:
                    s, v = mark
                    for b in bufs:
                        if b.readers.get(s, 0) < v:
                            b.readers[s] = v
                else:
                    rest.append((peng, bufs))
            self._pmr = rest
        return mark

    def seal(self, bufs):
        mx = {}
        for b in bufs:
            for s, v in b.pw:
                if mx.get(s.name, (None, 0))[1] < v:
                    mx[s.name] = (s, v)
        combined = list(mx.values())
        for b in bufs:
            b.pw = list(combined)

    def claim(self, news, olds):
        for nb in news:
            for ob in olds:
                nb.pw.extend(ob.pw)
                for s, v in ob.readers.items():
                    if nb.readers.get(s, 0) < v:
                        nb.readers[s] = v

    def replay(self, block):
        for ename in ENGS:
            ops = list(self.ops[ename])
            if ops:
                getattr(block, ename)(lambda eng, ops=ops: [f(eng) for f in ops])


def build_nc():
    nc = bass.Bass("TRN2", target_bir_lowering=False)

    x_d = nc.dram_tensor("x", [T, E], F32, kind="ExternalInput").ap()
    wq_d = nc.dram_tensor("wq", [E, E], F32, kind="ExternalInput").ap()
    wk_d = nc.dram_tensor("wk", [E, E], F32, kind="ExternalInput").ap()
    wv_d = nc.dram_tensor("wv", [E, E], F32, kind="ExternalInput").ap()
    wp_d = nc.dram_tensor("wp", [E, E], F32, kind="ExternalInput").ap()
    bp_d = nc.dram_tensor("bp", [E], F32, kind="ExternalInput").ap()
    w1_d = nc.dram_tensor("w1", [E, HID], F32, kind="ExternalInput").ap()
    w2_d = nc.dram_tensor("w2", [E, HID], F32, kind="ExternalInput").ap()
    h1_d = nc.dram_tensor("h1", [HID, HID], F32, kind="ExternalInput").ap()
    h2_d = nc.dram_tensor("h2", [HID, HID], F32, kind="ExternalInput").ap()
    w3_d = nc.dram_tensor("w3", [HID, E], F32, kind="ExternalInput").ap()
    s1_d = nc.dram_tensor("s1", [HID], F32, kind="ExternalInput").ap()
    s2_d = nc.dram_tensor("s2", [HID], F32, kind="ExternalInput").ap()
    sh1_d = nc.dram_tensor("sh1", [HID], F32, kind="ExternalInput").ap()
    sh2_d = nc.dram_tensor("sh2", [HID], F32, kind="ExternalInput").ap()
    s3_d = nc.dram_tensor("s3", [E], F32, kind="ExternalInput").ap()
    y_d = nc.dram_tensor("y", [T, E], F32, kind="ExternalOutput").ap()
    w1_s = nc.dram_tensor("w1_s", [E, HID], BF16, kind="Internal").ap()
    w2_s = nc.dram_tensor("w2_s", [E, HID], BF16, kind="Internal").ap()
    h1_s = nc.dram_tensor("h1_s", [HID, HID], BF16, kind="Internal").ap()
    h2_s = nc.dram_tensor("h2_s", [HID, HID], BF16, kind="Internal").ap()
    w3_s = nc.dram_tensor("w3_s", [HID, E], BF16, kind="Internal").ap()
    x2_d = nc.dram_tensor("x2_d", [T, E], F32, kind="Internal").ap()

    P = Prog()

    base = (nc.sbuf_base + 127) // 128 * 128
    off = [base]

    def alloc(name, shape, dt, at=None):
        if at is None:
            at = off[0]
            nbytes = int(np.prod(shape[1:])) * mybir.dt.size(dt)
            off[0] = (at + nbytes + 127) // 128 * 128
        t = nc.alloc_sbuf_tensor_at(name, shape, dt, offset=at)
        return t

    ident_bf = alloc("ident_bf", [128, 128], BF16)
    ident_f32 = alloc("ident_f32", [128, 128], F32)
    mask_f = alloc("mask_f", [128, 128], F32)
    ones_bf = alloc("ones_bf", [128, 16], BF16)
    ones_f1 = alloc("ones_f1", [1, 128], F32)
    BPt = alloc("BPt", [128, E], F32)
    S3t = alloc("S3t", [128, E], F32)
    sc_stage = alloc("sc_stage", [32, 128], F32)
    s1c = alloc("s1c", [128, 32], F32)
    s2c = alloc("s2c", [128, 32], F32)
    h1c = alloc("h1c", [128, 32], F32)
    h2c = alloc("h2c", [128, 32], F32)
    S32 = alloc("S32", [128, 8 * 65], F32)
    Sbf = alloc("Sbf", [128, 8 * 65], BF16)
    st_su = alloc("st_su", [128, 8], F32)
    st_ss = alloc("st_ss", [128, 8], F32)
    st_m = alloc("st_m", [128, 8], F32)
    st_m2 = alloc("st_m2", [128, 8], F32)
    st_v = alloc("st_v", [128, 8], F32)
    st_r = alloc("st_r", [128, 8], F32)
    st_nm = alloc("st_nm", [128, 8], F32)
    den_sb = alloc("den_sb", [128, 16], F32)
    rec_sb = alloc("rec_sb", [128, 16], F32)
    nd_sb = alloc("nd_sb", [128, 8 * 130], F32)
    epsc = alloc("epsc", [128, 8], F32)
    PERM_END = off[0]

    RX = PERM_END
    xslot = [alloc(f"xslot{i}", [128, E], F32, at=RX + i * 4096) for i in range(8)]
    htb = [alloc(f"htb{i}", [128, 8192], BF16, at=RX + i * 16384) for i in range(2)]
    outb = [alloc(f"outb{i}", [128, E], F32, at=RX + (2 + i) * 4096) for i in range(2)]
    RX_END = RX + 8 * 4096

    RS = RX_END
    scr0 = [alloc(f"scr0{i}", [128, E], F32, at=RS + i * 4096) for i in range(2)]
    bp_row = alloc("bp_row", [1, E], F32, at=RS)
    s3_row = alloc("s3_row", [1, E], F32, at=RS + 4096)
    scrT = alloc("scrT", [128, E], F32, at=RS + 8192)
    eluR = [alloc(f"eluR{i}", [128, 512], F32, at=RS + 12288 + i * 2048) for i in range(2)]
    eluM = [alloc(f"eluM{i}", [128, 512], F32, at=RS + 16384 + i * 2048) for i in range(2)]
    eluE = [alloc(f"eluE{i}", [128, 512], F32, at=RS + 20480 + i * 2048) for i in range(2)]
    scrS = [alloc(f"scrS{i}", [128, 512], F32, at=RS + 12288 + i * 2048) for i in range(2)]
    fo_t = [alloc(f"fo{i}", [128, E], BF16, at=RS + 16384 + i * 2048) for i in range(4)]
    RS_END = RS + 24576

    RB = RS_END
    wT = {}
    for i, nm in enumerate(("wq", "wk", "wv", "wp")):
        wT[nm] = alloc(nm + "T", [128, 8192], BF16, at=RB + i * 16384)
    ln0_fm = alloc("ln0_fm", [128, 4096], BF16, at=RB + 65536)
    wstage = alloc("wstage", [128, 4096], F32, at=RB + 65536)
    phiq_fm = alloc("phiq_fm", [128, 4096], BF16, at=RB + 65536 + 8192)
    phik_fm = alloc("phik_fm", [128, 4096], BF16, at=RB + 65536 + 16384)
    vT_sb = alloc("vT_sb", [128, 4096], BF16, at=RB + 65536 + 24576)
    y_fm = alloc("y_fm", [128, 4096], BF16, at=RB + 65536 + 32768)
    phikT = [alloc(f"phikT{i}", [128, 1024], BF16, at=RB + 106496 + i * 2048)
             for i in range(2)]
    y_tok = [alloc(f"y_tok{i}", [128, 1024], BF16, at=RB + 110592 + i * 2048)
             for i in range(2)]
    ATm = [alloc(f"ATm{i}", [128, 256], BF16, at=RB + 114688 + i * 512)
           for i in range(2)]
    ao_tok = [alloc(f"ao_tok{i}", [128, 1024], BF16, at=RB + 115712 + i * 2048)
              for i in range(2)]
    ln0_tok = [alloc(f"ln0_tok{i}", [128, 1024], BF16, at=RB + 119808 + i * 2048)
               for i in range(4)]
    h_fm = alloc("h_fm", [128, 32768], BF16, at=RB + 0)
    g2_fm = alloc("g2_fm", [128, 32768], BF16, at=RB + 0)
    g1_fm = alloc("g1_fm", [128, 32768], BF16, at=RB + 65536)
    w3T_sb = alloc("w3T_sb", [128, 32768], BF16, at=RB + 65536)
    ln2_fm = alloc("ln2_fm", [128, 8192], BF16, at=RB + 65536)
    ln2_tok = [alloc(f"ln2_tok{i}", [128, 1024], BF16, at=RB + 81920 + i * 2048)
               for i in range(8)]
    w12b = [alloc(f"w12b{i}", [128, 8192], BF16, at=RB + 98304 + i * 16384)
            for i in range(2)]
    RB_END = RB + 131072
    assert RB_END - base <= 212800, f"SBUF overflow: {RB_END - base}"

    psF = [nc.alloc_psum_tensor(f"psF{i}", [128, 512], F32) for i in range(4)]
    psG = [nc.alloc_psum_tensor(f"psG{i}", [128, 512], F32) for i in range(2)]
    psT = [nc.alloc_psum_tensor(f"psT{i}", [128, 1024], BF16) for i in range(2)]

    bF = [Buf(f"psF{i}") for i in range(4)]
    bG = [Buf(f"psG{i}") for i in range(2)]
    bT = [Buf(f"psT{i}") for i in range(2)]
    b_ident = Buf("ident_bf")
    b_identf = Buf("ident_f32")
    b_mask = Buf("mask")
    b_ones = Buf("ones_bf")
    b_ones1 = Buf("ones_f1")
    b_bprow = Buf("bp_row")
    b_s3row = Buf("s3_row")
    b_BP = Buf("BPt")
    b_S3 = Buf("S3t")
    b_scstage = Buf("sc_stage")
    b_s1c, b_s2c, b_h1c, b_h2c = Buf("s1c"), Buf("s2c"), Buf("h1c"), Buf("h2c")
    b_S32 = [Buf(f"S32_{i}") for i in range(8)]
    b_Sbf = [Buf(f"Sbf_{i}") for i in range(8)]
    b_stats = [Buf(f"stats{i}") for i in range(8)]
    b_den = Buf("den")
    b_rec = Buf("rec")
    b_ndsb = [Buf(f"ndsb{i}") for i in range(8)]
    b_epsc = Buf("epsc")
    b_fo = [Buf(f"fo{i}") for i in range(4)]
    b_xslot = [Buf(f"xslot{i}") for i in range(8)]
    b_htb = [Buf(f"htb{i}") for i in range(2)]
    b_outb = [Buf(f"outb{i}") for i in range(2)]
    b_scr0 = [Buf(f"scr0{i}") for i in range(2)]
    b_scrT = Buf("scrT")
    b_eluR = [Buf(f"eluR{i}") for i in range(2)]
    b_eluM = [Buf(f"eluM{i}") for i in range(2)]
    b_eluE = [Buf(f"eluE{i}") for i in range(2)]
    b_scrS = [Buf(f"scrS{i}") for i in range(2)]
    b_wT = {nm: [Buf(f"{nm}T{k}") for k in range(NE)] for nm in ("wq", "wk", "wv", "wp")}
    b_wstage = Buf("wstage")
    b_ln0fm = [Buf(f"ln0fm{c}") for c in range(NE)]
    b_phiq = [Buf(f"phiq{c}") for c in range(NE)]
    b_phik = [Buf(f"phik{c}") for c in range(NE)]
    b_vT = [Buf(f"vT{i}") for i in range(4)]
    b_yfm = [Buf(f"yfm{c}") for c in range(NE)]
    b_phikT = [Buf(f"phikT{i}") for i in range(2)]
    b_ytok = [Buf(f"ytok{i}") for i in range(2)]
    b_ATm = [Buf(f"ATm{i}") for i in range(2)]
    b_ao = [Buf(f"ao{i}") for i in range(2)]
    b_ln0tok = [Buf(f"ln0tok{i}") for i in range(4)]
    b_hfm = [Buf(f"hfm{k}") for k in range(NHID)]
    b_g1 = [Buf(f"g1_{k}") for k in range(NHID)]
    b_g2 = [Buf(f"g2_{k}") for k in range(NHID)]
    b_w3T = [Buf(f"w3T{k}") for k in range(NHID)]
    b_ln2fm = [Buf(f"ln2fm{c}") for c in range(NE)]
    b_ln2tok = [Buf(f"ln2tok{i}") for i in range(8)]
    b_w12b = [Buf(f"w12b{i}") for i in range(2)]
    b_ws = {nm: Buf(nm + "_s") for nm in ("w1", "w2", "h1", "h2", "w3")}
    b_x2d = [Buf(f"x2d{i}") for i in range(NT)]
    b_yd = [Buf(f"yd{i}") for i in range(NT)]

    ws_map = {"w1": (w1_d, w1_s), "w2": (w2_d, w2_s),
              "h1": (h1_d, h1_s), "h2": (h2_d, h2_s), "w3": (w3_d, w3_s)}

    P.op("gpsimd", lambda e: e.memset(ident_bf[:, :], 0.0), writes=[b_ident])
    P.op("gpsimd", lambda e: e.affine_select(
        out=ident_bf[:, :], in_=ident_bf[:, :], compare_op=ALU.not_equal,
        fill=1.0, base=0, pattern=[[-1, 128]], channel_multiplier=1),
        reads=[b_ident], writes=[b_ident])
    P.op("gpsimd", lambda e: e.memset(ident_f32[:, :], 0.0), writes=[b_identf])
    P.op("gpsimd", lambda e: e.affine_select(
        out=ident_f32[:, :], in_=ident_f32[:, :], compare_op=ALU.not_equal,
        fill=1.0, base=0, pattern=[[-1, 128]], channel_multiplier=1),
        reads=[b_identf], writes=[b_identf])
    P.op("gpsimd", lambda e: e.memset(mask_f[:, :], 0.0), writes=[b_mask])
    P.op("gpsimd", lambda e: e.affine_select(
        out=mask_f[:, :], in_=mask_f[:, :], compare_op=ALU.is_gt, fill=1.0,
        base=0, pattern=[[-1, 128]], channel_multiplier=1),
        reads=[b_mask], writes=[b_mask])
    P.op("gpsimd", lambda e: e.memset(ones_bf[:, :], 1.0), writes=[b_ones])
    P.op("gpsimd", lambda e: e.memset(ones_f1[:, :], 1.0), writes=[b_ones1])
    P.op("gpsimd", lambda e: e.memset(epsc[:, :], 100000.0), writes=[b_epsc])
    P.op("vector", lambda e: e.reciprocal(epsc[:, 0:1], epsc[:, 1:2]),
         reads=[b_epsc], writes=[b_epsc])
    P.op("gpsimd", lambda e: e.memset(S32[:, :], 0.0), writes=b_S32)
    P.op("gpsimd", lambda e: e.memset(Sbf[:, :], 0.0), writes=b_Sbf)


    def load_x(tt):
        sl = tt % 8
        P.op("sync", lambda e, tt=tt, sl=sl: e.dma_start(
            out=xslot[sl][:, :], in_=x_d[tt * 128:(tt + 1) * 128, :]),
            writes=[b_xslot[sl]], dma=f"qx{sl}")

    for tt in range(4):
        load_x(tt)
    P.op("sync", lambda e: e.dma_start(out=bp_row[:, :], in_=bp_d[None, :]),
         writes=[b_bprow], dma="qsc_bp")
    P.op("sync", lambda e: e.dma_start(out=s3_row[:, :], in_=s3_d[None, :]),
         writes=[b_s3row], dma="qsc_s3")
    for si, (sd, tb, bb) in enumerate(((s1_d, s1c, b_s1c), (s2_d, s2c, b_s2c),
                                       (sh1_d, h1c, b_h1c), (sh2_d, h2c, b_h2c))):
        P.op("sync", lambda e, sd=sd: e.dma_start(
            out=sc_stage[:, :], in_=sd.rearrange("(c p) -> c p", p=128)),
            writes=[b_scstage], dma=f"qsc_{si}")
        P.op("tensor", lambda e: e.transpose(
            psF[0][:, 0:32], sc_stage[:, :], ident_f32[0:32, 0:32]),
            reads=[b_scstage, b_identf], writes=[bF[0]])
        P.op("vector", lambda e, tb=tb: e.tensor_copy(tb[:, :], psF[0][:, 0:32]),
             reads=[bF[0]], writes=[bb])
    for row, tile, bt, brow in ((bp_row, BPt, b_BP, b_bprow),
                                (s3_row, S3t, b_S3, b_s3row)):
        for half in range(2):
            P.op("tensor", lambda e, row=row, half=half: e.matmul(
                psF[1][:, 0:512], ones_f1[:, :], row[:, half * 512:(half + 1) * 512],
                start=True, stop=True),
                reads=[b_ones1, brow], writes=[bF[1]])
            P.op("vector", lambda e, tile=tile, half=half: e.tensor_copy(
                tile[:, half * 512:(half + 1) * 512], psF[1][:, 0:512]),
                reads=[bF[1]], writes=[bt], wmode="part")

    P.claim([b_scr0[0]], [b_bprow])
    P.claim([b_scr0[1]], [b_s3row])
    P.claim(b_ln0fm + b_phiq, [b_wstage])

    gate = [(P.dsem(f"qw_{nm}"), P.dsem(f"qw_{nm}").n)
            for nm in ("wq", "wk", "wv", "wp")]
    gate += [(P.dsem(f"qx{i}"), P.dsem(f"qx{i}").n) for i in range(8)]
    _first_cast = [True]

    def cast_weight(nm):
        src_, dst = ws_map[nm]
        rows = src_.shape[0]
        nchunk = 4 if rows * src_.shape[1] >= HID * HID else 1
        rr = rows // nchunk
        for j in range(nchunk):
            ew = gate if _first_cast[0] else ()
            _first_cast[0] = False
            P.op("gpsimd", lambda e, src_=src_, dst=dst, j=j, rr=rr: e.dma_start(
                out=dst[j * rr:(j + 1) * rr, :], in_=src_[j * rr:(j + 1) * rr, :]),
                writes=[b_ws[nm]], wmode=("part" if nchunk > 1 else "full"),
                dma="qc_" + nm, extra_waits=ew)

    for nm in ("w1", "w2", "h1", "h2", "w3"):
        cast_weight(nm)

    def ln_stats_apply(src_tile, b_src, stat, b_stat, out_tile, b_out, sc,
                       apply_eng="vector"):
        i = stat
        P.op("scalar", lambda e: e.activation(
            scr0[sc][:, :], src_tile[:, :], AF.Square,
            accum_out=st_ss[:, i:i + 1]),
            reads=[b_src], writes=[b_scr0[sc], b_stat])
        P.op("scalar", lambda e: e.activation(
            scr0[sc][:, :], src_tile[:, :], AF.Copy,
            accum_out=st_su[:, i:i + 1]),
            reads=[b_src], writes=[b_scr0[sc], b_stat])
        P.op("vector", lambda e: e.tensor_scalar(
            st_m[:, i:i + 1], st_su[:, i:i + 1], 1.0 / E, None, ALU.mult),
            reads=[b_stat], writes=[b_stat])
        P.op("vector", lambda e: e.tensor_mul(
            st_m2[:, i:i + 1], st_m[:, i:i + 1], st_m[:, i:i + 1]),
            reads=[b_stat], writes=[b_stat])
        P.op("vector", lambda e: e.scalar_tensor_tensor(
            st_v[:, i:i + 1], st_ss[:, i:i + 1], 1.0 / E, st_m2[:, i:i + 1],
            ALU.mult, ALU.subtract),
            reads=[b_stat], writes=[b_stat])
        P.op("scalar", lambda e: e.activation(
            st_v[:, i:i + 1], st_v[:, i:i + 1], AF.Ln, bias=epsc[:, 0:1]),
            reads=[b_stat, b_epsc], writes=[b_stat])
        P.op("scalar", lambda e: e.activation(
            st_r[:, i:i + 1], st_v[:, i:i + 1], AF.Exp, scale=-0.5),
            reads=[b_stat], writes=[b_stat])
        P.op("vector", lambda e: e.scalar_tensor_tensor(
            st_nm[:, i:i + 1], st_m[:, i:i + 1], -1.0, st_r[:, i:i + 1],
            ALU.mult, ALU.mult),
            reads=[b_stat], writes=[b_stat])
        if out_tile is not None:
            P.op(apply_eng, lambda e: e.tensor_scalar(
                out_tile[:, :], src_tile[:, :], st_r[:, i:i + 1],
                st_nm[:, i:i + 1], ALU.mult, ALU.add),
                reads=[b_src, b_stat], writes=[b_out])

    def transpose_set(src_tiles, b_srcs, col0, dst, b_dst, dcol0, par, n_in=4):
        for j in range(n_in):
            P.op("tensor", lambda e, j=j, col0=col0, par=par: e.transpose(
                psT[par][:, j * 128:(j + 1) * 128],
                src_tiles[j][:, col0:col0 + 128], ident_bf[:, :]),
                reads=[b_srcs[j], b_ident], writes=[bT[par]], wmode="part")
        P.op("vector", lambda e, par=par, dcol0=dcol0, n_in=n_in: e.tensor_copy(
            dst[:, dcol0:dcol0 + n_in * 128], psT[par][:, 0:n_in * 128]),
            reads=[bT[par]], writes=[b_dst], wmode="part")

    def ln0_span(s):
        for tl in range(4):
            tt = 4 * s + tl
            sl = tt % 8
            ln_stats_apply(xslot[sl], b_xslot[sl], tl, b_stats[tl],
                           ln0_tok[tl], b_ln0tok[tl], tl % 2,
                           apply_eng="gpsimd")

    ln0_span(0)
    for nm, wd in (("wq", wq_d), ("wk", wk_d), ("wv", wv_d), ("wp", wp_d)):
        for rnd in range(2):
            ks = range(rnd * 4, rnd * 4 + 4)
            for j, k in enumerate(ks):
                P.op("sync", lambda e, wd=wd, k=k, j=j: e.dma_start(
                    out=wstage[:, j * 1024:(j + 1) * 1024],
                    in_=wd[k * 128:(k + 1) * 128, :]),
                    writes=[b_wstage], wmode=("part" if j else "full"),
                    dma=f"qw_{nm}")
            _qw = P.dsem(f"qw_{nm}")
            b_wstage.pw = [(_qw, _qw.n)]
            for j, k in enumerate(ks):
                P.op("vector", lambda e, nm=nm, k=k, j=j: e.tensor_copy(
                    wT[nm][:, k * 1024:(k + 1) * 1024],
                    wstage[:, j * 1024:(j + 1) * 1024]),
                    reads=[b_wstage], writes=[b_wT[nm][k]])
        P.seal(b_wT[nm])
        if nm == "wk":
            for tt in range(4, 8):
                load_x(tt)

    for s in range(NSPAN):
        for cp in range(4):
            par2 = cp % 2
            for half2 in range(2):
                c = 2 * cp + half2
                for j in range(4):
                    P.op("tensor", lambda e, j=j, c=c, half2=half2, par2=par2: e.transpose(
                        psT[par2][:, half2 * 512 + j * 128:half2 * 512 + (j + 1) * 128],
                        ln0_tok[j][:, c * 128:(c + 1) * 128], ident_bf[:, :]),
                        reads=[b_ln0tok[j], b_ident], writes=[bT[par2]], wmode="part")
            P.op("vector", lambda e, cp=cp, par2=par2: e.tensor_copy(
                ln0_fm[:, cp * 1024:(cp + 1) * 1024], psT[par2][:, :]),
                reads=[bT[par2]], writes=[b_ln0fm[2 * cp], b_ln0fm[2 * cp + 1]],
                wmode="part")
        P.seal(b_ln0fm)
        qbanks = [(psG[0], bG[0]), (psG[1], bG[1]), (psF[0], bF[0]), (psF[1], bF[1])]

        def proj_fm(wname, dst, b_dst, kind):
            for oc in range(NE):
                pg = oc % 2
                psq, bq = qbanks[oc % 4]
                for k in range(NE):
                    P.op("tensor", lambda e, wname=wname, oc=oc, k=k, psq=psq: e.matmul(
                        psq[:, 0:512],
                        wT[wname][:, k * 1024 + oc * 128:k * 1024 + oc * 128 + 128],
                        ln0_fm[:, k * 512:(k + 1) * 512],
                        start=(k == 0), stop=(k == NE - 1)),
                        reads=[b_wT[wname][k], b_ln0fm[k]], writes=[bq],
                        wmode=("full" if k == NE - 1 else "mid"))
                if kind == "phi":
                    P.op("scalar", lambda e, pg=pg, oc=oc, psq=psq: e.activation(
                        eluR[pg][:, :], psq[:, 0:512], AF.Relu),
                        reads=[bq], writes=[b_eluR[pg]])
                    P.op("vector", lambda e, pg=pg, psq=psq: e.tensor_sub(
                        eluM[pg][:, :], psq[:, 0:512], eluR[pg][:, :]),
                        reads=[bq, b_eluR[pg]], writes=[b_eluM[pg]])
                    P.op("scalar", lambda e, pg=pg: e.activation(
                        eluE[pg][:, :], eluM[pg][:, :], AF.Exp),
                        reads=[b_eluM[pg]], writes=[b_eluE[pg]])
                    P.op("vector", lambda e, pg=pg, oc=oc, dst=dst: e.tensor_tensor(
                        dst[:, oc * 512:(oc + 1) * 512], eluR[pg][:, :],
                        eluE[pg][:, :], ALU.add),
                        reads=[b_eluR[pg], b_eluE[pg]], writes=[b_dst[oc]])
                else:
                    P.op("vector", lambda e, pg=pg, oc=oc, dst=dst, psq=psq: e.tensor_copy(
                        dst[:, oc * 512:(oc + 1) * 512], psq[:, 0:512]),
                        reads=[bq], writes=[b_dst[oc]])

        proj_fm("wk", phik_fm, b_phik, "phi")
        P.seal(b_phik)
        proj_fm("wq", phiq_fm, b_phiq, "phi")
        P.seal(b_phiq)

        def v_proj(tl):
            for osp in range(2):
                if tl == 0:
                    psq, bq = qbanks[osp % 4 if False else (tl * 2 + osp) % 4]
                else:
                    psq, bq = psG[(tl * 2 + osp) % 2], bG[(tl * 2 + osp) % 2]
                for k in range(NE):
                    P.op("tensor", lambda e, tl=tl, osp=osp, k=k, psq=psq: e.matmul(
                        psq[:, 0:512],
                        ln0_fm[:, k * 512 + tl * 128:k * 512 + tl * 128 + 128],
                        wT["wv"][:, k * 1024 + osp * 512:k * 1024 + osp * 512 + 512],
                        start=(k == 0), stop=(k == NE - 1)),
                        reads=[b_ln0fm[k], b_wT["wv"][k]], writes=[bq],
                        wmode=("full" if k == NE - 1 else "mid"))
                P.op("vector", lambda e, tl=tl, osp=osp, psq=psq: e.tensor_copy(
                    vT_sb[:, tl * 1024 + osp * 512:tl * 1024 + osp * 512 + 512],
                    psq[:, 0:512]),
                    reads=[bq], writes=[b_vT[tl]], wmode="part")

        v_proj(0)

        def y_transposes(nl):
            parx = nl % 2
            for c in range(NE):
                P.op("tensor", lambda e, c=c, parx=parx: e.transpose(
                    psT[parx][:, c * 128:(c + 1) * 128],
                    y_tok[parx][:, c * 128:(c + 1) * 128], ident_bf[:, :]),
                    reads=[b_ytok[parx], b_ident], writes=[bT[parx]], wmode="part")
            for c in range(NE):
                P.op("vector", lambda e, c=c, parx=parx, nl=nl: e.tensor_copy(
                    y_fm[:, c * 512 + nl * 128:c * 512 + nl * 128 + 128],
                    psT[parx][:, c * 128:(c + 1) * 128]),
                    reads=[bT[parx]], writes=[b_yfm[c]], wmode="part")

        for nl in range(4):
            par = nl % 2
            def emit_AT(hp):
                for hi in range(2):
                    fb = bF[hi]
                    pb = 64 * hi
                    P.op("tensor", lambda e, hp=hp, hi=hi, nl=nl, pb=pb: e.matmul(
                        psF[hi][:, 0:128],
                        phik_fm[pb:pb + 64, hp * 512 + nl * 128:hp * 512 + nl * 128 + 128],
                        phiq_fm[pb:pb + 64, hp * 512 + nl * 128:hp * 512 + nl * 128 + 128],
                        start=True, stop=True),
                        reads=[b_phik[hp], b_phiq[hp]], writes=[fb])
                    P.op("vector", lambda e, hp=hp, hi=hi: e.tensor_tensor(
                        ATm[hp % 2][:, hi * 128:(hi + 1) * 128],
                        psF[hi][:, 0:128], mask_f[:, :], ALU.mult),
                        reads=[fb, b_mask], writes=[b_ATm[hp % 2]], wmode="part")

            emit_AT(0)
            for hp in range(8):
                fn_ = bF[2 + hp % 2]
                if hp + 1 < 8:
                    emit_AT(hp + 1)
                for hi in range(2):
                    h = 2 * hp + hi
                    pb = 64 * hi
                    bs = hi * 65
                    P.op("tensor", lambda e, hp=hp, hi=hi, h=h, nl=nl, bs=bs: e.matmul(
                        psF[2 + hp % 2][:, bs:bs + 64],
                        ATm[hp % 2][:, hi * 128:(hi + 1) * 128],
                        vT_sb[:, nl * 1024 + h * 64:nl * 1024 + h * 64 + 64],
                        start=(hi == 0), stop=False),
                        reads=[b_ATm[hp % 2], b_vT[nl]], writes=[fn_], wmode="mid")
                    P.op("tensor", lambda e, hp=hp, hi=hi, bs=bs: e.matmul(
                        psF[2 + hp % 2][:, bs + 64:bs + 65],
                        ATm[hp % 2][:, hi * 128:(hi + 1) * 128],
                        ones_bf[:, 0:1], start=False, stop=False),
                        reads=[b_ATm[hp % 2], b_ones], writes=[fn_], wmode="mid")
                    P.op("tensor", lambda e, hp=hp, hi=hi, nl=nl, pb=pb, bs=bs: e.matmul(
                        psF[2 + hp % 2][:, bs:bs + 65],
                        phiq_fm[pb:pb + 64, hp * 512 + nl * 128:hp * 512 + nl * 128 + 128],
                        Sbf[pb:pb + 64, hp * 65:hp * 65 + 65],
                        start=False, stop=(hi == 1)),
                        reads=[b_phiq[hp], b_Sbf[hp]], writes=[fn_],
                        wmode=("full" if hi == 1 else "mid"))
                P.op("scalar", lambda e, hp=hp: e.activation(
                    nd_sb[:, hp * 130:(hp + 1) * 130], psF[2 + hp % 2][:, 0:130],
                    AF.Copy),
                    reads=[fn_], writes=[b_ndsb[hp]])
            for c in range(NE):
                P.op("tensor", lambda e, c=c, nl=nl, par=par: e.transpose(
                    psT[par][:, c * 128:(c + 1) * 128],
                    phik_fm[:, c * 512 + nl * 128:c * 512 + nl * 128 + 128],
                    ident_bf[:, :]),
                    reads=[b_phik[c], b_ident], writes=[bT[par]], wmode="part")
            P.op("vector", lambda e, par=par: e.tensor_copy(
                phikT[par][:, :], psT[par][:, :]),
                reads=[bT[par]], writes=[b_phikT[par]])
            for hp in range(8):
                gb = bG[hp % 2]
                for hi in range(2):
                    h = 2 * hp + hi
                    pb = 64 * hi
                    P.op("tensor", lambda e, h=h, hp=hp, hi=hi, nl=nl, pb=pb, par=par: e.matmul(
                        psG[hp % 2][pb:pb + 64, 0:64],
                        phikT[par][:, h * 64:h * 64 + 64],
                        vT_sb[:, nl * 1024 + h * 64:nl * 1024 + h * 64 + 64],
                        start=True, stop=False),
                        reads=[b_phikT[par], b_vT[nl]], writes=[gb], wmode="mid")
                    P.op("tensor", lambda e, h=h, hp=hp, hi=hi, pb=pb, par=par: e.matmul(
                        psG[hp % 2][pb:pb + 64, 64:65],
                        phikT[par][:, h * 64:h * 64 + 64],
                        ones_bf[:, 0:1], start=False, stop=True),
                        reads=[b_phikT[par], b_ones], writes=[gb],
                        wmode="part")
                P.op("vector", lambda e, hp=hp: e.tensor_add(
                    S32[:, hp * 65:hp * 65 + 65], S32[:, hp * 65:hp * 65 + 65],
                    psG[hp % 2][:, 0:65]),
                    reads=[b_S32[hp], gb], writes=[b_S32[hp]])
                P.op("scalar", lambda e, hp=hp: e.activation(
                    Sbf[:, hp * 65:hp * 65 + 65], S32[:, hp * 65:hp * 65 + 65],
                    AF.Copy),
                    reads=[b_S32[hp]], writes=[b_Sbf[hp]])
            if nl + 1 < 4:
                v_proj(nl + 1)
            if nl > 0:
                y_transposes(nl - 1)
            dsrc = bass.AP(nd_sb, 64, [[8 * 130, 128], [130, 8], [65, 2]])
            P.op("scalar", lambda e, dsrc=dsrc: e.activation(
                den_sb[:, 0:16], dsrc, AF.Copy, bias=ATEPS),
                reads=b_ndsb, writes=[b_den])
            P.op("vector", lambda e: e.reciprocal(rec_sb[:, 0:16], den_sb[:, 0:16]),
                 reads=[b_den], writes=[b_rec])
            for h in range(16):
                hp, hi = h // 2, h % 2
                P.op("scalar", lambda e, h=h, hp=hp, hi=hi, par=par: e.activation(
                    y_tok[par][:, h * 64:h * 64 + 64],
                    nd_sb[:, hp * 130 + hi * 65:hp * 130 + hi * 65 + 64], AF.Copy,
                    scale=rec_sb[:, h:h + 1]),
                    reads=[b_ndsb[hp], b_rec], writes=[b_ytok[par]], wmode="part")
        y_transposes(3)
        P.seal(b_yfm)
        if s + 1 < NSPAN:
            ln0_span(s + 1)

        for tl in range(4):
            tt = 4 * s + tl
            sl = tt % 8
            for osp in range(2):
                psq, bq = qbanks[(tl * 2 + osp) % 4]
                for k in range(NE):
                    P.op("tensor", lambda e, tl=tl, osp=osp, k=k, psq=psq: e.matmul(
                        psq[:, 0:512],
                        y_fm[:, k * 512 + tl * 128:k * 512 + tl * 128 + 128],
                        wT["wp"][:, k * 1024 + osp * 512:k * 1024 + osp * 512 + 512],
                        start=(k == 0), stop=(k == NE - 1)),
                        reads=[b_yfm[k], b_wT["wp"][k]], writes=[bq],
                        wmode=("full" if k == NE - 1 else "mid"))
                P.op("vector", lambda e, tl=tl, osp=osp, psq=psq: e.tensor_tensor(
                    ao_tok[tl % 2][:, osp * 512:(osp + 1) * 512],
                    psq[:, 0:512], BPt[:, osp * 512:(osp + 1) * 512], ALU.add),
                    reads=[bq, b_BP], writes=[b_ao[tl % 2]], wmode="part")
            ln_stats_apply(ao_tok[tl % 2], b_ao[tl % 2], 4 + (tl % 2),
                           b_stats[4 + (tl % 2)], None, None, tl % 2)
            i = 4 + (tl % 2)
            P.op("vector", lambda e, tl=tl, i=i: e.tensor_scalar(
                scrT[:, :], ao_tok[tl % 2][:, :], st_r[:, i:i + 1],
                st_nm[:, i:i + 1], ALU.mult, ALU.add),
                reads=[b_ao[tl % 2], b_stats[i]], writes=[b_scrT])
            P.op("vector", lambda e, sl=sl: e.tensor_add(
                xslot[sl][:, :], xslot[sl][:, :], scrT[:, :]),
                reads=[b_xslot[sl], b_scrT], writes=[b_xslot[sl]])
            P.op("scalar", lambda e, tt=tt, sl=sl: e.dma_start(
                out=x2_d[tt * 128:(tt + 1) * 128, :], in_=xslot[sl][:, :]),
                reads=[b_xslot[sl]], writes=[b_x2d[tt]], dma=f"qx2s{sl}")
        if 4 * (s + 2) < NT:
            for tt in range(4 * (s + 2), 4 * (s + 2) + 4):
                load_x(tt)

    for tt in range(NT):
        _qs = P.dsem(f"qx2s{tt % 8}")
        b_x2d[tt].pw = [(_qs, _qs.n)]

    P.claim(b_hfm, sum([b_wT[nm] for nm in ("wq", "wk", "wv", "wp")], []))
    P.claim(b_fo, [b_eluM[0], b_eluM[1], b_eluE[0], b_eluE[1]])
    P.claim(b_ln2fm + b_ln2tok + b_w12b + [b_scrS[0], b_scrS[1]],
            b_ln0fm + b_phiq + b_phik + [b_vT[i] for i in range(4)] +
            b_yfm + b_phikT + b_ytok + b_ATm + b_ao + b_ln0tok +
            [b_eluR[0], b_eluR[1], b_eluM[0], b_eluM[1], b_eluE[0], b_eluE[1]])
    P.claim(b_htb + b_outb, b_xslot)

    x2ld_total = P.dsem("qx2s")

    for sl in range(NSLAB):
        ts0 = sl * 8
        for tl in range(8):
            tt = ts0 + tl
            slx = tl % 2
            P.op("sync", lambda e, tt=tt, slx=slx: e.dma_start(
                out=xslot[slx][:, :], in_=x2_d[tt * 128:(tt + 1) * 128, :]),
                reads=[b_x2d[tt]], writes=[b_xslot[slx]], dma=f"qx2l{slx}")
            ln_stats_apply(xslot[slx], b_xslot[slx], tl % 8, b_stats[tl % 8],
                           ln2_tok[tl], b_ln2tok[tl], tl % 2)
        for c in range(NE):
            par = c % 2
            for j in range(8):
                P.op("tensor", lambda e, c=c, j=j, par=par: e.transpose(
                    psT[par][:, j * 128:(j + 1) * 128],
                    ln2_tok[j][:, c * 128:(c + 1) * 128], ident_bf[:, :]),
                    reads=[b_ln2tok[j], b_ident], writes=[bT[par]], wmode="part")
            P.op("vector", lambda e, c=c, par=par: e.tensor_copy(
                ln2_fm[:, c * 1024:(c + 1) * 1024], psT[par][:, :]),
                reads=[bT[par]], writes=[b_ln2fm[c]])
        P.seal(b_ln2fm)

        def load_w12(osp):
            wb = osp % 2
            P.op("sync", lambda e, osp=osp, wb=wb: e.dma_start(
                out=w12b[wb][:, 0:4096].rearrange("p (k o) -> p k o", k=NE),
                in_=w1_s[:, osp * 512:(osp + 1) * 512]
                .rearrange("(k p) o -> p k o", p=128)),
                reads=[b_ws["w1"]], writes=[b_w12b[wb]], dma=f"qw12{wb}")
            P.op("sync", lambda e, osp=osp, wb=wb: e.dma_start(
                out=w12b[wb][:, 4096:8192].rearrange("p (k o) -> p k o", k=NE),
                in_=w2_s[:, osp * 512:(osp + 1) * 512]
                .rearrange("(k p) o -> p k o", p=128)),
                reads=[b_ws["w2"]], writes=[b_w12b[wb]], wmode="part",
                dma=f"qw12{wb}")

        load_w12(0)
        load_w12(1)
        for osp in range(8):
            wb = osp % 2
            for oc4 in range(4):
                oc = osp * 4 + oc4
                for tc in range(2):
                    pa = 2 * ((oc4 * 2 + tc) % 2)
                    for k in range(NE):
                        P.op("tensor", lambda e, oc4=oc4, tc=tc, k=k, pa=pa, wb=wb: e.matmul(
                            psF[pa][:, 0:512],
                            w12b[wb][:, k * 512 + oc4 * 128:k * 512 + oc4 * 128 + 128],
                            ln2_fm[:, k * 1024 + tc * 512:k * 1024 + tc * 512 + 512],
                            start=(k == 0), stop=(k == NE - 1)),
                            reads=[b_w12b[wb], b_ln2fm[k]], writes=[bF[pa]],
                            wmode=("full" if k == NE - 1 else "mid"))
                    for k in range(NE):
                        P.op("tensor", lambda e, oc4=oc4, tc=tc, k=k, pa=pa, wb=wb: e.matmul(
                            psF[pa + 1][:, 0:512],
                            w12b[wb][:, 4096 + k * 512 + oc4 * 128:4096 + k * 512 + oc4 * 128 + 128],
                            ln2_fm[:, k * 1024 + tc * 512:k * 1024 + tc * 512 + 512],
                            start=(k == 0), stop=(k == NE - 1)),
                            reads=[b_w12b[wb], b_ln2fm[k]], writes=[bF[pa + 1]],
                            wmode=("full" if k == NE - 1 else "mid"))
                    sp = (oc4 * 2 + tc) % 2
                    P.op("scalar", lambda e, oc=oc, pa=pa, sp=sp: e.activation(
                        scrS[sp][:, :], psF[pa][:, 0:512], AF.Silu,
                        scale=s1c[:, oc:oc + 1]),
                        reads=[bF[pa], b_s1c], writes=[b_scrS[sp]])
                    P.op("vector", lambda e, oc=oc, tc=tc, pa=pa, sp=sp: e.scalar_tensor_tensor(
                        h_fm[:, oc * 1024 + tc * 512:oc * 1024 + tc * 512 + 512],
                        scrS[sp][:, :], s2c[:, oc:oc + 1], psF[pa + 1][:, 0:512],
                        ALU.mult, ALU.mult),
                        reads=[b_scrS[sp], b_s2c, bF[pa + 1]], writes=[b_hfm[oc]],
                        wmode="part")
            if osp + 2 < 8:
                load_w12(osp + 2)
        P.seal(b_hfm)

        def hidden_stage(w_s, b_wsrc, scol, b_scol, src_bufs, src_t, dst, b_dst):
            def load_span(osp):
                wb2 = osp % 2
                P.op("sync", lambda e, osp=osp, wb2=wb2, w_s=w_s: e.dma_start(
                    out=htb[wb2][:, :].rearrange("p (k o) -> p k o", k=NHID),
                    in_=w_s[:, osp * 256:(osp + 1) * 256]
                    .rearrange("(k p) o -> p k o", p=128)),
                    reads=[b_wsrc], writes=[b_htb[wb2]], dma=f"qht{wb2}")

            load_span(0)
            load_span(1)
            for osp in range(16):
                wb2 = osp % 2
                for oc2 in range(2):
                    oc = osp * 2 + oc2
                    for tc in range(2):
                        pa = (oc2 * 2 + tc) % 4
                        for k in range(NHID):
                            P.op("tensor", lambda e, oc2=oc2, tc=tc, k=k, pa=pa, wb2=wb2: e.matmul(
                                psF[pa][:, 0:512],
                                htb[wb2][:, k * 256 + oc2 * 128:k * 256 + oc2 * 128 + 128],
                                src_t[:, k * 1024 + tc * 512:k * 1024 + tc * 512 + 512],
                                start=(k == 0), stop=(k == NHID - 1)),
                                reads=[b_htb[wb2], src_bufs[k]], writes=[bF[pa]],
                                wmode=("full" if k == NHID - 1 else "mid"))
                        P.op("scalar", lambda e, oc=oc, tc=tc, pa=pa: e.activation(
                            dst[:, oc * 1024 + tc * 512:oc * 1024 + tc * 512 + 512],
                            psF[pa][:, 0:512], AF.Gelu, scale=scol[:, oc:oc + 1]),
                            reads=[bF[pa], b_scol], writes=[b_dst[oc]], wmode="part")
                if osp + 2 < 16:
                    load_span(osp + 2)

        P.claim(b_g1, b_ln2fm + b_ln2tok + b_w12b)
        P.claim(b_htb, b_xslot + b_outb)
        hidden_stage(h1_s, b_ws["h1"], h1c, b_h1c, b_hfm, h_fm, g1_fm, b_g1)
        P.seal(b_g1)
        P.claim(b_g2, b_hfm)
        hidden_stage(h2_s, b_ws["h2"], h2c, b_h2c, b_g1, g1_fm, g2_fm, b_g2)
        P.seal(b_g2)

        P.seal(b_g1)
        P.claim(b_w3T, b_g1)
        P.op("sync", lambda e: e.dma_start(
            out=w3T_sb[:, :].rearrange("p (k o) -> p k o", k=NHID),
            in_=w3_s.rearrange("(k p) o -> p k o", p=128)),
            reads=[b_ws["w3"]], writes=b_w3T, dma="qw3")
        P.claim([b_xslot[0], b_xslot[1]], b_htb)
        P.claim(b_outb, b_htb)
        for half in range(2):
            loop = ([(osp, tl) for osp in range(4) for tl in range(4)] if half == 0
                    else [(osp, tl) for tl in range(4) for osp in range(4)])
            for idx, (osp, tl) in enumerate(loop):
                    gt = half * 4 + tl
                    pa = idx % 4
                    for k in range(NHID):
                        P.op("tensor", lambda e, gt=gt, osp=osp, k=k, pa=pa: e.matmul(
                            psF[pa][:, 0:256],
                            g2_fm[:, k * 1024 + gt * 128:k * 1024 + gt * 128 + 128],
                            w3T_sb[:, k * 1024 + osp * 256:k * 1024 + osp * 256 + 256],
                            start=(k == 0), stop=(k == NHID - 1)),
                            reads=[b_g2[k], b_w3T[k]], writes=[bF[pa]],
                            wmode=("full" if k == NHID - 1 else "mid"))
                    P.op("vector", lambda e, tl=tl, osp=osp, pa=pa: e.tensor_tensor(
                        fo_t[tl][:, osp * 256:(osp + 1) * 256], psF[pa][:, 0:256],
                        S3t[:, osp * 256:(osp + 1) * 256], ALU.mult),
                        reads=[bF[pa], b_S3], writes=[b_fo[tl]], wmode="part")
            for tl in range(4):
                gt = half * 4 + tl
                tt = ts0 + gt
                slx = tl % 2
                P.op("sync", lambda e, tt=tt, slx=slx: e.dma_start(
                    out=xslot[slx][:, :], in_=x2_d[tt * 128:(tt + 1) * 128, :]),
                    reads=[b_x2d[tt]], writes=[b_xslot[slx]], dma=f"qx2l{slx}")
                ln_stats_apply(fo_t[tl], b_fo[tl], tl % 8, b_stats[tl % 8],
                               None, None, tl % 2)
                i = tl % 8
                P.op("vector", lambda e, tl=tl, i=i: e.tensor_scalar(
                    scrT[:, :], fo_t[tl][:, :], st_r[:, i:i + 1],
                    st_nm[:, i:i + 1], ALU.mult, ALU.add),
                    reads=[b_fo[tl], b_stats[i]], writes=[b_scrT])
                ob = tl % 2
                P.op("vector", lambda e, slx=slx, ob=ob: e.tensor_add(
                    outb[ob][:, :], xslot[slx][:, :], scrT[:, :]),
                    reads=[b_xslot[slx], b_scrT], writes=[b_outb[ob]])
                P.op("gpsimd", lambda e, tt=tt, ob=ob: e.dma_start(
                    out=y_d[tt * 128:(tt + 1) * 128, :], in_=outb[ob][:, :]),
                    reads=[b_outb[ob]], writes=[b_yd[tt]], dma=f"qy{ob}")

        if sl + 1 < NSLAB:
            P.claim(b_hfm, b_g2)
            P.claim(b_ln2fm + b_ln2tok + b_w12b, b_w3T)
            P.claim([b_xslot[0], b_xslot[1]], b_htb + b_outb)

    P.op("gpsimd", lambda e: e.nop(),
         extra_waits=[(P.dsem("qy0"), P.dsem("qy0").n),
                      (P.dsem("qy1"), P.dsem("qy1").n)])

    with ExitStack() as sstk:
        for s in P.sems():
            s.h = sstk.enter_context(nc.semaphore(s.name))
        with nc.Block() as block:
            P.replay(block)
    return nc


_NC_CACHE = []


def _get_nc():
    if not _NC_CACHE:
        _NC_CACHE.append(build_nc())
    return _NC_CACHE[0]


def kernel(L0_attn_x, L0_attn_Wq, L0_attn_Wk, L0_attn_Wv, L0_attn_Wp,
           L0_attn_bp, L0_ffn_w1_w, L0_ffn_w1_s, L0_ffn_w2_w, L0_ffn_w2_s,
           L0_ffn_h1_w, L0_ffn_h1_s, L0_ffn_h2_w, L0_ffn_h2_s, L0_ffn_w3_w,
           L0_ffn_w3_s):
    x = np.ascontiguousarray(np.asarray(L0_attn_x, dtype=np.float32))
    common = {
        "wq": np.ascontiguousarray(np.asarray(L0_attn_Wq, np.float32).T),
        "wk": np.ascontiguousarray(np.asarray(L0_attn_Wk, np.float32).T),
        "wv": np.ascontiguousarray(np.asarray(L0_attn_Wv, np.float32).T),
        "wp": np.ascontiguousarray(np.asarray(L0_attn_Wp, np.float32).T),
        "bp": np.ascontiguousarray(np.asarray(L0_attn_bp, np.float32)),
        "w1": np.ascontiguousarray(np.asarray(L0_ffn_w1_w, np.float32).T),
        "w2": np.ascontiguousarray(np.asarray(L0_ffn_w2_w, np.float32).T),
        "h1": np.ascontiguousarray(np.asarray(L0_ffn_h1_w, np.float32).T),
        "h2": np.ascontiguousarray(np.asarray(L0_ffn_h2_w, np.float32).T),
        "w3": np.ascontiguousarray(np.asarray(L0_ffn_w3_w, np.float32).T),
        "s1": np.ascontiguousarray(np.asarray(L0_ffn_w1_s, np.float32)),
        "s2": np.ascontiguousarray(np.asarray(L0_ffn_w2_s, np.float32)),
        "sh1": np.ascontiguousarray(np.asarray(L0_ffn_h1_s, np.float32)),
        "sh2": np.ascontiguousarray(np.asarray(L0_ffn_h2_s, np.float32)),
        "s3": np.ascontiguousarray(np.asarray(L0_ffn_w3_s, np.float32)),
    }
    nc = _get_nc()
    in_maps = [dict(common, x=np.ascontiguousarray(x[b])) for b in range(8)]
    res = run_bass_kernel_spmd(nc, in_maps, core_ids=list(range(8)))
    return np.stack([res.results[b]["y"] for b in range(8)], axis=0)


# version 18
# speedup vs baseline: 1.1176x; 1.1176x over previous
import numpy as np
import ml_dtypes
from contextlib import ExitStack

import concourse.bass as bass
import concourse.mybir as mybir
from concourse.bass_utils import run_bass_kernel_spmd

F32 = mybir.dt.float32
BF16 = mybir.dt.bfloat16
AF = mybir.ActivationFunctionType
ALU = mybir.AluOpType

T = 2048
E = 1024
HID = 4096
H = 16
D = 64
CH = 128
NT = 16
NE = 8
NHID = 32
NSPAN = 4
NSLAB = 2
LNEPS = 1e-5
ATEPS = 1e-6

ENGS = ("tensor", "vector", "scalar", "gpsimd", "sync")


class Sem:
    __slots__ = ("name", "n", "h")

    def __init__(self, name):
        self.name = name
        self.n = 0
        self.h = None


class Buf:
    __slots__ = ("name", "pw", "readers", "wave_open")

    def __init__(self, name):
        self.name = name
        self.pw = []
        self.readers = {}
        self.wave_open = False


class Prog:
    def __init__(self):
        self.ops = {e: [] for e in ENGS}
        self.lastw = {e: {} for e in ENGS}
        self.esem = {e: Sem("sem_" + e) for e in ENGS}
        self.qsem = {}
        self.nwaits = 0
        self.nops = 0
        self._pmr = []

    def dsem(self, name):
        if name not in self.qsem:
            self.qsem[name] = Sem(name)
        return self.qsem[name]

    def sems(self):
        return list(self.esem.values()) + list(self.qsem.values())

    def _wait(self, eng, mark):
        if mark is None:
            return
        sem, val = mark
        if val <= 0:
            return
        lw = self.lastw[eng]
        if lw.get(sem.name, 0) >= val:
            return
        lw[sem.name] = val
        self.nwaits += 1
        self.ops[eng].append(lambda e, s=sem, v=val: e.wait_ge(s.h, v))

    def op(self, eng, fn, reads=(), writes=(), wmode="full", dma=None,
           extra_waits=()):
        waits = list(extra_waits)
        for b in reads:
            waits.extend(b.pw)
        opening = []
        for b in writes:
            if wmode == "full" or not b.wave_open:
                waits.extend(b.pw)
                for s, v in b.readers.items():
                    waits.append((s, v))
                opening.append(b)
        wmax = {}
        for m in waits:
            if m is None:
                continue
            s, v = m
            if wmax.get(s.name, (None, 0))[1] < v:
                wmax[s.name] = (s, v)
        for s, v in wmax.values():
            self._wait(eng, (s, v))
        self.nops += 1
        if wmode == "mid":
            mark = None
            self.ops[eng].append(fn)
        else:
            sem = self.dsem(dma) if dma else self.esem[eng]
            inck = 16 if dma else 1
            sem.n += inck
            mark = (sem, sem.n)
            self.ops[eng].append(
                lambda e, fn=fn, s=sem, v=inck: fn(e).then_inc(s.h, v))
        for b in opening:
            b.pw = []
            b.readers = {}
            b.wave_open = True
        for b in writes:
            if mark is not None:
                if wmode == "full":
                    b.pw = [mark]
                else:
                    b.pw.append(mark)
        for b in reads:
            b.wave_open = False
            if mark is not None:
                s, v = mark
                if b.readers.get(s, 0) < v:
                    b.readers[s] = v
        if mark is None and reads:
            self._pmr.append((eng, list(reads)))
        elif mark is not None and self._pmr:
            rest = []
            for peng, bufs in self._pmr:
                if peng == eng:
                    s, v = mark
                    for b in bufs:
                        if b.readers.get(s, 0) < v:
                            b.readers[s] = v
                else:
                    rest.append((peng, bufs))
            self._pmr = rest
        return mark

    def seal(self, bufs):
        mx = {}
        for b in bufs:
            for s, v in b.pw:
                if mx.get(s.name, (None, 0))[1] < v:
                    mx[s.name] = (s, v)
        combined = list(mx.values())
        for b in bufs:
            b.pw = list(combined)

    def claim(self, news, olds):
        for nb in news:
            for ob in olds:
                nb.pw.extend(ob.pw)
                for s, v in ob.readers.items():
                    if nb.readers.get(s, 0) < v:
                        nb.readers[s] = v

    def replay(self, block):
        for ename in ENGS:
            ops = list(self.ops[ename])
            if ops:
                getattr(block, ename)(lambda eng, ops=ops: [f(eng) for f in ops])


def build_nc():
    nc = bass.Bass("TRN2", target_bir_lowering=False)

    x_d = nc.dram_tensor("x", [T, E], F32, kind="ExternalInput").ap()
    wq_d = nc.dram_tensor("wq", [E, E], F32, kind="ExternalInput").ap()
    wk_d = nc.dram_tensor("wk", [E, E], F32, kind="ExternalInput").ap()
    wv_d = nc.dram_tensor("wv", [E, E], F32, kind="ExternalInput").ap()
    wp_d = nc.dram_tensor("wp", [E, E], F32, kind="ExternalInput").ap()
    bp_d = nc.dram_tensor("bp", [E], F32, kind="ExternalInput").ap()
    w1_d = nc.dram_tensor("w1", [E, HID], F32, kind="ExternalInput").ap()
    w2_d = nc.dram_tensor("w2", [E, HID], F32, kind="ExternalInput").ap()
    h1_d = nc.dram_tensor("h1", [HID, HID], F32, kind="ExternalInput").ap()
    h2_d = nc.dram_tensor("h2", [HID, HID], F32, kind="ExternalInput").ap()
    w3_d = nc.dram_tensor("w3", [HID, E], F32, kind="ExternalInput").ap()
    s1_d = nc.dram_tensor("s1", [HID], F32, kind="ExternalInput").ap()
    s2_d = nc.dram_tensor("s2", [HID], F32, kind="ExternalInput").ap()
    sh1_d = nc.dram_tensor("sh1", [HID], F32, kind="ExternalInput").ap()
    sh2_d = nc.dram_tensor("sh2", [HID], F32, kind="ExternalInput").ap()
    s3_d = nc.dram_tensor("s3", [E], F32, kind="ExternalInput").ap()
    y_d = nc.dram_tensor("y", [T, E], F32, kind="ExternalOutput").ap()
    w1_s = nc.dram_tensor("w1_s", [E, HID], BF16, kind="Internal").ap()
    w2_s = nc.dram_tensor("w2_s", [E, HID], BF16, kind="Internal").ap()
    h1_s = nc.dram_tensor("h1_s", [HID, HID], BF16, kind="Internal").ap()
    h2_s = nc.dram_tensor("h2_s", [HID, HID], BF16, kind="Internal").ap()
    w3_s = nc.dram_tensor("w3_s", [HID, E], BF16, kind="Internal").ap()
    x2_d = nc.dram_tensor("x2_d", [T, E], F32, kind="Internal").ap()

    P = Prog()

    base = (nc.sbuf_base + 127) // 128 * 128
    off = [base]

    def alloc(name, shape, dt, at=None):
        if at is None:
            at = off[0]
            nbytes = int(np.prod(shape[1:])) * mybir.dt.size(dt)
            off[0] = (at + nbytes + 127) // 128 * 128
        t = nc.alloc_sbuf_tensor_at(name, shape, dt, offset=at)
        return t

    ident_bf = alloc("ident_bf", [128, 128], BF16)
    ident_f32 = alloc("ident_f32", [128, 128], F32)
    mask_f = alloc("mask_f", [128, 128], F32)
    ones_bf = alloc("ones_bf", [128, 16], BF16)
    ones_f1 = alloc("ones_f1", [1, 128], F32)
    BPt = alloc("BPt", [128, E], F32)
    S3t = alloc("S3t", [128, E], F32)
    sc_stage = alloc("sc_stage", [32, 128], F32)
    s1c = alloc("s1c", [128, 32], F32)
    s2c = alloc("s2c", [128, 32], F32)
    h1c = alloc("h1c", [128, 32], F32)
    h2c = alloc("h2c", [128, 32], F32)
    S32 = alloc("S32", [128, 8 * 65], F32)
    Sbf = alloc("Sbf", [128, 8 * 65], BF16)
    st_su = alloc("st_su", [128, 8], F32)
    st_ss = alloc("st_ss", [128, 8], F32)
    st_m = alloc("st_m", [128, 8], F32)
    st_m2 = alloc("st_m2", [128, 8], F32)
    st_v = alloc("st_v", [128, 8], F32)
    st_r = alloc("st_r", [128, 8], F32)
    st_nm = alloc("st_nm", [128, 8], F32)
    den_sb = alloc("den_sb", [128, 16], F32)
    rec_sb = alloc("rec_sb", [128, 16], F32)
    nd_sb = alloc("nd_sb", [128, 8 * 130], F32)
    epsc = alloc("epsc", [128, 8], F32)
    PERM_END = off[0]

    RX = PERM_END
    xslot = [alloc(f"xslot{i}", [128, E], F32, at=RX + i * 4096) for i in range(8)]
    htb = [alloc(f"htb{i}", [128, 8192], BF16, at=RX + i * 16384) for i in range(2)]
    outb = [alloc(f"outb{i}", [128, E], F32, at=RX + (2 + i) * 4096) for i in range(2)]
    RX_END = RX + 8 * 4096

    RS = RX_END
    scr0 = [alloc(f"scr0{i}", [128, E], F32, at=RS + i * 4096) for i in range(2)]
    bp_row = alloc("bp_row", [1, E], F32, at=RS)
    s3_row = alloc("s3_row", [1, E], F32, at=RS + 4096)
    scrT = alloc("scrT", [128, E], F32, at=RS + 8192)
    eluR = [alloc(f"eluR{i}", [128, 512], F32, at=RS + 12288 + i * 2048) for i in range(2)]
    eluM = [alloc(f"eluM{i}", [128, 512], F32, at=RS + 16384 + i * 2048) for i in range(2)]
    eluE = [alloc(f"eluE{i}", [128, 512], F32, at=RS + 20480 + i * 2048) for i in range(2)]
    scrS = [alloc(f"scrS{i}", [128, 512], F32, at=RS + 12288 + i * 2048) for i in range(2)]
    fo_t = [alloc(f"fo{i}", [128, E], BF16, at=RS + 16384 + i * 2048) for i in range(4)]
    RS_END = RS + 24576

    RB = RS_END
    wT = {}
    for i, nm in enumerate(("wq", "wk", "wv", "wp")):
        wT[nm] = alloc(nm + "T", [128, 8192], BF16, at=RB + i * 16384)
    ln0_fm = alloc("ln0_fm", [128, 4096], BF16, at=RB + 65536)
    wstage = alloc("wstage", [128, 4096], F32, at=RB + 65536)
    phiq_fm = alloc("phiq_fm", [128, 4096], BF16, at=RB + 65536 + 8192)
    phik_fm = alloc("phik_fm", [128, 4096], BF16, at=RB + 65536 + 16384)
    vT_sb = alloc("vT_sb", [128, 4096], BF16, at=RB + 65536 + 24576)
    y_fm = alloc("y_fm", [128, 4096], BF16, at=RB + 65536 + 32768)
    phikT = [alloc(f"phikT{i}", [128, 1024], BF16, at=RB + 106496 + i * 2048)
             for i in range(2)]
    y_tok = [alloc(f"y_tok{i}", [128, 1024], BF16, at=RB + 110592 + i * 2048)
             for i in range(2)]
    ATm = [alloc(f"ATm{i}", [128, 256], BF16, at=RB + 114688 + i * 512)
           for i in range(2)]
    ao_tok = [alloc(f"ao_tok{i}", [128, 1024], BF16, at=RB + 115712 + i * 2048)
              for i in range(2)]
    ln0_tok = [alloc(f"ln0_tok{i}", [128, 1024], BF16, at=RB + 119808 + i * 2048)
               for i in range(4)]
    h_fm = alloc("h_fm", [128, 32768], BF16, at=RB + 0)
    g2_fm = alloc("g2_fm", [128, 32768], BF16, at=RB + 0)
    g1_fm = alloc("g1_fm", [128, 32768], BF16, at=RB + 65536)
    w3T_sb = alloc("w3T_sb", [128, 32768], BF16, at=RB + 65536)
    ln2_fm = alloc("ln2_fm", [128, 8192], BF16, at=RB + 65536)
    ln2_tok = [alloc(f"ln2_tok{i}", [128, 1024], BF16, at=RB + 81920 + i * 2048)
               for i in range(8)]
    w12b = [alloc(f"w12b{i}", [128, 8192], BF16, at=RB + 98304 + i * 16384)
            for i in range(2)]
    RB_END = RB + 131072
    assert RB_END - base <= 212800, f"SBUF overflow: {RB_END - base}"

    psF = [nc.alloc_psum_tensor(f"psF{i}", [128, 512], F32) for i in range(4)]
    psG = [nc.alloc_psum_tensor(f"psG{i}", [128, 512], F32) for i in range(2)]
    psT = [nc.alloc_psum_tensor(f"psT{i}", [128, 1024], BF16) for i in range(2)]

    bF = [Buf(f"psF{i}") for i in range(4)]
    bG = [Buf(f"psG{i}") for i in range(2)]
    bT = [Buf(f"psT{i}") for i in range(2)]
    b_ident = Buf("ident_bf")
    b_identf = Buf("ident_f32")
    b_mask = Buf("mask")
    b_ones = Buf("ones_bf")
    b_ones1 = Buf("ones_f1")
    b_bprow = Buf("bp_row")
    b_s3row = Buf("s3_row")
    b_BP = Buf("BPt")
    b_S3 = Buf("S3t")
    b_scstage = Buf("sc_stage")
    b_s1c, b_s2c, b_h1c, b_h2c = Buf("s1c"), Buf("s2c"), Buf("h1c"), Buf("h2c")
    b_S32 = [Buf(f"S32_{i}") for i in range(8)]
    b_Sbf = [Buf(f"Sbf_{i}") for i in range(8)]
    b_stats = [Buf(f"stats{i}") for i in range(8)]
    b_den = Buf("den")
    b_rec = Buf("rec")
    b_ndsb = [Buf(f"ndsb{i}") for i in range(8)]
    b_epsc = Buf("epsc")
    b_fo = [Buf(f"fo{i}") for i in range(4)]
    b_xslot = [Buf(f"xslot{i}") for i in range(8)]
    b_htb = [Buf(f"htb{i}") for i in range(2)]
    b_outb = [Buf(f"outb{i}") for i in range(2)]
    b_scr0 = [Buf(f"scr0{i}") for i in range(2)]
    b_scrT = Buf("scrT")
    b_eluR = [Buf(f"eluR{i}") for i in range(2)]
    b_eluM = [Buf(f"eluM{i}") for i in range(2)]
    b_eluE = [Buf(f"eluE{i}") for i in range(2)]
    b_scrS = [Buf(f"scrS{i}") for i in range(2)]
    b_wT = {nm: [Buf(f"{nm}T{k}") for k in range(NE)] for nm in ("wq", "wk", "wv", "wp")}
    b_wstage = Buf("wstage")
    b_ln0fm = [Buf(f"ln0fm{c}") for c in range(NE)]
    b_phiq = [Buf(f"phiq{c}") for c in range(NE)]
    b_phik = [Buf(f"phik{c}") for c in range(NE)]
    b_vT = [Buf(f"vT{i}") for i in range(4)]
    b_yfm = [Buf(f"yfm{c}") for c in range(NE)]
    b_phikT = [Buf(f"phikT{i}") for i in range(2)]
    b_ytok = [Buf(f"ytok{i}") for i in range(2)]
    b_ATm = [Buf(f"ATm{i}") for i in range(2)]
    b_ao = [Buf(f"ao{i}") for i in range(2)]
    b_ln0tok = [Buf(f"ln0tok{i}") for i in range(4)]
    b_hfm = [Buf(f"hfm{k}") for k in range(NHID)]
    b_g1 = [Buf(f"g1_{k}") for k in range(NHID)]
    b_g2 = [Buf(f"g2_{k}") for k in range(NHID)]
    b_w3T = [Buf(f"w3T{k}") for k in range(NHID)]
    b_ln2fm = [Buf(f"ln2fm{c}") for c in range(NE)]
    b_ln2tok = [Buf(f"ln2tok{i}") for i in range(8)]
    b_w12b = [Buf(f"w12b{i}") for i in range(2)]
    b_ws = {nm: Buf(nm + "_s") for nm in ("w1", "w2", "h1", "h2", "w3")}
    b_x2d = [Buf(f"x2d{i}") for i in range(NT)]
    b_yd = [Buf(f"yd{i}") for i in range(NT)]

    ws_map = {"w1": (w1_d, w1_s), "w2": (w2_d, w2_s),
              "h1": (h1_d, h1_s), "h2": (h2_d, h2_s), "w3": (w3_d, w3_s)}

    P.op("gpsimd", lambda e: e.memset(ident_bf[:, :], 0.0), writes=[b_ident])
    P.op("gpsimd", lambda e: e.affine_select(
        out=ident_bf[:, :], in_=ident_bf[:, :], compare_op=ALU.not_equal,
        fill=1.0, base=0, pattern=[[-1, 128]], channel_multiplier=1),
        reads=[b_ident], writes=[b_ident])
    P.op("gpsimd", lambda e: e.memset(ident_f32[:, :], 0.0), writes=[b_identf])
    P.op("gpsimd", lambda e: e.affine_select(
        out=ident_f32[:, :], in_=ident_f32[:, :], compare_op=ALU.not_equal,
        fill=1.0, base=0, pattern=[[-1, 128]], channel_multiplier=1),
        reads=[b_identf], writes=[b_identf])
    P.op("gpsimd", lambda e: e.memset(mask_f[:, :], 0.0), writes=[b_mask])
    P.op("gpsimd", lambda e: e.affine_select(
        out=mask_f[:, :], in_=mask_f[:, :], compare_op=ALU.is_gt, fill=1.0,
        base=0, pattern=[[-1, 128]], channel_multiplier=1),
        reads=[b_mask], writes=[b_mask])
    P.op("gpsimd", lambda e: e.memset(ones_bf[:, :], 1.0), writes=[b_ones])
    P.op("gpsimd", lambda e: e.memset(ones_f1[:, :], 1.0), writes=[b_ones1])
    P.op("gpsimd", lambda e: e.memset(epsc[:, :], 100000.0), writes=[b_epsc])
    P.op("vector", lambda e: e.reciprocal(epsc[:, 0:1], epsc[:, 1:2]),
         reads=[b_epsc], writes=[b_epsc])
    P.op("gpsimd", lambda e: e.memset(S32[:, :], 0.0), writes=b_S32)
    P.op("gpsimd", lambda e: e.memset(Sbf[:, :], 0.0), writes=b_Sbf)


    def load_x(tt):
        sl = tt % 8
        P.op("sync", lambda e, tt=tt, sl=sl: e.dma_start(
            out=xslot[sl][:, :], in_=x_d[tt * 128:(tt + 1) * 128, :]),
            writes=[b_xslot[sl]], dma=f"qx{sl}")

    for tt in range(4):
        load_x(tt)
    P.op("sync", lambda e: e.dma_start(out=bp_row[:, :], in_=bp_d[None, :]),
         writes=[b_bprow], dma="qsc_bp")
    P.op("sync", lambda e: e.dma_start(out=s3_row[:, :], in_=s3_d[None, :]),
         writes=[b_s3row], dma="qsc_s3")
    for si, (sd, tb, bb) in enumerate(((s1_d, s1c, b_s1c), (s2_d, s2c, b_s2c),
                                       (sh1_d, h1c, b_h1c), (sh2_d, h2c, b_h2c))):
        P.op("sync", lambda e, sd=sd: e.dma_start(
            out=sc_stage[:, :], in_=sd.rearrange("(c p) -> c p", p=128)),
            writes=[b_scstage], dma=f"qsc_{si}")
        P.op("tensor", lambda e: e.transpose(
            psF[0][:, 0:32], sc_stage[:, :], ident_f32[0:32, 0:32]),
            reads=[b_scstage, b_identf], writes=[bF[0]])
        P.op("vector", lambda e, tb=tb: e.tensor_copy(tb[:, :], psF[0][:, 0:32]),
             reads=[bF[0]], writes=[bb])
    for row, tile, bt, brow in ((bp_row, BPt, b_BP, b_bprow),
                                (s3_row, S3t, b_S3, b_s3row)):
        for half in range(2):
            P.op("tensor", lambda e, row=row, half=half: e.matmul(
                psF[1][:, 0:512], ones_f1[:, :], row[:, half * 512:(half + 1) * 512],
                start=True, stop=True),
                reads=[b_ones1, brow], writes=[bF[1]])
            P.op("vector", lambda e, tile=tile, half=half: e.tensor_copy(
                tile[:, half * 512:(half + 1) * 512], psF[1][:, 0:512]),
                reads=[bF[1]], writes=[bt], wmode="part")

    P.claim([b_scr0[0]], [b_bprow])
    P.claim([b_scr0[1]], [b_s3row])

    def ln_stats_apply(src_tile, b_src, stat, b_stat, out_tile, b_out, sc,
                       apply_eng="vector"):
        i = stat
        P.op("scalar", lambda e: e.activation(
            scr0[sc][:, :], src_tile[:, :], AF.Square,
            accum_out=st_ss[:, i:i + 1]),
            reads=[b_src], writes=[b_scr0[sc], b_stat])
        P.op("scalar", lambda e: e.activation(
            scr0[sc][:, :], src_tile[:, :], AF.Copy,
            accum_out=st_su[:, i:i + 1]),
            reads=[b_src], writes=[b_scr0[sc], b_stat])
        P.op("vector", lambda e: e.tensor_scalar(
            st_m[:, i:i + 1], st_su[:, i:i + 1], 1.0 / E, None, ALU.mult),
            reads=[b_stat], writes=[b_stat])
        P.op("vector", lambda e: e.tensor_mul(
            st_m2[:, i:i + 1], st_m[:, i:i + 1], st_m[:, i:i + 1]),
            reads=[b_stat], writes=[b_stat])
        P.op("vector", lambda e: e.scalar_tensor_tensor(
            st_v[:, i:i + 1], st_ss[:, i:i + 1], 1.0 / E, st_m2[:, i:i + 1],
            ALU.mult, ALU.subtract),
            reads=[b_stat], writes=[b_stat])
        P.op("scalar", lambda e: e.activation(
            st_v[:, i:i + 1], st_v[:, i:i + 1], AF.Ln, bias=epsc[:, 0:1]),
            reads=[b_stat, b_epsc], writes=[b_stat])
        P.op("scalar", lambda e: e.activation(
            st_r[:, i:i + 1], st_v[:, i:i + 1], AF.Exp, scale=-0.5),
            reads=[b_stat], writes=[b_stat])
        P.op("vector", lambda e: e.scalar_tensor_tensor(
            st_nm[:, i:i + 1], st_m[:, i:i + 1], -1.0, st_r[:, i:i + 1],
            ALU.mult, ALU.mult),
            reads=[b_stat], writes=[b_stat])
        if out_tile is not None:
            P.op(apply_eng, lambda e: e.tensor_scalar(
                out_tile[:, :], src_tile[:, :], st_r[:, i:i + 1],
                st_nm[:, i:i + 1], ALU.mult, ALU.add),
                reads=[b_src, b_stat], writes=[b_out])

    def transpose_set(src_tiles, b_srcs, col0, dst, b_dst, dcol0, par, n_in=4):
        for j in range(n_in):
            P.op("tensor", lambda e, j=j, col0=col0, par=par: e.transpose(
                psT[par][:, j * 128:(j + 1) * 128],
                src_tiles[j][:, col0:col0 + 128], ident_bf[:, :]),
                reads=[b_srcs[j], b_ident], writes=[bT[par]], wmode="part")
        P.op("vector", lambda e, par=par, dcol0=dcol0, n_in=n_in: e.tensor_copy(
            dst[:, dcol0:dcol0 + n_in * 128], psT[par][:, 0:n_in * 128]),
            reads=[bT[par]], writes=[b_dst], wmode="part")

    def ln0_span(s):
        for tl in range(4):
            tt = 4 * s + tl
            sl = tt % 8
            ln_stats_apply(xslot[sl], b_xslot[sl], tl, b_stats[tl],
                           ln0_tok[tl], b_ln0tok[tl], tl % 2,
                           apply_eng="gpsimd")

    ln0_span(0)
    for nm, wd in (("wq", wq_d), ("wk", wk_d), ("wv", wv_d), ("wp", wp_d)):
        for rnd in range(2):
            ks = range(rnd * 4, rnd * 4 + 4)
            for j, k in enumerate(ks):
                P.op("sync", lambda e, wd=wd, k=k, j=j: e.dma_start(
                    out=wstage[:, j * 1024:(j + 1) * 1024],
                    in_=wd[k * 128:(k + 1) * 128, :]),
                    writes=[b_wstage], wmode=("part" if j else "full"),
                    dma=f"qw_{nm}")
            _qw = P.dsem(f"qw_{nm}")
            b_wstage.pw = [(_qw, _qw.n)]
            for j, k in enumerate(ks):
                P.op("vector", lambda e, nm=nm, k=k, j=j: e.tensor_copy(
                    wT[nm][:, k * 1024:(k + 1) * 1024],
                    wstage[:, j * 1024:(j + 1) * 1024]),
                    reads=[b_wstage], writes=[b_wT[nm][k]])
        P.seal(b_wT[nm])
        if nm == "wk":
            for tt in range(4, 8):
                load_x(tt)

    P.claim(b_ln0fm + b_phiq, [b_wstage])

    gate = [(P.dsem(f"qw_{nm}"), P.dsem(f"qw_{nm}").n)
            for nm in ("wq", "wk", "wv", "wp")]
    gate += [(P.dsem(f"qx{i}"), P.dsem(f"qx{i}").n) for i in range(8)]
    _first_cast = [True]

    def cast_weight(nm):
        src_, dst = ws_map[nm]
        rows = src_.shape[0]
        nchunk = 4 if rows * src_.shape[1] >= HID * HID else 1
        rr = rows // nchunk
        for j in range(nchunk):
            ew = gate if _first_cast[0] else ()
            _first_cast[0] = False
            P.op("gpsimd", lambda e, src_=src_, dst=dst, j=j, rr=rr: e.dma_start(
                out=dst[j * rr:(j + 1) * rr, :], in_=src_[j * rr:(j + 1) * rr, :]),
                writes=[b_ws[nm]], wmode=("part" if nchunk > 1 else "full"),
                dma="qc_" + nm, extra_waits=ew)

    for nm in ("w1", "w2", "h1", "h2", "w3"):
        cast_weight(nm)

    for s in range(NSPAN):
        for cp in range(4):
            par2 = cp % 2
            for half2 in range(2):
                c = 2 * cp + half2
                for j in range(4):
                    P.op("tensor", lambda e, j=j, c=c, half2=half2, par2=par2: e.transpose(
                        psT[par2][:, half2 * 512 + j * 128:half2 * 512 + (j + 1) * 128],
                        ln0_tok[j][:, c * 128:(c + 1) * 128], ident_bf[:, :]),
                        reads=[b_ln0tok[j], b_ident], writes=[bT[par2]], wmode="part")
            P.op("vector", lambda e, cp=cp, par2=par2: e.tensor_copy(
                ln0_fm[:, cp * 1024:(cp + 1) * 1024], psT[par2][:, :]),
                reads=[bT[par2]], writes=[b_ln0fm[2 * cp], b_ln0fm[2 * cp + 1]],
                wmode="part")
        P.seal(b_ln0fm)
        qbanks = [(psG[0], bG[0]), (psG[1], bG[1]), (psF[0], bF[0]), (psF[1], bF[1])]

        def proj_fm(wname, dst, b_dst, kind):
            for oc in range(NE):
                pg = oc % 2
                psq, bq = qbanks[oc % 4]
                for k in range(NE):
                    P.op("tensor", lambda e, wname=wname, oc=oc, k=k, psq=psq: e.matmul(
                        psq[:, 0:512],
                        wT[wname][:, k * 1024 + oc * 128:k * 1024 + oc * 128 + 128],
                        ln0_fm[:, k * 512:(k + 1) * 512],
                        start=(k == 0), stop=(k == NE - 1)),
                        reads=[b_wT[wname][k], b_ln0fm[k]], writes=[bq],
                        wmode=("full" if k == NE - 1 else "mid"))
                if kind == "phi":
                    P.op("scalar", lambda e, pg=pg, oc=oc, psq=psq: e.activation(
                        eluR[pg][:, :], psq[:, 0:512], AF.Relu),
                        reads=[bq], writes=[b_eluR[pg]])
                    P.op("vector", lambda e, pg=pg, psq=psq: e.tensor_sub(
                        eluM[pg][:, :], psq[:, 0:512], eluR[pg][:, :]),
                        reads=[bq, b_eluR[pg]], writes=[b_eluM[pg]])
                    P.op("scalar", lambda e, pg=pg: e.activation(
                        eluE[pg][:, :], eluM[pg][:, :], AF.Exp),
                        reads=[b_eluM[pg]], writes=[b_eluE[pg]])
                    P.op("vector", lambda e, pg=pg, oc=oc, dst=dst: e.tensor_tensor(
                        dst[:, oc * 512:(oc + 1) * 512], eluR[pg][:, :],
                        eluE[pg][:, :], ALU.add),
                        reads=[b_eluR[pg], b_eluE[pg]], writes=[b_dst[oc]])
                else:
                    P.op("vector", lambda e, pg=pg, oc=oc, dst=dst, psq=psq: e.tensor_copy(
                        dst[:, oc * 512:(oc + 1) * 512], psq[:, 0:512]),
                        reads=[bq], writes=[b_dst[oc]])

        proj_fm("wk", phik_fm, b_phik, "phi")
        P.seal(b_phik)
        proj_fm("wq", phiq_fm, b_phiq, "phi")
        P.seal(b_phiq)

        def v_proj(tl):
            for osp in range(2):
                if tl == 0:
                    psq, bq = qbanks[osp % 4 if False else (tl * 2 + osp) % 4]
                else:
                    psq, bq = psG[(tl * 2 + osp) % 2], bG[(tl * 2 + osp) % 2]
                for k in range(NE):
                    P.op("tensor", lambda e, tl=tl, osp=osp, k=k, psq=psq: e.matmul(
                        psq[:, 0:512],
                        ln0_fm[:, k * 512 + tl * 128:k * 512 + tl * 128 + 128],
                        wT["wv"][:, k * 1024 + osp * 512:k * 1024 + osp * 512 + 512],
                        start=(k == 0), stop=(k == NE - 1)),
                        reads=[b_ln0fm[k], b_wT["wv"][k]], writes=[bq],
                        wmode=("full" if k == NE - 1 else "mid"))
                P.op("vector", lambda e, tl=tl, osp=osp, psq=psq: e.tensor_copy(
                    vT_sb[:, tl * 1024 + osp * 512:tl * 1024 + osp * 512 + 512],
                    psq[:, 0:512]),
                    reads=[bq], writes=[b_vT[tl]], wmode="part")

        v_proj(0)

        def y_transposes(nl):
            parx = nl % 2
            for c in range(NE):
                P.op("tensor", lambda e, c=c, parx=parx: e.transpose(
                    psT[parx][:, c * 128:(c + 1) * 128],
                    y_tok[parx][:, c * 128:(c + 1) * 128], ident_bf[:, :]),
                    reads=[b_ytok[parx], b_ident], writes=[bT[parx]], wmode="part")
            for c in range(NE):
                P.op("vector", lambda e, c=c, parx=parx, nl=nl: e.tensor_copy(
                    y_fm[:, c * 512 + nl * 128:c * 512 + nl * 128 + 128],
                    psT[parx][:, c * 128:(c + 1) * 128]),
                    reads=[bT[parx]], writes=[b_yfm[c]], wmode="part")

        for nl in range(4):
            par = nl % 2
            def emit_AT(hp):
                for hi in range(2):
                    fb = bF[hi]
                    pb = 64 * hi
                    P.op("tensor", lambda e, hp=hp, hi=hi, nl=nl, pb=pb: e.matmul(
                        psF[hi][:, 0:128],
                        phik_fm[pb:pb + 64, hp * 512 + nl * 128:hp * 512 + nl * 128 + 128],
                        phiq_fm[pb:pb + 64, hp * 512 + nl * 128:hp * 512 + nl * 128 + 128],
                        start=True, stop=True),
                        reads=[b_phik[hp], b_phiq[hp]], writes=[fb])
                    P.op("vector", lambda e, hp=hp, hi=hi: e.tensor_tensor(
                        ATm[hp % 2][:, hi * 128:(hi + 1) * 128],
                        psF[hi][:, 0:128], mask_f[:, :], ALU.mult),
                        reads=[fb, b_mask], writes=[b_ATm[hp % 2]], wmode="part")

            emit_AT(0)
            for hp in range(8):
                fn_ = bF[2 + hp % 2]
                if hp + 1 < 8:
                    emit_AT(hp + 1)
                for hi in range(2):
                    h = 2 * hp + hi
                    pb = 64 * hi
                    bs = hi * 65
                    P.op("tensor", lambda e, hp=hp, hi=hi, h=h, nl=nl, bs=bs: e.matmul(
                        psF[2 + hp % 2][:, bs:bs + 64],
                        ATm[hp % 2][:, hi * 128:(hi + 1) * 128],
                        vT_sb[:, nl * 1024 + h * 64:nl * 1024 + h * 64 + 64],
                        start=(hi == 0), stop=False),
                        reads=[b_ATm[hp % 2], b_vT[nl]], writes=[fn_], wmode="mid")
                    P.op("tensor", lambda e, hp=hp, hi=hi, bs=bs: e.matmul(
                        psF[2 + hp % 2][:, bs + 64:bs + 65],
                        ATm[hp % 2][:, hi * 128:(hi + 1) * 128],
                        ones_bf[:, 0:1], start=False, stop=False),
                        reads=[b_ATm[hp % 2], b_ones], writes=[fn_], wmode="mid")
                    P.op("tensor", lambda e, hp=hp, hi=hi, nl=nl, pb=pb, bs=bs: e.matmul(
                        psF[2 + hp % 2][:, bs:bs + 65],
                        phiq_fm[pb:pb + 64, hp * 512 + nl * 128:hp * 512 + nl * 128 + 128],
                        Sbf[pb:pb + 64, hp * 65:hp * 65 + 65],
                        start=False, stop=(hi == 1)),
                        reads=[b_phiq[hp], b_Sbf[hp]], writes=[fn_],
                        wmode=("full" if hi == 1 else "mid"))
                P.op("scalar", lambda e, hp=hp: e.activation(
                    nd_sb[:, hp * 130:(hp + 1) * 130], psF[2 + hp % 2][:, 0:130],
                    AF.Copy),
                    reads=[fn_], writes=[b_ndsb[hp]])
            for c in range(NE):
                P.op("tensor", lambda e, c=c, nl=nl, par=par: e.transpose(
                    psT[par][:, c * 128:(c + 1) * 128],
                    phik_fm[:, c * 512 + nl * 128:c * 512 + nl * 128 + 128],
                    ident_bf[:, :]),
                    reads=[b_phik[c], b_ident], writes=[bT[par]], wmode="part")
            P.op("vector", lambda e, par=par: e.tensor_copy(
                phikT[par][:, :], psT[par][:, :]),
                reads=[bT[par]], writes=[b_phikT[par]])
            for hp in range(8):
                gb = bG[hp % 2]
                for hi in range(2):
                    h = 2 * hp + hi
                    pb = 64 * hi
                    P.op("tensor", lambda e, h=h, hp=hp, hi=hi, nl=nl, pb=pb, par=par: e.matmul(
                        psG[hp % 2][pb:pb + 64, 0:64],
                        phikT[par][:, h * 64:h * 64 + 64],
                        vT_sb[:, nl * 1024 + h * 64:nl * 1024 + h * 64 + 64],
                        start=True, stop=False),
                        reads=[b_phikT[par], b_vT[nl]], writes=[gb], wmode="mid")
                    P.op("tensor", lambda e, h=h, hp=hp, hi=hi, pb=pb, par=par: e.matmul(
                        psG[hp % 2][pb:pb + 64, 64:65],
                        phikT[par][:, h * 64:h * 64 + 64],
                        ones_bf[:, 0:1], start=False, stop=True),
                        reads=[b_phikT[par], b_ones], writes=[gb],
                        wmode="part")
                P.op("vector", lambda e, hp=hp: e.tensor_add(
                    S32[:, hp * 65:hp * 65 + 65], S32[:, hp * 65:hp * 65 + 65],
                    psG[hp % 2][:, 0:65]),
                    reads=[b_S32[hp], gb], writes=[b_S32[hp]])
                P.op("scalar", lambda e, hp=hp: e.activation(
                    Sbf[:, hp * 65:hp * 65 + 65], S32[:, hp * 65:hp * 65 + 65],
                    AF.Copy),
                    reads=[b_S32[hp]], writes=[b_Sbf[hp]])
            if nl + 1 < 4:
                v_proj(nl + 1)
            if nl > 0:
                y_transposes(nl - 1)
            dsrc = bass.AP(nd_sb, 64, [[8 * 130, 128], [130, 8], [65, 2]])
            P.op("scalar", lambda e, dsrc=dsrc: e.activation(
                den_sb[:, 0:16], dsrc, AF.Copy, bias=ATEPS),
                reads=b_ndsb, writes=[b_den])
            P.op("vector", lambda e: e.reciprocal(rec_sb[:, 0:16], den_sb[:, 0:16]),
                 reads=[b_den], writes=[b_rec])
            for h in range(16):
                hp, hi = h // 2, h % 2
                P.op("scalar", lambda e, h=h, hp=hp, hi=hi, par=par: e.activation(
                    y_tok[par][:, h * 64:h * 64 + 64],
                    nd_sb[:, hp * 130 + hi * 65:hp * 130 + hi * 65 + 64], AF.Copy,
                    scale=rec_sb[:, h:h + 1]),
                    reads=[b_ndsb[hp], b_rec], writes=[b_ytok[par]], wmode="part")
        y_transposes(3)
        P.seal(b_yfm)
        if s + 1 < NSPAN:
            ln0_span(s + 1)

        for tl in range(4):
            tt = 4 * s + tl
            sl = tt % 8
            for osp in range(2):
                psq, bq = qbanks[(tl * 2 + osp) % 4]
                for k in range(NE):
                    P.op("tensor", lambda e, tl=tl, osp=osp, k=k, psq=psq: e.matmul(
                        psq[:, 0:512],
                        y_fm[:, k * 512 + tl * 128:k * 512 + tl * 128 + 128],
                        wT["wp"][:, k * 1024 + osp * 512:k * 1024 + osp * 512 + 512],
                        start=(k == 0), stop=(k == NE - 1)),
                        reads=[b_yfm[k], b_wT["wp"][k]], writes=[bq],
                        wmode=("full" if k == NE - 1 else "mid"))
                P.op("vector", lambda e, tl=tl, osp=osp, psq=psq: e.tensor_tensor(
                    ao_tok[tl % 2][:, osp * 512:(osp + 1) * 512],
                    psq[:, 0:512], BPt[:, osp * 512:(osp + 1) * 512], ALU.add),
                    reads=[bq, b_BP], writes=[b_ao[tl % 2]], wmode="part")
            ln_stats_apply(ao_tok[tl % 2], b_ao[tl % 2], 4 + (tl % 2),
                           b_stats[4 + (tl % 2)], None, None, tl % 2)
            i = 4 + (tl % 2)
            P.op("vector", lambda e, tl=tl, i=i: e.tensor_scalar(
                scrT[:, :], ao_tok[tl % 2][:, :], st_r[:, i:i + 1],
                st_nm[:, i:i + 1], ALU.mult, ALU.add),
                reads=[b_ao[tl % 2], b_stats[i]], writes=[b_scrT])
            P.op("vector", lambda e, sl=sl: e.tensor_add(
                xslot[sl][:, :], xslot[sl][:, :], scrT[:, :]),
                reads=[b_xslot[sl], b_scrT], writes=[b_xslot[sl]])
            P.op("scalar", lambda e, tt=tt, sl=sl: e.dma_start(
                out=x2_d[tt * 128:(tt + 1) * 128, :], in_=xslot[sl][:, :]),
                reads=[b_xslot[sl]], writes=[b_x2d[tt]], dma=f"qx2s{sl}")
        if 4 * (s + 2) < NT:
            for tt in range(4 * (s + 2), 4 * (s + 2) + 4):
                load_x(tt)

    for tt in range(NT):
        _qs = P.dsem(f"qx2s{tt % 8}")
        b_x2d[tt].pw = [(_qs, _qs.n)]

    P.claim(b_hfm, sum([b_wT[nm] for nm in ("wq", "wk", "wv", "wp")], []))
    P.claim(b_fo, [b_eluM[0], b_eluM[1], b_eluE[0], b_eluE[1]])
    P.claim(b_ln2fm + b_ln2tok + b_w12b + [b_scrS[0], b_scrS[1]],
            b_ln0fm + b_phiq + b_phik + [b_vT[i] for i in range(4)] +
            b_yfm + b_phikT + b_ytok + b_ATm + b_ao + b_ln0tok +
            [b_eluR[0], b_eluR[1], b_eluM[0], b_eluM[1], b_eluE[0], b_eluE[1]])
    P.claim(b_htb + b_outb, b_xslot)

    x2ld_total = P.dsem("qx2s")

    for sl in range(NSLAB):
        ts0 = sl * 8
        for tl in range(8):
            tt = ts0 + tl
            slx = tl % 2
            P.op("sync", lambda e, tt=tt, slx=slx: e.dma_start(
                out=xslot[slx][:, :], in_=x2_d[tt * 128:(tt + 1) * 128, :]),
                reads=[b_x2d[tt]], writes=[b_xslot[slx]], dma=f"qx2l{slx}")
            ln_stats_apply(xslot[slx], b_xslot[slx], tl % 8, b_stats[tl % 8],
                           ln2_tok[tl], b_ln2tok[tl], tl % 2)
        for c in range(NE):
            par = c % 2
            for j in range(8):
                P.op("tensor", lambda e, c=c, j=j, par=par: e.transpose(
                    psT[par][:, j * 128:(j + 1) * 128],
                    ln2_tok[j][:, c * 128:(c + 1) * 128], ident_bf[:, :]),
                    reads=[b_ln2tok[j], b_ident], writes=[bT[par]], wmode="part")
            P.op("vector", lambda e, c=c, par=par: e.tensor_copy(
                ln2_fm[:, c * 1024:(c + 1) * 1024], psT[par][:, :]),
                reads=[bT[par]], writes=[b_ln2fm[c]])
        P.seal(b_ln2fm)

        def load_w12(osp):
            wb = osp % 2
            P.op("sync", lambda e, osp=osp, wb=wb: e.dma_start(
                out=w12b[wb][:, 0:4096].rearrange("p (k o) -> p k o", k=NE),
                in_=w1_s[:, osp * 512:(osp + 1) * 512]
                .rearrange("(k p) o -> p k o", p=128)),
                reads=[b_ws["w1"]], writes=[b_w12b[wb]], dma=f"qw12{wb}")
            P.op("sync", lambda e, osp=osp, wb=wb: e.dma_start(
                out=w12b[wb][:, 4096:8192].rearrange("p (k o) -> p k o", k=NE),
                in_=w2_s[:, osp * 512:(osp + 1) * 512]
                .rearrange("(k p) o -> p k o", p=128)),
                reads=[b_ws["w2"]], writes=[b_w12b[wb]], wmode="part",
                dma=f"qw12{wb}")

        load_w12(0)
        load_w12(1)
        for osp in range(8):
            wb = osp % 2
            for oc4 in range(4):
                oc = osp * 4 + oc4
                for tc in range(2):
                    pa = 2 * ((oc4 * 2 + tc) % 2)
                    for k in range(NE):
                        P.op("tensor", lambda e, oc4=oc4, tc=tc, k=k, pa=pa, wb=wb: e.matmul(
                            psF[pa][:, 0:512],
                            w12b[wb][:, k * 512 + oc4 * 128:k * 512 + oc4 * 128 + 128],
                            ln2_fm[:, k * 1024 + tc * 512:k * 1024 + tc * 512 + 512],
                            start=(k == 0), stop=(k == NE - 1)),
                            reads=[b_w12b[wb], b_ln2fm[k]], writes=[bF[pa]],
                            wmode=("full" if k == NE - 1 else "mid"))
                    for k in range(NE):
                        P.op("tensor", lambda e, oc4=oc4, tc=tc, k=k, pa=pa, wb=wb: e.matmul(
                            psF[pa + 1][:, 0:512],
                            w12b[wb][:, 4096 + k * 512 + oc4 * 128:4096 + k * 512 + oc4 * 128 + 128],
                            ln2_fm[:, k * 1024 + tc * 512:k * 1024 + tc * 512 + 512],
                            start=(k == 0), stop=(k == NE - 1)),
                            reads=[b_w12b[wb], b_ln2fm[k]], writes=[bF[pa + 1]],
                            wmode=("full" if k == NE - 1 else "mid"))
                    sp = (oc4 * 2 + tc) % 2
                    P.op("scalar", lambda e, oc=oc, pa=pa, sp=sp: e.activation(
                        scrS[sp][:, :], psF[pa][:, 0:512], AF.Silu,
                        scale=s1c[:, oc:oc + 1]),
                        reads=[bF[pa], b_s1c], writes=[b_scrS[sp]])
                    P.op("vector", lambda e, oc=oc, tc=tc, pa=pa, sp=sp: e.scalar_tensor_tensor(
                        h_fm[:, oc * 1024 + tc * 512:oc * 1024 + tc * 512 + 512],
                        scrS[sp][:, :], s2c[:, oc:oc + 1], psF[pa + 1][:, 0:512],
                        ALU.mult, ALU.mult),
                        reads=[b_scrS[sp], b_s2c, bF[pa + 1]], writes=[b_hfm[oc]],
                        wmode="part")
            if osp + 2 < 8:
                load_w12(osp + 2)
        P.seal(b_hfm)

        def hidden_stage(w_s, b_wsrc, scol, b_scol, src_bufs, src_t, dst, b_dst):
            def load_span(osp):
                wb2 = osp % 2
                P.op("sync", lambda e, osp=osp, wb2=wb2, w_s=w_s: e.dma_start(
                    out=htb[wb2][:, :].rearrange("p (k o) -> p k o", k=NHID),
                    in_=w_s[:, osp * 256:(osp + 1) * 256]
                    .rearrange("(k p) o -> p k o", p=128)),
                    reads=[b_wsrc], writes=[b_htb[wb2]], dma=f"qht{wb2}")

            load_span(0)
            load_span(1)
            for osp in range(16):
                wb2 = osp % 2
                for oc2 in range(2):
                    oc = osp * 2 + oc2
                    for tc in range(2):
                        pa = (oc2 * 2 + tc) % 4
                        for k in range(NHID):
                            P.op("tensor", lambda e, oc2=oc2, tc=tc, k=k, pa=pa, wb2=wb2: e.matmul(
                                psF[pa][:, 0:512],
                                htb[wb2][:, k * 256 + oc2 * 128:k * 256 + oc2 * 128 + 128],
                                src_t[:, k * 1024 + tc * 512:k * 1024 + tc * 512 + 512],
                                start=(k == 0), stop=(k == NHID - 1)),
                                reads=[b_htb[wb2], src_bufs[k]], writes=[bF[pa]],
                                wmode=("full" if k == NHID - 1 else "mid"))
                        P.op("scalar", lambda e, oc=oc, tc=tc, pa=pa: e.activation(
                            dst[:, oc * 1024 + tc * 512:oc * 1024 + tc * 512 + 512],
                            psF[pa][:, 0:512], AF.Gelu, scale=scol[:, oc:oc + 1]),
                            reads=[bF[pa], b_scol], writes=[b_dst[oc]], wmode="part")
                if osp + 2 < 16:
                    load_span(osp + 2)

        P.claim(b_g1, b_ln2fm + b_ln2tok + b_w12b)
        P.claim(b_htb, b_xslot + b_outb)
        hidden_stage(h1_s, b_ws["h1"], h1c, b_h1c, b_hfm, h_fm, g1_fm, b_g1)
        P.seal(b_g1)
        P.claim(b_g2, b_hfm)
        hidden_stage(h2_s, b_ws["h2"], h2c, b_h2c, b_g1, g1_fm, g2_fm, b_g2)
        P.seal(b_g2)

        P.seal(b_g1)
        P.claim(b_w3T, b_g1)
        P.op("sync", lambda e: e.dma_start(
            out=w3T_sb[:, :].rearrange("p (k o) -> p k o", k=NHID),
            in_=w3_s.rearrange("(k p) o -> p k o", p=128)),
            reads=[b_ws["w3"]], writes=b_w3T, dma="qw3")
        P.claim([b_xslot[0], b_xslot[1]], b_htb)
        P.claim(b_outb, b_htb)
        for half in range(2):
            loop = ([(osp, tl) for osp in range(4) for tl in range(4)] if half == 0
                    else [(osp, tl) for tl in range(4) for osp in range(4)])
            for idx, (osp, tl) in enumerate(loop):
                    gt = half * 4 + tl
                    pa = idx % 4
                    for k in range(NHID):
                        P.op("tensor", lambda e, gt=gt, osp=osp, k=k, pa=pa: e.matmul(
                            psF[pa][:, 0:256],
                            g2_fm[:, k * 1024 + gt * 128:k * 1024 + gt * 128 + 128],
                            w3T_sb[:, k * 1024 + osp * 256:k * 1024 + osp * 256 + 256],
                            start=(k == 0), stop=(k == NHID - 1)),
                            reads=[b_g2[k], b_w3T[k]], writes=[bF[pa]],
                            wmode=("full" if k == NHID - 1 else "mid"))
                    P.op("vector", lambda e, tl=tl, osp=osp, pa=pa: e.tensor_tensor(
                        fo_t[tl][:, osp * 256:(osp + 1) * 256], psF[pa][:, 0:256],
                        S3t[:, osp * 256:(osp + 1) * 256], ALU.mult),
                        reads=[bF[pa], b_S3], writes=[b_fo[tl]], wmode="part")
            for tl in range(4):
                gt = half * 4 + tl
                tt = ts0 + gt
                slx = tl % 2
                P.op("sync", lambda e, tt=tt, slx=slx: e.dma_start(
                    out=xslot[slx][:, :], in_=x2_d[tt * 128:(tt + 1) * 128, :]),
                    reads=[b_x2d[tt]], writes=[b_xslot[slx]], dma=f"qx2l{slx}")
                ln_stats_apply(fo_t[tl], b_fo[tl], tl % 8, b_stats[tl % 8],
                               None, None, tl % 2)
                i = tl % 8
                P.op("vector", lambda e, tl=tl, i=i: e.tensor_scalar(
                    scrT[:, :], fo_t[tl][:, :], st_r[:, i:i + 1],
                    st_nm[:, i:i + 1], ALU.mult, ALU.add),
                    reads=[b_fo[tl], b_stats[i]], writes=[b_scrT])
                ob = tl % 2
                P.op("vector", lambda e, slx=slx, ob=ob: e.tensor_add(
                    outb[ob][:, :], xslot[slx][:, :], scrT[:, :]),
                    reads=[b_xslot[slx], b_scrT], writes=[b_outb[ob]])
                P.op("gpsimd", lambda e, tt=tt, ob=ob: e.dma_start(
                    out=y_d[tt * 128:(tt + 1) * 128, :], in_=outb[ob][:, :]),
                    reads=[b_outb[ob]], writes=[b_yd[tt]], dma=f"qy{ob}")

        if sl + 1 < NSLAB:
            P.claim(b_hfm, b_g2)
            P.claim(b_ln2fm + b_ln2tok + b_w12b, b_w3T)
            P.claim([b_xslot[0], b_xslot[1]], b_htb + b_outb)

    P.op("gpsimd", lambda e: e.nop(),
         extra_waits=[(P.dsem("qy0"), P.dsem("qy0").n),
                      (P.dsem("qy1"), P.dsem("qy1").n)])

    with ExitStack() as sstk:
        for s in P.sems():
            s.h = sstk.enter_context(nc.semaphore(s.name))
        with nc.Block() as block:
            P.replay(block)
    return nc


_NC_CACHE = []


def _get_nc():
    if not _NC_CACHE:
        _NC_CACHE.append(build_nc())
    return _NC_CACHE[0]


def kernel(L0_attn_x, L0_attn_Wq, L0_attn_Wk, L0_attn_Wv, L0_attn_Wp,
           L0_attn_bp, L0_ffn_w1_w, L0_ffn_w1_s, L0_ffn_w2_w, L0_ffn_w2_s,
           L0_ffn_h1_w, L0_ffn_h1_s, L0_ffn_h2_w, L0_ffn_h2_s, L0_ffn_w3_w,
           L0_ffn_w3_s):
    x = np.ascontiguousarray(np.asarray(L0_attn_x, dtype=np.float32))
    common = {
        "wq": np.ascontiguousarray(np.asarray(L0_attn_Wq, np.float32).T),
        "wk": np.ascontiguousarray(np.asarray(L0_attn_Wk, np.float32).T),
        "wv": np.ascontiguousarray(np.asarray(L0_attn_Wv, np.float32).T),
        "wp": np.ascontiguousarray(np.asarray(L0_attn_Wp, np.float32).T),
        "bp": np.ascontiguousarray(np.asarray(L0_attn_bp, np.float32)),
        "w1": np.ascontiguousarray(np.asarray(L0_ffn_w1_w, np.float32).T),
        "w2": np.ascontiguousarray(np.asarray(L0_ffn_w2_w, np.float32).T),
        "h1": np.ascontiguousarray(np.asarray(L0_ffn_h1_w, np.float32).T),
        "h2": np.ascontiguousarray(np.asarray(L0_ffn_h2_w, np.float32).T),
        "w3": np.ascontiguousarray(np.asarray(L0_ffn_w3_w, np.float32).T),
        "s1": np.ascontiguousarray(np.asarray(L0_ffn_w1_s, np.float32)),
        "s2": np.ascontiguousarray(np.asarray(L0_ffn_w2_s, np.float32)),
        "sh1": np.ascontiguousarray(np.asarray(L0_ffn_h1_s, np.float32)),
        "sh2": np.ascontiguousarray(np.asarray(L0_ffn_h2_s, np.float32)),
        "s3": np.ascontiguousarray(np.asarray(L0_ffn_w3_s, np.float32)),
    }
    nc = _get_nc()
    in_maps = [dict(common, x=np.ascontiguousarray(x[b])) for b in range(8)]
    res = run_bass_kernel_spmd(nc, in_maps, core_ids=list(range(8)))
    return np.stack([res.results[b]["y"] for b in range(8)], axis=0)


# version 19
# speedup vs baseline: 1.1386x; 1.0188x over previous
import numpy as np
import ml_dtypes
from contextlib import ExitStack

import concourse.bass as bass
import concourse.mybir as mybir
from concourse.bass_utils import run_bass_kernel_spmd

F32 = mybir.dt.float32
BF16 = mybir.dt.bfloat16
AF = mybir.ActivationFunctionType
ALU = mybir.AluOpType

T = 2048
E = 1024
HID = 4096
H = 16
D = 64
CH = 128
NT = 16
NE = 8
NHID = 32
NSPAN = 4
NSLAB = 2
LNEPS = 1e-5
ATEPS = 1e-6

ENGS = ("tensor", "vector", "scalar", "gpsimd", "sync")


class Sem:
    __slots__ = ("name", "n", "h")

    def __init__(self, name):
        self.name = name
        self.n = 0
        self.h = None


class Buf:
    __slots__ = ("name", "pw", "readers", "wave_open")

    def __init__(self, name):
        self.name = name
        self.pw = []
        self.readers = {}
        self.wave_open = False


class Prog:
    def __init__(self):
        self.ops = {e: [] for e in ENGS}
        self.lastw = {e: {} for e in ENGS}
        self.esem = {e: Sem("sem_" + e) for e in ENGS}
        self.qsem = {}
        self.nwaits = 0
        self.nops = 0
        self._pmr = []

    def dsem(self, name):
        if name not in self.qsem:
            self.qsem[name] = Sem(name)
        return self.qsem[name]

    def sems(self):
        return list(self.esem.values()) + list(self.qsem.values())

    def _wait(self, eng, mark):
        if mark is None:
            return
        sem, val = mark
        if val <= 0:
            return
        lw = self.lastw[eng]
        if lw.get(sem.name, 0) >= val:
            return
        lw[sem.name] = val
        self.nwaits += 1
        self.ops[eng].append(lambda e, s=sem, v=val: e.wait_ge(s.h, v))

    def op(self, eng, fn, reads=(), writes=(), wmode="full", dma=None,
           extra_waits=()):
        waits = list(extra_waits)
        for b in reads:
            waits.extend(b.pw)
        opening = []
        for b in writes:
            if wmode == "full" or not b.wave_open:
                waits.extend(b.pw)
                for s, v in b.readers.items():
                    waits.append((s, v))
                opening.append(b)
        wmax = {}
        for m in waits:
            if m is None:
                continue
            s, v = m
            if wmax.get(s.name, (None, 0))[1] < v:
                wmax[s.name] = (s, v)
        for s, v in wmax.values():
            self._wait(eng, (s, v))
        self.nops += 1
        if wmode == "mid":
            mark = None
            self.ops[eng].append(fn)
        else:
            sem = self.dsem(dma) if dma else self.esem[eng]
            inck = 16 if dma else 1
            sem.n += inck
            mark = (sem, sem.n)
            self.ops[eng].append(
                lambda e, fn=fn, s=sem, v=inck: fn(e).then_inc(s.h, v))
        for b in opening:
            b.pw = []
            b.readers = {}
            b.wave_open = True
        for b in writes:
            if mark is not None:
                if wmode == "full":
                    b.pw = [mark]
                else:
                    b.pw.append(mark)
        for b in reads:
            b.wave_open = False
            if mark is not None:
                s, v = mark
                if b.readers.get(s, 0) < v:
                    b.readers[s] = v
        if mark is None and reads:
            self._pmr.append((eng, list(reads)))
        elif mark is not None and self._pmr:
            rest = []
            for peng, bufs in self._pmr:
                if peng == eng:
                    s, v = mark
                    for b in bufs:
                        if b.readers.get(s, 0) < v:
                            b.readers[s] = v
                else:
                    rest.append((peng, bufs))
            self._pmr = rest
        return mark

    def seal(self, bufs):
        mx = {}
        for b in bufs:
            for s, v in b.pw:
                if mx.get(s.name, (None, 0))[1] < v:
                    mx[s.name] = (s, v)
        combined = list(mx.values())
        for b in bufs:
            b.pw = list(combined)

    def claim(self, news, olds):
        for nb in news:
            for ob in olds:
                nb.pw.extend(ob.pw)
                for s, v in ob.readers.items():
                    if nb.readers.get(s, 0) < v:
                        nb.readers[s] = v

    def replay(self, block):
        for ename in ENGS:
            ops = list(self.ops[ename])
            if ops:
                getattr(block, ename)(lambda eng, ops=ops: [f(eng) for f in ops])


def build_nc():
    nc = bass.Bass("TRN2", target_bir_lowering=False)

    x_d = nc.dram_tensor("x", [T, E], F32, kind="ExternalInput").ap()
    wq_d = nc.dram_tensor("wq", [E, E], F32, kind="ExternalInput").ap()
    wk_d = nc.dram_tensor("wk", [E, E], F32, kind="ExternalInput").ap()
    wv_d = nc.dram_tensor("wv", [E, E], F32, kind="ExternalInput").ap()
    wp_d = nc.dram_tensor("wp", [E, E], F32, kind="ExternalInput").ap()
    bp_d = nc.dram_tensor("bp", [E], F32, kind="ExternalInput").ap()
    w1_d = nc.dram_tensor("w1", [E, HID], F32, kind="ExternalInput").ap()
    w2_d = nc.dram_tensor("w2", [E, HID], F32, kind="ExternalInput").ap()
    h1_d = nc.dram_tensor("h1", [HID, HID], F32, kind="ExternalInput").ap()
    h2_d = nc.dram_tensor("h2", [HID, HID], F32, kind="ExternalInput").ap()
    w3_d = nc.dram_tensor("w3", [HID, E], F32, kind="ExternalInput").ap()
    s1_d = nc.dram_tensor("s1", [HID], F32, kind="ExternalInput").ap()
    s2_d = nc.dram_tensor("s2", [HID], F32, kind="ExternalInput").ap()
    sh1_d = nc.dram_tensor("sh1", [HID], F32, kind="ExternalInput").ap()
    sh2_d = nc.dram_tensor("sh2", [HID], F32, kind="ExternalInput").ap()
    s3_d = nc.dram_tensor("s3", [E], F32, kind="ExternalInput").ap()
    y_d = nc.dram_tensor("y", [T, E], F32, kind="ExternalOutput").ap()
    w1_s = nc.dram_tensor("w1_s", [E, HID], BF16, kind="Internal").ap()
    w2_s = nc.dram_tensor("w2_s", [E, HID], BF16, kind="Internal").ap()
    h1_s = nc.dram_tensor("h1_s", [HID, HID], BF16, kind="Internal").ap()
    h2_s = nc.dram_tensor("h2_s", [HID, HID], BF16, kind="Internal").ap()
    w3_s = nc.dram_tensor("w3_s", [HID, E], BF16, kind="Internal").ap()
    x2_d = nc.dram_tensor("x2_d", [T, E], F32, kind="Internal").ap()

    P = Prog()

    base = (nc.sbuf_base + 127) // 128 * 128
    off = [base]

    def alloc(name, shape, dt, at=None):
        if at is None:
            at = off[0]
            nbytes = int(np.prod(shape[1:])) * mybir.dt.size(dt)
            off[0] = (at + nbytes + 127) // 128 * 128
        t = nc.alloc_sbuf_tensor_at(name, shape, dt, offset=at)
        return t

    ident_bf = alloc("ident_bf", [128, 128], BF16)
    ident_f32 = alloc("ident_f32", [128, 128], F32)
    mask_f = alloc("mask_f", [128, 128], F32)
    ones_bf = alloc("ones_bf", [128, 16], BF16)
    ones_f1 = alloc("ones_f1", [1, 128], F32)
    BPt = alloc("BPt", [128, E], F32)
    S3t = alloc("S3t", [128, E], F32)
    sc_stage = alloc("sc_stage", [32, 128], F32)
    s1c = alloc("s1c", [128, 32], F32)
    s2c = alloc("s2c", [128, 32], F32)
    h1c = alloc("h1c", [128, 32], F32)
    h2c = alloc("h2c", [128, 32], F32)
    S32 = alloc("S32", [128, 8 * 65], F32)
    Sbf = alloc("Sbf", [128, 8 * 65], BF16)
    st_su = alloc("st_su", [128, 8], F32)
    st_ss = alloc("st_ss", [128, 8], F32)
    st_m = alloc("st_m", [128, 8], F32)
    st_m2 = alloc("st_m2", [128, 8], F32)
    st_v = alloc("st_v", [128, 8], F32)
    st_r = alloc("st_r", [128, 8], F32)
    st_nm = alloc("st_nm", [128, 8], F32)
    den_sb = alloc("den_sb", [128, 16], F32)
    rec_sb = alloc("rec_sb", [128, 16], F32)
    nd_sb = alloc("nd_sb", [128, 8 * 130], F32)
    epsc = alloc("epsc", [128, 8], F32)
    PERM_END = off[0]

    RX = PERM_END
    xslot = [alloc(f"xslot{i}", [128, E], F32, at=RX + i * 4096) for i in range(8)]
    htb = [alloc(f"htb{i}", [128, 8192], BF16, at=RX + i * 16384) for i in range(2)]
    outb = [alloc(f"outb{i}", [128, E], F32, at=RX + (2 + i) * 4096) for i in range(2)]
    RX_END = RX + 8 * 4096

    RS = RX_END
    scr0 = [alloc(f"scr0{i}", [128, E], F32, at=RS + i * 4096) for i in range(2)]
    bp_row = alloc("bp_row", [1, E], F32, at=RS)
    s3_row = alloc("s3_row", [1, E], F32, at=RS + 4096)
    scrT = alloc("scrT", [128, E], F32, at=RS + 8192)
    eluR = [alloc(f"eluR{i}", [128, 512], F32, at=RS + 12288 + i * 2048) for i in range(2)]
    eluM = [alloc(f"eluM{i}", [128, 512], F32, at=RS + 16384 + i * 2048) for i in range(2)]
    eluE = [alloc(f"eluE{i}", [128, 512], F32, at=RS + 20480 + i * 2048) for i in range(2)]
    scrS = [alloc(f"scrS{i}", [128, 512], F32, at=RS + 12288 + i * 2048) for i in range(2)]
    fo_t = [alloc(f"fo{i}", [128, E], BF16, at=RS + 16384 + i * 2048) for i in range(4)]
    RS_END = RS + 24576

    RB = RS_END
    wT = {}
    for i, nm in enumerate(("wq", "wk", "wv", "wp")):
        wT[nm] = alloc(nm + "T", [128, 8192], BF16, at=RB + i * 16384)
    ln0_fm = alloc("ln0_fm", [128, 4096], BF16, at=RB + 65536)
    wstage = alloc("wstage", [128, 4096], F32, at=RB + 65536)
    phiq_fm = alloc("phiq_fm", [128, 4096], BF16, at=RB + 65536 + 8192)
    phik_fm = alloc("phik_fm", [128, 4096], BF16, at=RB + 65536 + 16384)
    vT_sb = alloc("vT_sb", [128, 4096], BF16, at=RB + 65536 + 24576)
    y_fm = alloc("y_fm", [128, 4096], BF16, at=RB + 65536 + 32768)
    phikT = [alloc(f"phikT{i}", [128, 1024], BF16, at=RB + 106496 + i * 2048)
             for i in range(2)]
    y_tok = [alloc(f"y_tok{i}", [128, 1024], BF16, at=RB + 110592 + i * 2048)
             for i in range(2)]
    ATm = [alloc(f"ATm{i}", [128, 256], BF16, at=RB + 114688 + i * 512)
           for i in range(2)]
    ao_tok = [alloc(f"ao_tok{i}", [128, 1024], BF16, at=RB + 115712 + i * 2048)
              for i in range(2)]
    ln0_tok = [alloc(f"ln0_tok{i}", [128, 1024], BF16, at=RB + 119808 + i * 2048)
               for i in range(4)]
    h_fm = alloc("h_fm", [128, 32768], BF16, at=RB + 0)
    g2_fm = alloc("g2_fm", [128, 32768], BF16, at=RB + 0)
    g1_fm = alloc("g1_fm", [128, 32768], BF16, at=RB + 65536)
    w3T_sb = alloc("w3T_sb", [128, 32768], BF16, at=RB + 65536)
    ln2_fm = alloc("ln2_fm", [128, 8192], BF16, at=RB + 65536)
    ln2_tok = [alloc(f"ln2_tok{i}", [128, 1024], BF16, at=RB + 81920 + i * 2048)
               for i in range(8)]
    w12b = [alloc(f"w12b{i}", [128, 8192], BF16, at=RB + 98304 + i * 16384)
            for i in range(2)]
    RB_END = RB + 131072
    assert RB_END - base <= 212800, f"SBUF overflow: {RB_END - base}"

    psF = [nc.alloc_psum_tensor(f"psF{i}", [128, 512], F32) for i in range(4)]
    psG = [nc.alloc_psum_tensor(f"psG{i}", [128, 512], F32) for i in range(2)]
    psT = [nc.alloc_psum_tensor(f"psT{i}", [128, 1024], BF16) for i in range(2)]

    bF = [Buf(f"psF{i}") for i in range(4)]
    bG = [Buf(f"psG{i}") for i in range(2)]
    bT = [Buf(f"psT{i}") for i in range(2)]
    b_ident = Buf("ident_bf")
    b_identf = Buf("ident_f32")
    b_mask = Buf("mask")
    b_ones = Buf("ones_bf")
    b_ones1 = Buf("ones_f1")
    b_bprow = Buf("bp_row")
    b_s3row = Buf("s3_row")
    b_BP = Buf("BPt")
    b_S3 = Buf("S3t")
    b_scstage = Buf("sc_stage")
    b_s1c, b_s2c, b_h1c, b_h2c = Buf("s1c"), Buf("s2c"), Buf("h1c"), Buf("h2c")
    b_S32 = [Buf(f"S32_{i}") for i in range(8)]
    b_Sbf = [Buf(f"Sbf_{i}") for i in range(8)]
    b_stats = [Buf(f"stats{i}") for i in range(8)]
    b_den = Buf("den")
    b_rec = Buf("rec")
    b_ndsb = [Buf(f"ndsb{i}") for i in range(8)]
    b_epsc = Buf("epsc")
    b_fo = [Buf(f"fo{i}") for i in range(4)]
    b_xslot = [Buf(f"xslot{i}") for i in range(8)]
    b_htb = [Buf(f"htb{i}") for i in range(2)]
    b_outb = [Buf(f"outb{i}") for i in range(2)]
    b_scr0 = [Buf(f"scr0{i}") for i in range(2)]
    b_scrT = Buf("scrT")
    b_eluR = [Buf(f"eluR{i}") for i in range(2)]
    b_eluM = [Buf(f"eluM{i}") for i in range(2)]
    b_eluE = [Buf(f"eluE{i}") for i in range(2)]
    b_scrS = [Buf(f"scrS{i}") for i in range(2)]
    b_wT = {nm: [Buf(f"{nm}T{k}") for k in range(NE)] for nm in ("wq", "wk", "wv", "wp")}
    b_wstage = Buf("wstage")
    b_ln0fm = [Buf(f"ln0fm{c}") for c in range(NE)]
    b_phiq = [Buf(f"phiq{c}") for c in range(NE)]
    b_phik = [Buf(f"phik{c}") for c in range(NE)]
    b_vT = [Buf(f"vT{i}") for i in range(4)]
    b_yfm = [Buf(f"yfm{c}") for c in range(NE)]
    b_phikT = [Buf(f"phikT{i}") for i in range(2)]
    b_ytok = [Buf(f"ytok{i}") for i in range(2)]
    b_ATm = [Buf(f"ATm{i}") for i in range(2)]
    b_ao = [Buf(f"ao{i}") for i in range(2)]
    b_ln0tok = [Buf(f"ln0tok{i}") for i in range(4)]
    b_hfm = [Buf(f"hfm{k}") for k in range(NHID)]
    b_g1 = [Buf(f"g1_{k}") for k in range(NHID)]
    b_g2 = [Buf(f"g2_{k}") for k in range(NHID)]
    b_w3T = [Buf(f"w3T{k}") for k in range(NHID)]
    b_ln2fm = [Buf(f"ln2fm{c}") for c in range(NE)]
    b_ln2tok = [Buf(f"ln2tok{i}") for i in range(8)]
    b_w12b = [Buf(f"w12b{i}") for i in range(2)]
    b_ws = {nm: Buf(nm + "_s") for nm in ("w1", "w2", "h1", "h2", "w3")}
    b_x2d = [Buf(f"x2d{i}") for i in range(NT)]
    b_yd = [Buf(f"yd{i}") for i in range(NT)]

    ws_map = {"w1": (w1_d, w1_s), "w2": (w2_d, w2_s),
              "h1": (h1_d, h1_s), "h2": (h2_d, h2_s), "w3": (w3_d, w3_s)}

    P.op("gpsimd", lambda e: e.memset(ident_bf[:, :], 0.0), writes=[b_ident])
    P.op("gpsimd", lambda e: e.affine_select(
        out=ident_bf[:, :], in_=ident_bf[:, :], compare_op=ALU.not_equal,
        fill=1.0, base=0, pattern=[[-1, 128]], channel_multiplier=1),
        reads=[b_ident], writes=[b_ident])
    P.op("gpsimd", lambda e: e.memset(ident_f32[:, :], 0.0), writes=[b_identf])
    P.op("gpsimd", lambda e: e.affine_select(
        out=ident_f32[:, :], in_=ident_f32[:, :], compare_op=ALU.not_equal,
        fill=1.0, base=0, pattern=[[-1, 128]], channel_multiplier=1),
        reads=[b_identf], writes=[b_identf])
    P.op("gpsimd", lambda e: e.memset(mask_f[:, :], 0.0), writes=[b_mask])
    P.op("gpsimd", lambda e: e.affine_select(
        out=mask_f[:, :], in_=mask_f[:, :], compare_op=ALU.is_gt, fill=1.0,
        base=0, pattern=[[-1, 128]], channel_multiplier=1),
        reads=[b_mask], writes=[b_mask])
    P.op("gpsimd", lambda e: e.memset(ones_bf[:, :], 1.0), writes=[b_ones])
    P.op("gpsimd", lambda e: e.memset(ones_f1[:, :], 1.0), writes=[b_ones1])
    P.op("gpsimd", lambda e: e.memset(epsc[:, :], 100000.0), writes=[b_epsc])
    P.op("vector", lambda e: e.reciprocal(epsc[:, 0:1], epsc[:, 1:2]),
         reads=[b_epsc], writes=[b_epsc])
    P.op("gpsimd", lambda e: e.memset(S32[:, :], 0.0), writes=b_S32)
    P.op("gpsimd", lambda e: e.memset(Sbf[:, :], 0.0), writes=b_Sbf)


    def load_x(tt):
        sl = tt % 8
        P.op("sync", lambda e, tt=tt, sl=sl: e.dma_start(
            out=xslot[sl][:, :], in_=x_d[tt * 128:(tt + 1) * 128, :]),
            writes=[b_xslot[sl]], dma=f"qx{sl}")

    for tt in range(4):
        load_x(tt)
    P.op("sync", lambda e: e.dma_start(out=bp_row[:, :], in_=bp_d[None, :]),
         writes=[b_bprow], dma="qsc_bp")
    P.op("sync", lambda e: e.dma_start(out=s3_row[:, :], in_=s3_d[None, :]),
         writes=[b_s3row], dma="qsc_s3")
    for si, (sd, tb, bb) in enumerate(((s1_d, s1c, b_s1c), (s2_d, s2c, b_s2c),
                                       (sh1_d, h1c, b_h1c), (sh2_d, h2c, b_h2c))):
        P.op("sync", lambda e, sd=sd: e.dma_start(
            out=sc_stage[:, :], in_=sd.rearrange("(c p) -> c p", p=128)),
            writes=[b_scstage], dma=f"qsc_{si}")
        P.op("tensor", lambda e: e.transpose(
            psF[0][:, 0:32], sc_stage[:, :], ident_f32[0:32, 0:32]),
            reads=[b_scstage, b_identf], writes=[bF[0]])
        P.op("vector", lambda e, tb=tb: e.tensor_copy(tb[:, :], psF[0][:, 0:32]),
             reads=[bF[0]], writes=[bb])
    for row, tile, bt, brow in ((bp_row, BPt, b_BP, b_bprow),
                                (s3_row, S3t, b_S3, b_s3row)):
        for half in range(2):
            P.op("tensor", lambda e, row=row, half=half: e.matmul(
                psF[1][:, 0:512], ones_f1[:, :], row[:, half * 512:(half + 1) * 512],
                start=True, stop=True),
                reads=[b_ones1, brow], writes=[bF[1]])
            P.op("vector", lambda e, tile=tile, half=half: e.tensor_copy(
                tile[:, half * 512:(half + 1) * 512], psF[1][:, 0:512]),
                reads=[bF[1]], writes=[bt], wmode="part")

    P.claim([b_scr0[0]], [b_bprow])
    P.claim([b_scr0[1]], [b_s3row])

    def ln_stats_apply(src_tile, b_src, stat, b_stat, out_tile, b_out, sc,
                       apply_eng="vector"):
        i = stat
        P.op("scalar", lambda e: e.activation(
            scr0[sc][:, :], src_tile[:, :], AF.Square,
            accum_out=st_ss[:, i:i + 1]),
            reads=[b_src], writes=[b_scr0[sc], b_stat])
        P.op("scalar", lambda e: e.activation(
            scr0[sc][:, :], src_tile[:, :], AF.Copy,
            accum_out=st_su[:, i:i + 1]),
            reads=[b_src], writes=[b_scr0[sc], b_stat])
        P.op("vector", lambda e: e.tensor_scalar(
            st_m[:, i:i + 1], st_su[:, i:i + 1], 1.0 / E, None, ALU.mult),
            reads=[b_stat], writes=[b_stat])
        P.op("vector", lambda e: e.tensor_mul(
            st_m2[:, i:i + 1], st_m[:, i:i + 1], st_m[:, i:i + 1]),
            reads=[b_stat], writes=[b_stat])
        P.op("vector", lambda e: e.scalar_tensor_tensor(
            st_v[:, i:i + 1], st_ss[:, i:i + 1], 1.0 / E, st_m2[:, i:i + 1],
            ALU.mult, ALU.subtract),
            reads=[b_stat], writes=[b_stat])
        P.op("scalar", lambda e: e.activation(
            st_v[:, i:i + 1], st_v[:, i:i + 1], AF.Ln, bias=epsc[:, 0:1]),
            reads=[b_stat, b_epsc], writes=[b_stat])
        P.op("scalar", lambda e: e.activation(
            st_r[:, i:i + 1], st_v[:, i:i + 1], AF.Exp, scale=-0.5),
            reads=[b_stat], writes=[b_stat])
        P.op("vector", lambda e: e.scalar_tensor_tensor(
            st_nm[:, i:i + 1], st_m[:, i:i + 1], -1.0, st_r[:, i:i + 1],
            ALU.mult, ALU.mult),
            reads=[b_stat], writes=[b_stat])
        if out_tile is not None:
            P.op(apply_eng, lambda e: e.tensor_scalar(
                out_tile[:, :], src_tile[:, :], st_r[:, i:i + 1],
                st_nm[:, i:i + 1], ALU.mult, ALU.add),
                reads=[b_src, b_stat], writes=[b_out])

    def transpose_set(src_tiles, b_srcs, col0, dst, b_dst, dcol0, par, n_in=4):
        for j in range(n_in):
            P.op("tensor", lambda e, j=j, col0=col0, par=par: e.transpose(
                psT[par][:, j * 128:(j + 1) * 128],
                src_tiles[j][:, col0:col0 + 128], ident_bf[:, :]),
                reads=[b_srcs[j], b_ident], writes=[bT[par]], wmode="part")
        P.op("vector", lambda e, par=par, dcol0=dcol0, n_in=n_in: e.tensor_copy(
            dst[:, dcol0:dcol0 + n_in * 128], psT[par][:, 0:n_in * 128]),
            reads=[bT[par]], writes=[b_dst], wmode="part")

    def ln0_span(s):
        for tl in range(4):
            tt = 4 * s + tl
            sl = tt % 8
            ln_stats_apply(xslot[sl], b_xslot[sl], tl, b_stats[tl],
                           ln0_tok[tl], b_ln0tok[tl], tl % 2,
                           apply_eng="gpsimd")

    ln0_span(0)
    for nm, wd in (("wq", wq_d), ("wk", wk_d), ("wv", wv_d), ("wp", wp_d)):
        for rnd in range(2):
            ks = range(rnd * 4, rnd * 4 + 4)
            for j, k in enumerate(ks):
                P.op("sync", lambda e, wd=wd, k=k, j=j: e.dma_start(
                    out=wstage[:, j * 1024:(j + 1) * 1024],
                    in_=wd[k * 128:(k + 1) * 128, :]),
                    writes=[b_wstage], wmode=("part" if j else "full"),
                    dma=f"qw_{nm}")
            _qw = P.dsem(f"qw_{nm}")
            b_wstage.pw = [(_qw, _qw.n)]
            for j, k in enumerate(ks):
                P.op("vector", lambda e, nm=nm, k=k, j=j: e.tensor_copy(
                    wT[nm][:, k * 1024:(k + 1) * 1024],
                    wstage[:, j * 1024:(j + 1) * 1024]),
                    reads=[b_wstage], writes=[b_wT[nm][k]])
        P.seal(b_wT[nm])
        if nm == "wk":
            for tt in range(4, 8):
                load_x(tt)

    P.claim(b_ln0fm + b_phiq, [b_wstage])

    gate = [(P.dsem(f"qw_{nm}"), P.dsem(f"qw_{nm}").n)
            for nm in ("wq", "wk", "wv", "wp")]
    gate += [(P.dsem(f"qx{i}"), P.dsem(f"qx{i}").n) for i in range(8)]
    _first_cast = [True]

    def cast_weight(nm):
        src_, dst = ws_map[nm]
        rows = src_.shape[0]
        nchunk = 4 if rows * src_.shape[1] >= HID * HID else 1
        rr = rows // nchunk
        for j in range(nchunk):
            ew = gate if _first_cast[0] else ()
            _first_cast[0] = False
            P.op("gpsimd", lambda e, src_=src_, dst=dst, j=j, rr=rr: e.dma_start(
                out=dst[j * rr:(j + 1) * rr, :], in_=src_[j * rr:(j + 1) * rr, :]),
                writes=[b_ws[nm]], wmode=("part" if nchunk > 1 else "full"),
                dma="qc_" + nm, extra_waits=ew)

    for nm in ("w1", "w2", "h1", "h2", "w3"):
        cast_weight(nm)

    for s in range(NSPAN):
        for cp in range(4):
            par2 = cp % 2
            for half2 in range(2):
                c = 2 * cp + half2
                for j in range(4):
                    P.op("tensor", lambda e, j=j, c=c, half2=half2, par2=par2: e.transpose(
                        psT[par2][:, half2 * 512 + j * 128:half2 * 512 + (j + 1) * 128],
                        ln0_tok[j][:, c * 128:(c + 1) * 128], ident_bf[:, :]),
                        reads=[b_ln0tok[j], b_ident], writes=[bT[par2]], wmode="part")
            P.op("vector", lambda e, cp=cp, par2=par2: e.tensor_copy(
                ln0_fm[:, cp * 1024:(cp + 1) * 1024], psT[par2][:, :]),
                reads=[bT[par2]], writes=[b_ln0fm[2 * cp], b_ln0fm[2 * cp + 1]],
                wmode="part")
        P.seal(b_ln0fm)
        qbanks = [(psG[0], bG[0]), (psG[1], bG[1]), (psF[0], bF[0]), (psF[1], bF[1])]

        def proj_fm(wname, dst, b_dst, kind):
            for oc in range(NE):
                pg = oc % 2
                psq, bq = qbanks[oc % 4]
                for k in range(NE):
                    P.op("tensor", lambda e, wname=wname, oc=oc, k=k, psq=psq: e.matmul(
                        psq[:, 0:512],
                        wT[wname][:, k * 1024 + oc * 128:k * 1024 + oc * 128 + 128],
                        ln0_fm[:, k * 512:(k + 1) * 512],
                        start=(k == 0), stop=(k == NE - 1)),
                        reads=[b_wT[wname][k], b_ln0fm[k]], writes=[bq],
                        wmode=("full" if k == NE - 1 else "mid"))
                if kind == "phi":
                    P.op("scalar", lambda e, pg=pg, oc=oc, psq=psq: e.activation(
                        eluR[pg][:, :], psq[:, 0:512], AF.Relu),
                        reads=[bq], writes=[b_eluR[pg]])
                    P.op("vector", lambda e, pg=pg, psq=psq: e.tensor_sub(
                        eluM[pg][:, :], psq[:, 0:512], eluR[pg][:, :]),
                        reads=[bq, b_eluR[pg]], writes=[b_eluM[pg]])
                    P.op("scalar", lambda e, pg=pg: e.activation(
                        eluE[pg][:, :], eluM[pg][:, :], AF.Exp),
                        reads=[b_eluM[pg]], writes=[b_eluE[pg]])
                    P.op("vector", lambda e, pg=pg, oc=oc, dst=dst: e.tensor_tensor(
                        dst[:, oc * 512:(oc + 1) * 512], eluR[pg][:, :],
                        eluE[pg][:, :], ALU.add),
                        reads=[b_eluR[pg], b_eluE[pg]], writes=[b_dst[oc]])
                else:
                    P.op("vector", lambda e, pg=pg, oc=oc, dst=dst, psq=psq: e.tensor_copy(
                        dst[:, oc * 512:(oc + 1) * 512], psq[:, 0:512]),
                        reads=[bq], writes=[b_dst[oc]])

        proj_fm("wk", phik_fm, b_phik, "phi")
        P.seal(b_phik)
        proj_fm("wq", phiq_fm, b_phiq, "phi")
        P.seal(b_phiq)

        def v_proj(tl):
            for osp in range(2):
                if tl == 0:
                    psq, bq = qbanks[osp % 4 if False else (tl * 2 + osp) % 4]
                else:
                    psq, bq = psG[(tl * 2 + osp) % 2], bG[(tl * 2 + osp) % 2]
                for k in range(NE):
                    P.op("tensor", lambda e, tl=tl, osp=osp, k=k, psq=psq: e.matmul(
                        psq[:, 0:512],
                        ln0_fm[:, k * 512 + tl * 128:k * 512 + tl * 128 + 128],
                        wT["wv"][:, k * 1024 + osp * 512:k * 1024 + osp * 512 + 512],
                        start=(k == 0), stop=(k == NE - 1)),
                        reads=[b_ln0fm[k], b_wT["wv"][k]], writes=[bq],
                        wmode=("full" if k == NE - 1 else "mid"))
                P.op("vector", lambda e, tl=tl, osp=osp, psq=psq: e.tensor_copy(
                    vT_sb[:, tl * 1024 + osp * 512:tl * 1024 + osp * 512 + 512],
                    psq[:, 0:512]),
                    reads=[bq], writes=[b_vT[tl]], wmode="part")

        v_proj(0)

        def y_transposes(nl):
            parx = nl % 2
            for c in range(NE):
                P.op("tensor", lambda e, c=c, parx=parx: e.transpose(
                    psT[parx][:, c * 128:(c + 1) * 128],
                    y_tok[parx][:, c * 128:(c + 1) * 128], ident_bf[:, :]),
                    reads=[b_ytok[parx], b_ident], writes=[bT[parx]], wmode="part")
            for c in range(NE):
                P.op("vector", lambda e, c=c, parx=parx, nl=nl: e.tensor_copy(
                    y_fm[:, c * 512 + nl * 128:c * 512 + nl * 128 + 128],
                    psT[parx][:, c * 128:(c + 1) * 128]),
                    reads=[bT[parx]], writes=[b_yfm[c]], wmode="part")

        for nl in range(4):
            par = nl % 2
            def emit_AT(hp):
                for hi in range(2):
                    fb = bF[hi]
                    pb = 64 * hi
                    P.op("tensor", lambda e, hp=hp, hi=hi, nl=nl, pb=pb: e.matmul(
                        psF[hi][:, 0:128],
                        phik_fm[pb:pb + 64, hp * 512 + nl * 128:hp * 512 + nl * 128 + 128],
                        phiq_fm[pb:pb + 64, hp * 512 + nl * 128:hp * 512 + nl * 128 + 128],
                        start=True, stop=True),
                        reads=[b_phik[hp], b_phiq[hp]], writes=[fb])
                    P.op("vector", lambda e, hp=hp, hi=hi: e.tensor_tensor(
                        ATm[hp % 2][:, hi * 128:(hi + 1) * 128],
                        psF[hi][:, 0:128], mask_f[:, :], ALU.mult),
                        reads=[fb, b_mask], writes=[b_ATm[hp % 2]], wmode="part")

            emit_AT(0)
            for hp in range(8):
                fn_ = bF[2 + hp % 2]
                if hp + 1 < 8:
                    emit_AT(hp + 1)
                for hi in range(2):
                    h = 2 * hp + hi
                    pb = 64 * hi
                    bs = hi * 65
                    P.op("tensor", lambda e, hp=hp, hi=hi, h=h, nl=nl, bs=bs: e.matmul(
                        psF[2 + hp % 2][:, bs:bs + 64],
                        ATm[hp % 2][:, hi * 128:(hi + 1) * 128],
                        vT_sb[:, nl * 1024 + h * 64:nl * 1024 + h * 64 + 64],
                        start=(hi == 0), stop=False),
                        reads=[b_ATm[hp % 2], b_vT[nl]], writes=[fn_], wmode="mid")
                    P.op("tensor", lambda e, hp=hp, hi=hi, bs=bs: e.matmul(
                        psF[2 + hp % 2][:, bs + 64:bs + 65],
                        ATm[hp % 2][:, hi * 128:(hi + 1) * 128],
                        ones_bf[:, 0:1], start=False, stop=False),
                        reads=[b_ATm[hp % 2], b_ones], writes=[fn_], wmode="mid")
                    P.op("tensor", lambda e, hp=hp, hi=hi, nl=nl, pb=pb, bs=bs: e.matmul(
                        psF[2 + hp % 2][:, bs:bs + 65],
                        phiq_fm[pb:pb + 64, hp * 512 + nl * 128:hp * 512 + nl * 128 + 128],
                        Sbf[pb:pb + 64, hp * 65:hp * 65 + 65],
                        start=False, stop=(hi == 1)),
                        reads=[b_phiq[hp], b_Sbf[hp]], writes=[fn_],
                        wmode=("full" if hi == 1 else "mid"))
                P.op("scalar", lambda e, hp=hp: e.activation(
                    nd_sb[:, hp * 130:(hp + 1) * 130], psF[2 + hp % 2][:, 0:130],
                    AF.Copy),
                    reads=[fn_], writes=[b_ndsb[hp]])
            for c in range(NE):
                P.op("tensor", lambda e, c=c, nl=nl, par=par: e.transpose(
                    psT[par][:, c * 128:(c + 1) * 128],
                    phik_fm[:, c * 512 + nl * 128:c * 512 + nl * 128 + 128],
                    ident_bf[:, :]),
                    reads=[b_phik[c], b_ident], writes=[bT[par]], wmode="part")
            P.op("vector", lambda e, par=par: e.tensor_copy(
                phikT[par][:, :], psT[par][:, :]),
                reads=[bT[par]], writes=[b_phikT[par]])
            for hp in range(8):
                gb = bG[hp % 2]
                for hi in range(2):
                    h = 2 * hp + hi
                    pb = 64 * hi
                    P.op("tensor", lambda e, h=h, hp=hp, hi=hi, nl=nl, pb=pb, par=par: e.matmul(
                        psG[hp % 2][pb:pb + 64, 0:64],
                        phikT[par][:, h * 64:h * 64 + 64],
                        vT_sb[:, nl * 1024 + h * 64:nl * 1024 + h * 64 + 64],
                        start=True, stop=False),
                        reads=[b_phikT[par], b_vT[nl]], writes=[gb], wmode="mid")
                    P.op("tensor", lambda e, h=h, hp=hp, hi=hi, pb=pb, par=par: e.matmul(
                        psG[hp % 2][pb:pb + 64, 64:65],
                        phikT[par][:, h * 64:h * 64 + 64],
                        ones_bf[:, 0:1], start=False, stop=True),
                        reads=[b_phikT[par], b_ones], writes=[gb],
                        wmode="part")
                P.op("vector", lambda e, hp=hp: e.tensor_add(
                    S32[:, hp * 65:hp * 65 + 65], S32[:, hp * 65:hp * 65 + 65],
                    psG[hp % 2][:, 0:65]),
                    reads=[b_S32[hp], gb], writes=[b_S32[hp]])
                P.op("scalar", lambda e, hp=hp: e.activation(
                    Sbf[:, hp * 65:hp * 65 + 65], S32[:, hp * 65:hp * 65 + 65],
                    AF.Copy),
                    reads=[b_S32[hp]], writes=[b_Sbf[hp]])
            if nl + 1 < 4:
                v_proj(nl + 1)
            if nl > 0:
                y_transposes(nl - 1)
            dsrc = bass.AP(nd_sb, 64, [[8 * 130, 128], [130, 8], [65, 2]])
            P.op("scalar", lambda e, dsrc=dsrc: e.activation(
                den_sb[:, 0:16], dsrc, AF.Copy, bias=ATEPS),
                reads=b_ndsb, writes=[b_den])
            P.op("vector", lambda e: e.reciprocal(rec_sb[:, 0:16], den_sb[:, 0:16]),
                 reads=[b_den], writes=[b_rec])
            for h in range(16):
                hp, hi = h // 2, h % 2
                P.op("scalar", lambda e, h=h, hp=hp, hi=hi, par=par: e.activation(
                    y_tok[par][:, h * 64:h * 64 + 64],
                    nd_sb[:, hp * 130 + hi * 65:hp * 130 + hi * 65 + 64], AF.Copy,
                    scale=rec_sb[:, h:h + 1]),
                    reads=[b_ndsb[hp], b_rec], writes=[b_ytok[par]], wmode="part")
        y_transposes(3)
        P.seal(b_yfm)
        if s + 1 < NSPAN:
            ln0_span(s + 1)

        for tl in range(4):
            tt = 4 * s + tl
            sl = tt % 8
            for osp in range(2):
                psq, bq = qbanks[(tl * 2 + osp) % 4]
                for k in range(NE):
                    P.op("tensor", lambda e, tl=tl, osp=osp, k=k, psq=psq: e.matmul(
                        psq[:, 0:512],
                        y_fm[:, k * 512 + tl * 128:k * 512 + tl * 128 + 128],
                        wT["wp"][:, k * 1024 + osp * 512:k * 1024 + osp * 512 + 512],
                        start=(k == 0), stop=(k == NE - 1)),
                        reads=[b_yfm[k], b_wT["wp"][k]], writes=[bq],
                        wmode=("full" if k == NE - 1 else "mid"))
                P.op("vector", lambda e, tl=tl, osp=osp, psq=psq: e.tensor_tensor(
                    ao_tok[tl % 2][:, osp * 512:(osp + 1) * 512],
                    psq[:, 0:512], BPt[:, osp * 512:(osp + 1) * 512], ALU.add),
                    reads=[bq, b_BP], writes=[b_ao[tl % 2]], wmode="part")
            ln_stats_apply(ao_tok[tl % 2], b_ao[tl % 2], 4 + (tl % 2),
                           b_stats[4 + (tl % 2)], None, None, tl % 2)
            i = 4 + (tl % 2)
            P.op("vector", lambda e, tl=tl, i=i: e.tensor_scalar(
                scrT[:, :], ao_tok[tl % 2][:, :], st_r[:, i:i + 1],
                st_nm[:, i:i + 1], ALU.mult, ALU.add),
                reads=[b_ao[tl % 2], b_stats[i]], writes=[b_scrT])
            P.op("vector", lambda e, sl=sl: e.tensor_add(
                xslot[sl][:, :], xslot[sl][:, :], scrT[:, :]),
                reads=[b_xslot[sl], b_scrT], writes=[b_xslot[sl]])
            P.op("scalar", lambda e, tt=tt, sl=sl: e.dma_start(
                out=x2_d[tt * 128:(tt + 1) * 128, :], in_=xslot[sl][:, :]),
                reads=[b_xslot[sl]], writes=[b_x2d[tt]], dma=f"qx2s{sl}")
        if 4 * (s + 2) < NT:
            for tt in range(4 * (s + 2), 4 * (s + 2) + 4):
                load_x(tt)

    for tt in range(NT):
        _qs = P.dsem(f"qx2s{tt % 8}")
        b_x2d[tt].pw = [(_qs, _qs.n)]

    P.claim(b_hfm, sum([b_wT[nm] for nm in ("wq", "wk", "wv", "wp")], []))
    P.claim(b_fo, [b_eluM[0], b_eluM[1], b_eluE[0], b_eluE[1]])
    P.claim(b_ln2fm + b_ln2tok + b_w12b + [b_scrS[0], b_scrS[1]],
            b_ln0fm + b_phiq + b_phik + [b_vT[i] for i in range(4)] +
            b_yfm + b_phikT + b_ytok + b_ATm + b_ao + b_ln0tok +
            [b_eluR[0], b_eluR[1], b_eluM[0], b_eluM[1], b_eluE[0], b_eluE[1]])
    P.claim(b_htb + b_outb, b_xslot)

    x2ld_total = P.dsem("qx2s")

    for sl in range(NSLAB):
        ts0 = sl * 8
        for tl in range(8):
            tt = ts0 + tl
            slx = tl % 2
            P.op("sync", lambda e, tt=tt, slx=slx: e.dma_start(
                out=xslot[slx][:, :], in_=x2_d[tt * 128:(tt + 1) * 128, :]),
                reads=[b_x2d[tt]], writes=[b_xslot[slx]], dma=f"qx2l{slx}")
            ln_stats_apply(xslot[slx], b_xslot[slx], tl % 8, b_stats[tl % 8],
                           ln2_tok[tl], b_ln2tok[tl], tl % 2)
        for c in range(NE):
            par = c % 2
            for j in range(8):
                P.op("tensor", lambda e, c=c, j=j, par=par: e.transpose(
                    psT[par][:, j * 128:(j + 1) * 128],
                    ln2_tok[j][:, c * 128:(c + 1) * 128], ident_bf[:, :]),
                    reads=[b_ln2tok[j], b_ident], writes=[bT[par]], wmode="part")
            P.op("vector", lambda e, c=c, par=par: e.tensor_copy(
                ln2_fm[:, c * 1024:(c + 1) * 1024], psT[par][:, :]),
                reads=[bT[par]], writes=[b_ln2fm[c]])
        P.seal(b_ln2fm)

        def load_w12(osp):
            wb = osp % 2
            P.op("sync", lambda e, osp=osp, wb=wb: e.dma_start(
                out=w12b[wb][:, 0:4096].rearrange("p (k o) -> p k o", k=NE),
                in_=w1_s[:, osp * 512:(osp + 1) * 512]
                .rearrange("(k p) o -> p k o", p=128)),
                reads=[b_ws["w1"]], writes=[b_w12b[wb]], dma=f"qw12{wb}")
            P.op("sync", lambda e, osp=osp, wb=wb: e.dma_start(
                out=w12b[wb][:, 4096:8192].rearrange("p (k o) -> p k o", k=NE),
                in_=w2_s[:, osp * 512:(osp + 1) * 512]
                .rearrange("(k p) o -> p k o", p=128)),
                reads=[b_ws["w2"]], writes=[b_w12b[wb]], wmode="part",
                dma=f"qw12{wb}")

        load_w12(0)
        load_w12(1)
        for osp in range(8):
            wb = osp % 2
            for oc4 in range(4):
                oc = osp * 4 + oc4
                for tc in range(2):
                    pa = 2 * ((oc4 * 2 + tc) % 2)
                    for k in range(NE):
                        P.op("tensor", lambda e, oc4=oc4, tc=tc, k=k, pa=pa, wb=wb: e.matmul(
                            psF[pa][:, 0:512],
                            w12b[wb][:, k * 512 + oc4 * 128:k * 512 + oc4 * 128 + 128],
                            ln2_fm[:, k * 1024 + tc * 512:k * 1024 + tc * 512 + 512],
                            start=(k == 0), stop=(k == NE - 1)),
                            reads=[b_w12b[wb], b_ln2fm[k]], writes=[bF[pa]],
                            wmode=("full" if k == NE - 1 else "mid"))
                    for k in range(NE):
                        P.op("tensor", lambda e, oc4=oc4, tc=tc, k=k, pa=pa, wb=wb: e.matmul(
                            psF[pa + 1][:, 0:512],
                            w12b[wb][:, 4096 + k * 512 + oc4 * 128:4096 + k * 512 + oc4 * 128 + 128],
                            ln2_fm[:, k * 1024 + tc * 512:k * 1024 + tc * 512 + 512],
                            start=(k == 0), stop=(k == NE - 1)),
                            reads=[b_w12b[wb], b_ln2fm[k]], writes=[bF[pa + 1]],
                            wmode=("full" if k == NE - 1 else "mid"))
                    sp = (oc4 * 2 + tc) % 2
                    P.op("scalar", lambda e, oc=oc, pa=pa, sp=sp: e.activation(
                        scrS[sp][:, :], psF[pa][:, 0:512], AF.Silu,
                        scale=s1c[:, oc:oc + 1]),
                        reads=[bF[pa], b_s1c], writes=[b_scrS[sp]])
                    P.op("vector", lambda e, oc=oc, tc=tc, pa=pa, sp=sp: e.scalar_tensor_tensor(
                        h_fm[:, oc * 1024 + tc * 512:oc * 1024 + tc * 512 + 512],
                        scrS[sp][:, :], s2c[:, oc:oc + 1], psF[pa + 1][:, 0:512],
                        ALU.mult, ALU.mult),
                        reads=[b_scrS[sp], b_s2c, bF[pa + 1]], writes=[b_hfm[oc]],
                        wmode="part")
            if osp + 2 < 8:
                load_w12(osp + 2)
        P.seal(b_hfm)

        def hidden_stage(w_s, b_wsrc, scol, b_scol, src_bufs, src_t, dst, b_dst):
            def load_span(osp):
                wb2 = osp % 2
                P.op("sync", lambda e, osp=osp, wb2=wb2, w_s=w_s: e.dma_start(
                    out=htb[wb2][:, :].rearrange("p (k o) -> p k o", k=NHID),
                    in_=w_s[:, osp * 256:(osp + 1) * 256]
                    .rearrange("(k p) o -> p k o", p=128)),
                    reads=[b_wsrc], writes=[b_htb[wb2]], dma=f"qht{wb2}")

            load_span(0)
            load_span(1)
            for osp in range(16):
                wb2 = osp % 2
                for oc2 in range(2):
                    oc = osp * 2 + oc2
                    for tc in range(2):
                        pa = (oc2 * 2 + tc) % 4
                        for k in range(NHID):
                            P.op("tensor", lambda e, oc2=oc2, tc=tc, k=k, pa=pa, wb2=wb2: e.matmul(
                                psF[pa][:, 0:512],
                                htb[wb2][:, k * 256 + oc2 * 128:k * 256 + oc2 * 128 + 128],
                                src_t[:, k * 1024 + tc * 512:k * 1024 + tc * 512 + 512],
                                start=(k == 0), stop=(k == NHID - 1)),
                                reads=[b_htb[wb2], src_bufs[k]], writes=[bF[pa]],
                                wmode=("full" if k == NHID - 1 else "mid"))
                        P.op("scalar", lambda e, oc=oc, tc=tc, pa=pa: e.activation(
                            dst[:, oc * 1024 + tc * 512:oc * 1024 + tc * 512 + 512],
                            psF[pa][:, 0:512], AF.Gelu, scale=scol[:, oc:oc + 1]),
                            reads=[bF[pa], b_scol], writes=[b_dst[oc]], wmode="part")
                if osp + 2 < 16:
                    load_span(osp + 2)

        P.claim(b_g1, b_ln2fm + b_ln2tok + b_w12b)
        P.claim(b_htb, b_xslot + b_outb)
        hidden_stage(h1_s, b_ws["h1"], h1c, b_h1c, b_hfm, h_fm, g1_fm, b_g1)
        P.seal(b_g1)
        P.claim(b_g2, b_hfm)
        hidden_stage(h2_s, b_ws["h2"], h2c, b_h2c, b_g1, g1_fm, g2_fm, b_g2)
        P.seal(b_g2)

        P.seal(b_g1)
        P.claim(b_w3T, b_g1)
        for hw3 in range(2):
            P.op("sync", lambda e, hw3=hw3: e.dma_start(
                out=w3T_sb[:, hw3 * 16384:(hw3 + 1) * 16384]
                .rearrange("p (k o) -> p k o", k=NHID // 2),
                in_=w3_s[hw3 * 2048:(hw3 + 1) * 2048, :]
                .rearrange("(k p) o -> p k o", p=128)),
                reads=[b_ws["w3"]], writes=b_w3T[hw3 * 16:(hw3 + 1) * 16],
                dma="qw3")
        P.claim([b_xslot[0], b_xslot[1]], b_htb)
        P.claim(b_outb, b_htb)
        for half in range(2):
            loop = ([(osp, tl) for osp in range(4) for tl in range(4)] if half == 0
                    else [(osp, tl) for tl in range(4) for osp in range(4)])
            for idx, (osp, tl) in enumerate(loop):
                    gt = half * 4 + tl
                    pa = idx % 4
                    for k in range(NHID):
                        P.op("tensor", lambda e, gt=gt, osp=osp, k=k, pa=pa: e.matmul(
                            psF[pa][:, 0:256],
                            g2_fm[:, k * 1024 + gt * 128:k * 1024 + gt * 128 + 128],
                            w3T_sb[:, k * 1024 + osp * 256:k * 1024 + osp * 256 + 256],
                            start=(k == 0), stop=(k == NHID - 1)),
                            reads=[b_g2[k], b_w3T[k]], writes=[bF[pa]],
                            wmode=("full" if k == NHID - 1 else "mid"))
                    P.op("vector", lambda e, tl=tl, osp=osp, pa=pa: e.tensor_tensor(
                        fo_t[tl][:, osp * 256:(osp + 1) * 256], psF[pa][:, 0:256],
                        S3t[:, osp * 256:(osp + 1) * 256], ALU.mult),
                        reads=[bF[pa], b_S3], writes=[b_fo[tl]], wmode="part")
            for tl in range(4):
                gt = half * 4 + tl
                tt = ts0 + gt
                slx = tl % 2
                P.op("sync", lambda e, tt=tt, slx=slx: e.dma_start(
                    out=xslot[slx][:, :], in_=x2_d[tt * 128:(tt + 1) * 128, :]),
                    reads=[b_x2d[tt]], writes=[b_xslot[slx]], dma=f"qx2l{slx}")
                ln_stats_apply(fo_t[tl], b_fo[tl], tl % 8, b_stats[tl % 8],
                               None, None, tl % 2)
                i = tl % 8
                P.op("vector", lambda e, tl=tl, i=i: e.tensor_scalar(
                    scrT[:, :], fo_t[tl][:, :], st_r[:, i:i + 1],
                    st_nm[:, i:i + 1], ALU.mult, ALU.add),
                    reads=[b_fo[tl], b_stats[i]], writes=[b_scrT])
                ob = tl % 2
                P.op("vector", lambda e, slx=slx, ob=ob: e.tensor_add(
                    outb[ob][:, :], xslot[slx][:, :], scrT[:, :]),
                    reads=[b_xslot[slx], b_scrT], writes=[b_outb[ob]])
                P.op("gpsimd", lambda e, tt=tt, ob=ob: e.dma_start(
                    out=y_d[tt * 128:(tt + 1) * 128, :], in_=outb[ob][:, :]),
                    reads=[b_outb[ob]], writes=[b_yd[tt]], dma=f"qy{ob}")

        if sl + 1 < NSLAB:
            P.claim(b_hfm, b_g2)
            P.claim(b_ln2fm + b_ln2tok + b_w12b, b_w3T)
            P.claim([b_xslot[0], b_xslot[1]], b_htb + b_outb)

    P.op("gpsimd", lambda e: e.nop(),
         extra_waits=[(P.dsem("qy0"), P.dsem("qy0").n),
                      (P.dsem("qy1"), P.dsem("qy1").n)])

    with ExitStack() as sstk:
        for s in P.sems():
            s.h = sstk.enter_context(nc.semaphore(s.name))
        with nc.Block() as block:
            P.replay(block)
    return nc


_NC_CACHE = []


def _get_nc():
    if not _NC_CACHE:
        _NC_CACHE.append(build_nc())
    return _NC_CACHE[0]


def kernel(L0_attn_x, L0_attn_Wq, L0_attn_Wk, L0_attn_Wv, L0_attn_Wp,
           L0_attn_bp, L0_ffn_w1_w, L0_ffn_w1_s, L0_ffn_w2_w, L0_ffn_w2_s,
           L0_ffn_h1_w, L0_ffn_h1_s, L0_ffn_h2_w, L0_ffn_h2_s, L0_ffn_w3_w,
           L0_ffn_w3_s):
    x = np.ascontiguousarray(np.asarray(L0_attn_x, dtype=np.float32))
    common = {
        "wq": np.ascontiguousarray(np.asarray(L0_attn_Wq, np.float32).T),
        "wk": np.ascontiguousarray(np.asarray(L0_attn_Wk, np.float32).T),
        "wv": np.ascontiguousarray(np.asarray(L0_attn_Wv, np.float32).T),
        "wp": np.ascontiguousarray(np.asarray(L0_attn_Wp, np.float32).T),
        "bp": np.ascontiguousarray(np.asarray(L0_attn_bp, np.float32)),
        "w1": np.ascontiguousarray(np.asarray(L0_ffn_w1_w, np.float32).T),
        "w2": np.ascontiguousarray(np.asarray(L0_ffn_w2_w, np.float32).T),
        "h1": np.ascontiguousarray(np.asarray(L0_ffn_h1_w, np.float32).T),
        "h2": np.ascontiguousarray(np.asarray(L0_ffn_h2_w, np.float32).T),
        "w3": np.ascontiguousarray(np.asarray(L0_ffn_w3_w, np.float32).T),
        "s1": np.ascontiguousarray(np.asarray(L0_ffn_w1_s, np.float32)),
        "s2": np.ascontiguousarray(np.asarray(L0_ffn_w2_s, np.float32)),
        "sh1": np.ascontiguousarray(np.asarray(L0_ffn_h1_s, np.float32)),
        "sh2": np.ascontiguousarray(np.asarray(L0_ffn_h2_s, np.float32)),
        "s3": np.ascontiguousarray(np.asarray(L0_ffn_w3_s, np.float32)),
    }
    nc = _get_nc()
    in_maps = [dict(common, x=np.ascontiguousarray(x[b])) for b in range(8)]
    res = run_bass_kernel_spmd(nc, in_maps, core_ids=list(range(8)))
    return np.stack([res.results[b]["y"] for b in range(8)], axis=0)


# version 20 (final)
# speedup vs baseline: 1.1417x; 1.0027x over previous
import numpy as np
import ml_dtypes
from contextlib import ExitStack

import concourse.bass as bass
import concourse.mybir as mybir
from concourse.bass_utils import run_bass_kernel_spmd

F32 = mybir.dt.float32
BF16 = mybir.dt.bfloat16
AF = mybir.ActivationFunctionType
ALU = mybir.AluOpType

T = 2048
E = 1024
HID = 4096
H = 16
D = 64
CH = 128
NT = 16
NE = 8
NHID = 32
NSPAN = 4
NSLAB = 2
LNEPS = 1e-5
ATEPS = 1e-6

ENGS = ("tensor", "vector", "scalar", "gpsimd", "sync")


class Sem:
    __slots__ = ("name", "n", "h")

    def __init__(self, name):
        self.name = name
        self.n = 0
        self.h = None


class Buf:
    __slots__ = ("name", "pw", "readers", "wave_open")

    def __init__(self, name):
        self.name = name
        self.pw = []
        self.readers = {}
        self.wave_open = False


class Prog:
    def __init__(self):
        self.ops = {e: [] for e in ENGS}
        self.lastw = {e: {} for e in ENGS}
        self.esem = {e: Sem("sem_" + e) for e in ENGS}
        self.qsem = {}
        self.nwaits = 0
        self.nops = 0
        self._pmr = []

    def dsem(self, name):
        if name not in self.qsem:
            self.qsem[name] = Sem(name)
        return self.qsem[name]

    def sems(self):
        return list(self.esem.values()) + list(self.qsem.values())

    def _wait(self, eng, mark):
        if mark is None:
            return
        sem, val = mark
        if val <= 0:
            return
        lw = self.lastw[eng]
        if lw.get(sem.name, 0) >= val:
            return
        lw[sem.name] = val
        self.nwaits += 1
        self.ops[eng].append(lambda e, s=sem, v=val: e.wait_ge(s.h, v))

    def op(self, eng, fn, reads=(), writes=(), wmode="full", dma=None,
           extra_waits=()):
        waits = list(extra_waits)
        for b in reads:
            waits.extend(b.pw)
        opening = []
        for b in writes:
            if wmode == "full" or not b.wave_open:
                waits.extend(b.pw)
                for s, v in b.readers.items():
                    waits.append((s, v))
                opening.append(b)
        wmax = {}
        for m in waits:
            if m is None:
                continue
            s, v = m
            if wmax.get(s.name, (None, 0))[1] < v:
                wmax[s.name] = (s, v)
        for s, v in wmax.values():
            self._wait(eng, (s, v))
        self.nops += 1
        if wmode == "mid":
            mark = None
            self.ops[eng].append(fn)
        else:
            sem = self.dsem(dma) if dma else self.esem[eng]
            inck = 16 if dma else 1
            sem.n += inck
            mark = (sem, sem.n)
            self.ops[eng].append(
                lambda e, fn=fn, s=sem, v=inck: fn(e).then_inc(s.h, v))
        for b in opening:
            b.pw = []
            b.readers = {}
            b.wave_open = True
        for b in writes:
            if mark is not None:
                if wmode == "full":
                    b.pw = [mark]
                else:
                    b.pw.append(mark)
        for b in reads:
            b.wave_open = False
            if mark is not None:
                s, v = mark
                if b.readers.get(s, 0) < v:
                    b.readers[s] = v
        if mark is None and reads:
            self._pmr.append((eng, list(reads)))
        elif mark is not None and self._pmr:
            rest = []
            for peng, bufs in self._pmr:
                if peng == eng:
                    s, v = mark
                    for b in bufs:
                        if b.readers.get(s, 0) < v:
                            b.readers[s] = v
                else:
                    rest.append((peng, bufs))
            self._pmr = rest
        return mark

    def seal(self, bufs):
        mx = {}
        for b in bufs:
            for s, v in b.pw:
                if mx.get(s.name, (None, 0))[1] < v:
                    mx[s.name] = (s, v)
        combined = list(mx.values())
        for b in bufs:
            b.pw = list(combined)

    def claim(self, news, olds):
        for nb in news:
            for ob in olds:
                nb.pw.extend(ob.pw)
                for s, v in ob.readers.items():
                    if nb.readers.get(s, 0) < v:
                        nb.readers[s] = v

    def replay(self, block):
        for ename in ENGS:
            ops = list(self.ops[ename])
            if ops:
                getattr(block, ename)(lambda eng, ops=ops: [f(eng) for f in ops])


def build_nc():
    nc = bass.Bass("TRN2", target_bir_lowering=False)

    x_d = nc.dram_tensor("x", [T, E], F32, kind="ExternalInput").ap()
    wq_d = nc.dram_tensor("wq", [E, E], F32, kind="ExternalInput").ap()
    wk_d = nc.dram_tensor("wk", [E, E], F32, kind="ExternalInput").ap()
    wv_d = nc.dram_tensor("wv", [E, E], F32, kind="ExternalInput").ap()
    wp_d = nc.dram_tensor("wp", [E, E], F32, kind="ExternalInput").ap()
    bp_d = nc.dram_tensor("bp", [E], F32, kind="ExternalInput").ap()
    w1_d = nc.dram_tensor("w1", [E, HID], F32, kind="ExternalInput").ap()
    w2_d = nc.dram_tensor("w2", [E, HID], F32, kind="ExternalInput").ap()
    h1_d = nc.dram_tensor("h1", [HID, HID], F32, kind="ExternalInput").ap()
    h2_d = nc.dram_tensor("h2", [HID, HID], F32, kind="ExternalInput").ap()
    w3_d = nc.dram_tensor("w3", [HID, E], F32, kind="ExternalInput").ap()
    s1_d = nc.dram_tensor("s1", [HID], F32, kind="ExternalInput").ap()
    s2_d = nc.dram_tensor("s2", [HID], F32, kind="ExternalInput").ap()
    sh1_d = nc.dram_tensor("sh1", [HID], F32, kind="ExternalInput").ap()
    sh2_d = nc.dram_tensor("sh2", [HID], F32, kind="ExternalInput").ap()
    s3_d = nc.dram_tensor("s3", [E], F32, kind="ExternalInput").ap()
    y_d = nc.dram_tensor("y", [T, E], F32, kind="ExternalOutput").ap()
    w1_s = nc.dram_tensor("w1_s", [E, HID], BF16, kind="Internal").ap()
    w2_s = nc.dram_tensor("w2_s", [E, HID], BF16, kind="Internal").ap()
    h1_s = nc.dram_tensor("h1_s", [HID, HID], BF16, kind="Internal").ap()
    h2_s = nc.dram_tensor("h2_s", [HID, HID], BF16, kind="Internal").ap()
    w3_s = nc.dram_tensor("w3_s", [HID, E], BF16, kind="Internal").ap()
    x2_d = nc.dram_tensor("x2_d", [T, E], F32, kind="Internal").ap()

    P = Prog()

    base = (nc.sbuf_base + 127) // 128 * 128
    off = [base]

    def alloc(name, shape, dt, at=None):
        if at is None:
            at = off[0]
            nbytes = int(np.prod(shape[1:])) * mybir.dt.size(dt)
            off[0] = (at + nbytes + 127) // 128 * 128
        t = nc.alloc_sbuf_tensor_at(name, shape, dt, offset=at)
        return t

    ident_bf = alloc("ident_bf", [128, 128], BF16)
    ident_f32 = alloc("ident_f32", [128, 128], F32)
    mask_f = alloc("mask_f", [128, 128], F32)
    ones_bf = alloc("ones_bf", [128, 16], BF16)
    ones_f1 = alloc("ones_f1", [1, 128], F32)
    BPt = alloc("BPt", [128, E], F32)
    S3t = alloc("S3t", [128, E], F32)
    sc_stage = alloc("sc_stage", [32, 128], F32)
    s1c = alloc("s1c", [128, 32], F32)
    s2c = alloc("s2c", [128, 32], F32)
    h1c = alloc("h1c", [128, 32], F32)
    h2c = alloc("h2c", [128, 32], F32)
    S32 = alloc("S32", [128, 8 * 65], F32)
    Sbf = alloc("Sbf", [128, 8 * 65], BF16)
    st_su = alloc("st_su", [128, 8], F32)
    st_ss = alloc("st_ss", [128, 8], F32)
    st_m = alloc("st_m", [128, 8], F32)
    st_m2 = alloc("st_m2", [128, 8], F32)
    st_v = alloc("st_v", [128, 8], F32)
    st_r = alloc("st_r", [128, 8], F32)
    st_nm = alloc("st_nm", [128, 8], F32)
    den_sb = alloc("den_sb", [128, 16], F32)
    rec_sb = alloc("rec_sb", [128, 16], F32)
    nd_sb = alloc("nd_sb", [128, 8 * 130], F32)
    epsc = alloc("epsc", [128, 8], F32)
    PERM_END = off[0]

    RX = PERM_END
    xslot = [alloc(f"xslot{i}", [128, E], F32, at=RX + i * 4096) for i in range(8)]
    htb = [alloc(f"htb{i}", [128, 8192], BF16, at=RX + i * 16384) for i in range(2)]
    outb = [alloc(f"outb{i}", [128, E], F32, at=RX + (2 + i) * 4096) for i in range(2)]
    RX_END = RX + 8 * 4096

    RS = RX_END
    scr0 = [alloc(f"scr0{i}", [128, E], F32, at=RS + i * 4096) for i in range(2)]
    bp_row = alloc("bp_row", [1, E], F32, at=RS)
    s3_row = alloc("s3_row", [1, E], F32, at=RS + 4096)
    scrT = alloc("scrT", [128, E], F32, at=RS + 8192)
    eluR = [alloc(f"eluR{i}", [128, 512], F32, at=RS + 12288 + i * 2048) for i in range(2)]
    eluM = [alloc(f"eluM{i}", [128, 512], F32, at=RS + 16384 + i * 2048) for i in range(2)]
    eluE = [alloc(f"eluE{i}", [128, 512], F32, at=RS + 20480 + i * 2048) for i in range(2)]
    scrS = [alloc(f"scrS{i}", [128, 512], F32, at=RS + 12288 + i * 2048) for i in range(2)]
    fo_t = [alloc(f"fo{i}", [128, E], BF16, at=RS + 16384 + i * 2048) for i in range(4)]
    RS_END = RS + 24576

    RB = RS_END
    wT = {}
    for i, nm in enumerate(("wq", "wk", "wv", "wp")):
        wT[nm] = alloc(nm + "T", [128, 8192], BF16, at=RB + i * 16384)
    ln0_fm = alloc("ln0_fm", [128, 4096], BF16, at=RB + 65536)
    wstage = alloc("wstage", [128, 4096], F32, at=RB + 65536)
    phiq_fm = alloc("phiq_fm", [128, 4096], BF16, at=RB + 65536 + 8192)
    phik_fm = alloc("phik_fm", [128, 4096], BF16, at=RB + 65536 + 16384)
    vT_sb = alloc("vT_sb", [128, 4096], BF16, at=RB + 65536 + 24576)
    y_fm = alloc("y_fm", [128, 4096], BF16, at=RB + 65536 + 32768)
    phikT = [alloc(f"phikT{i}", [128, 1024], BF16, at=RB + 106496 + i * 2048)
             for i in range(2)]
    y_tok = [alloc(f"y_tok{i}", [128, 1024], BF16, at=RB + 110592 + i * 2048)
             for i in range(2)]
    ATm = [alloc(f"ATm{i}", [128, 256], BF16, at=RB + 114688 + i * 512)
           for i in range(2)]
    ao_tok = [alloc(f"ao_tok{i}", [128, 1024], BF16, at=RB + 115712 + i * 2048)
              for i in range(2)]
    ln0_tok = [alloc(f"ln0_tok{i}", [128, 1024], BF16, at=RB + 119808 + i * 2048)
               for i in range(4)]
    h_fm = alloc("h_fm", [128, 32768], BF16, at=RB + 0)
    g2_fm = alloc("g2_fm", [128, 32768], BF16, at=RB + 0)
    g1_fm = alloc("g1_fm", [128, 32768], BF16, at=RB + 65536)
    w3T_sb = alloc("w3T_sb", [128, 32768], BF16, at=RB + 65536)
    ln2_fm = alloc("ln2_fm", [128, 8192], BF16, at=RB + 65536)
    ln2_tok = [alloc(f"ln2_tok{i}", [128, 1024], BF16, at=RB + 81920 + i * 2048)
               for i in range(8)]
    w12b = [alloc(f"w12b{i}", [128, 8192], BF16, at=RB + 98304 + i * 16384)
            for i in range(2)]
    RB_END = RB + 131072
    assert RB_END - base <= 212800, f"SBUF overflow: {RB_END - base}"

    psF = [nc.alloc_psum_tensor(f"psF{i}", [128, 512], F32) for i in range(4)]
    psG = [nc.alloc_psum_tensor(f"psG{i}", [128, 512], F32) for i in range(2)]
    psT = [nc.alloc_psum_tensor(f"psT{i}", [128, 1024], BF16) for i in range(2)]

    bF = [Buf(f"psF{i}") for i in range(4)]
    bG = [Buf(f"psG{i}") for i in range(2)]
    bT = [Buf(f"psT{i}") for i in range(2)]
    b_ident = Buf("ident_bf")
    b_identf = Buf("ident_f32")
    b_mask = Buf("mask")
    b_ones = Buf("ones_bf")
    b_ones1 = Buf("ones_f1")
    b_bprow = Buf("bp_row")
    b_s3row = Buf("s3_row")
    b_BP = Buf("BPt")
    b_S3 = Buf("S3t")
    b_scstage = Buf("sc_stage")
    b_s1c, b_s2c, b_h1c, b_h2c = Buf("s1c"), Buf("s2c"), Buf("h1c"), Buf("h2c")
    b_S32 = [Buf(f"S32_{i}") for i in range(8)]
    b_Sbf = [Buf(f"Sbf_{i}") for i in range(8)]
    b_stats = [Buf(f"stats{i}") for i in range(8)]
    b_den = Buf("den")
    b_rec = Buf("rec")
    b_ndsb = [Buf(f"ndsb{i}") for i in range(8)]
    b_epsc = Buf("epsc")
    b_fo = [Buf(f"fo{i}") for i in range(4)]
    b_xslot = [Buf(f"xslot{i}") for i in range(8)]
    b_htb = [Buf(f"htb{i}") for i in range(2)]
    b_outb = [Buf(f"outb{i}") for i in range(2)]
    b_scr0 = [Buf(f"scr0{i}") for i in range(2)]
    b_scrT = Buf("scrT")
    b_eluR = [Buf(f"eluR{i}") for i in range(2)]
    b_eluM = [Buf(f"eluM{i}") for i in range(2)]
    b_eluE = [Buf(f"eluE{i}") for i in range(2)]
    b_scrS = [Buf(f"scrS{i}") for i in range(2)]
    b_wT = {nm: [Buf(f"{nm}T{k}") for k in range(NE)] for nm in ("wq", "wk", "wv", "wp")}
    b_wstage = Buf("wstage")
    b_ln0fm = [Buf(f"ln0fm{c}") for c in range(NE)]
    b_phiq = [Buf(f"phiq{c}") for c in range(NE)]
    b_phik = [Buf(f"phik{c}") for c in range(NE)]
    b_vT = [Buf(f"vT{i}") for i in range(4)]
    b_yfm = [Buf(f"yfm{c}") for c in range(NE)]
    b_phikT = [Buf(f"phikT{i}") for i in range(2)]
    b_ytok = [Buf(f"ytok{i}") for i in range(2)]
    b_ATm = [Buf(f"ATm{i}") for i in range(2)]
    b_ao = [Buf(f"ao{i}") for i in range(2)]
    b_ln0tok = [Buf(f"ln0tok{i}") for i in range(4)]
    b_hfm = [Buf(f"hfm{k}") for k in range(NHID)]
    b_g1 = [Buf(f"g1_{k}") for k in range(NHID)]
    b_g2 = [Buf(f"g2_{k}") for k in range(NHID)]
    b_w3T = [Buf(f"w3T{k}") for k in range(NHID)]
    b_ln2fm = [Buf(f"ln2fm{c}") for c in range(NE)]
    b_ln2tok = [Buf(f"ln2tok{i}") for i in range(8)]
    b_w12b = [Buf(f"w12b{i}") for i in range(2)]
    b_ws = {nm: Buf(nm + "_s") for nm in ("w1", "w2", "h1", "h2", "w3")}
    b_x2d = [Buf(f"x2d{i}") for i in range(NT)]
    b_yd = [Buf(f"yd{i}") for i in range(NT)]

    ws_map = {"w1": (w1_d, w1_s), "w2": (w2_d, w2_s),
              "h1": (h1_d, h1_s), "h2": (h2_d, h2_s), "w3": (w3_d, w3_s)}

    P.op("gpsimd", lambda e: e.memset(ident_bf[:, :], 0.0), writes=[b_ident])
    P.op("gpsimd", lambda e: e.affine_select(
        out=ident_bf[:, :], in_=ident_bf[:, :], compare_op=ALU.not_equal,
        fill=1.0, base=0, pattern=[[-1, 128]], channel_multiplier=1),
        reads=[b_ident], writes=[b_ident])
    P.op("gpsimd", lambda e: e.memset(ident_f32[:, :], 0.0), writes=[b_identf])
    P.op("gpsimd", lambda e: e.affine_select(
        out=ident_f32[:, :], in_=ident_f32[:, :], compare_op=ALU.not_equal,
        fill=1.0, base=0, pattern=[[-1, 128]], channel_multiplier=1),
        reads=[b_identf], writes=[b_identf])
    P.op("gpsimd", lambda e: e.memset(mask_f[:, :], 0.0), writes=[b_mask])
    P.op("gpsimd", lambda e: e.affine_select(
        out=mask_f[:, :], in_=mask_f[:, :], compare_op=ALU.is_gt, fill=1.0,
        base=0, pattern=[[-1, 128]], channel_multiplier=1),
        reads=[b_mask], writes=[b_mask])
    P.op("gpsimd", lambda e: e.memset(ones_bf[:, :], 1.0), writes=[b_ones])
    P.op("gpsimd", lambda e: e.memset(ones_f1[:, :], 1.0), writes=[b_ones1])
    P.op("gpsimd", lambda e: e.memset(epsc[:, :], 100000.0), writes=[b_epsc])
    P.op("vector", lambda e: e.reciprocal(epsc[:, 0:1], epsc[:, 1:2]),
         reads=[b_epsc], writes=[b_epsc])
    P.op("gpsimd", lambda e: e.memset(S32[:, :], 0.0), writes=b_S32)
    P.op("gpsimd", lambda e: e.memset(Sbf[:, :], 0.0), writes=b_Sbf)


    def load_x(tt):
        sl = tt % 8
        P.op("sync", lambda e, tt=tt, sl=sl: e.dma_start(
            out=xslot[sl][:, :], in_=x_d[tt * 128:(tt + 1) * 128, :]),
            writes=[b_xslot[sl]], dma=f"qx{sl}")

    for tt in range(4):
        load_x(tt)
    P.op("sync", lambda e: e.dma_start(out=bp_row[:, :], in_=bp_d[None, :]),
         writes=[b_bprow], dma="qsc_bp")
    P.op("sync", lambda e: e.dma_start(out=s3_row[:, :], in_=s3_d[None, :]),
         writes=[b_s3row], dma="qsc_s3")
    for si, (sd, tb, bb) in enumerate(((s1_d, s1c, b_s1c), (s2_d, s2c, b_s2c),
                                       (sh1_d, h1c, b_h1c), (sh2_d, h2c, b_h2c))):
        P.op("sync", lambda e, sd=sd: e.dma_start(
            out=sc_stage[:, :], in_=sd.rearrange("(c p) -> c p", p=128)),
            writes=[b_scstage], dma=f"qsc_{si}")
        P.op("tensor", lambda e: e.transpose(
            psF[0][:, 0:32], sc_stage[:, :], ident_f32[0:32, 0:32]),
            reads=[b_scstage, b_identf], writes=[bF[0]])
        P.op("vector", lambda e, tb=tb: e.tensor_copy(tb[:, :], psF[0][:, 0:32]),
             reads=[bF[0]], writes=[bb])
    for row, tile, bt, brow in ((bp_row, BPt, b_BP, b_bprow),
                                (s3_row, S3t, b_S3, b_s3row)):
        for half in range(2):
            P.op("tensor", lambda e, row=row, half=half: e.matmul(
                psF[1][:, 0:512], ones_f1[:, :], row[:, half * 512:(half + 1) * 512],
                start=True, stop=True),
                reads=[b_ones1, brow], writes=[bF[1]])
            P.op("vector", lambda e, tile=tile, half=half: e.tensor_copy(
                tile[:, half * 512:(half + 1) * 512], psF[1][:, 0:512]),
                reads=[bF[1]], writes=[bt], wmode="part")

    P.claim([b_scr0[0]], [b_bprow])
    P.claim([b_scr0[1]], [b_s3row])

    def ln_stats_apply(src_tile, b_src, stat, b_stat, out_tile, b_out, sc,
                       apply_eng="vector"):
        i = stat
        P.op("scalar", lambda e: e.activation(
            scr0[sc][:, :], src_tile[:, :], AF.Square,
            accum_out=st_ss[:, i:i + 1]),
            reads=[b_src], writes=[b_scr0[sc], b_stat])
        P.op("scalar", lambda e: e.activation(
            scr0[sc][:, :], src_tile[:, :], AF.Copy,
            accum_out=st_su[:, i:i + 1]),
            reads=[b_src], writes=[b_scr0[sc], b_stat])
        P.op("vector", lambda e: e.tensor_scalar(
            st_m[:, i:i + 1], st_su[:, i:i + 1], 1.0 / E, None, ALU.mult),
            reads=[b_stat], writes=[b_stat])
        P.op("vector", lambda e: e.tensor_mul(
            st_m2[:, i:i + 1], st_m[:, i:i + 1], st_m[:, i:i + 1]),
            reads=[b_stat], writes=[b_stat])
        P.op("vector", lambda e: e.scalar_tensor_tensor(
            st_v[:, i:i + 1], st_ss[:, i:i + 1], 1.0 / E, st_m2[:, i:i + 1],
            ALU.mult, ALU.subtract),
            reads=[b_stat], writes=[b_stat])
        P.op("scalar", lambda e: e.activation(
            st_v[:, i:i + 1], st_v[:, i:i + 1], AF.Ln, bias=epsc[:, 0:1]),
            reads=[b_stat, b_epsc], writes=[b_stat])
        P.op("scalar", lambda e: e.activation(
            st_r[:, i:i + 1], st_v[:, i:i + 1], AF.Exp, scale=-0.5),
            reads=[b_stat], writes=[b_stat])
        P.op("vector", lambda e: e.scalar_tensor_tensor(
            st_nm[:, i:i + 1], st_m[:, i:i + 1], -1.0, st_r[:, i:i + 1],
            ALU.mult, ALU.mult),
            reads=[b_stat], writes=[b_stat])
        if out_tile is not None:
            P.op(apply_eng, lambda e: e.tensor_scalar(
                out_tile[:, :], src_tile[:, :], st_r[:, i:i + 1],
                st_nm[:, i:i + 1], ALU.mult, ALU.add),
                reads=[b_src, b_stat], writes=[b_out])

    def transpose_set(src_tiles, b_srcs, col0, dst, b_dst, dcol0, par, n_in=4):
        for j in range(n_in):
            P.op("tensor", lambda e, j=j, col0=col0, par=par: e.transpose(
                psT[par][:, j * 128:(j + 1) * 128],
                src_tiles[j][:, col0:col0 + 128], ident_bf[:, :]),
                reads=[b_srcs[j], b_ident], writes=[bT[par]], wmode="part")
        P.op("vector", lambda e, par=par, dcol0=dcol0, n_in=n_in: e.tensor_copy(
            dst[:, dcol0:dcol0 + n_in * 128], psT[par][:, 0:n_in * 128]),
            reads=[bT[par]], writes=[b_dst], wmode="part")

    def ln0_span(s):
        for tl in range(4):
            tt = 4 * s + tl
            sl = tt % 8
            ln_stats_apply(xslot[sl], b_xslot[sl], tl, b_stats[tl],
                           ln0_tok[tl], b_ln0tok[tl], tl % 2,
                           apply_eng="gpsimd")

    ln0_span(0)
    for nm, wd in (("wq", wq_d), ("wk", wk_d), ("wv", wv_d), ("wp", wp_d)):
        for rnd in range(2):
            ks = range(rnd * 4, rnd * 4 + 4)
            for j, k in enumerate(ks):
                P.op("sync", lambda e, wd=wd, k=k, j=j: e.dma_start(
                    out=wstage[:, j * 1024:(j + 1) * 1024],
                    in_=wd[k * 128:(k + 1) * 128, :]),
                    writes=[b_wstage], wmode=("part" if j else "full"),
                    dma=f"qw_{nm}")
            _qw = P.dsem(f"qw_{nm}")
            b_wstage.pw = [(_qw, _qw.n)]
            for j, k in enumerate(ks):
                P.op("vector", lambda e, nm=nm, k=k, j=j: e.tensor_copy(
                    wT[nm][:, k * 1024:(k + 1) * 1024],
                    wstage[:, j * 1024:(j + 1) * 1024]),
                    reads=[b_wstage], writes=[b_wT[nm][k]])
        P.seal(b_wT[nm])
        if nm == "wk":
            for tt in range(4, 8):
                load_x(tt)

    P.claim(b_ln0fm + b_phiq, [b_wstage])

    gate = [(P.dsem(f"qw_{nm}"), P.dsem(f"qw_{nm}").n)
            for nm in ("wq", "wk", "wv", "wp")]
    gate += [(P.dsem(f"qx{i}"), P.dsem(f"qx{i}").n) for i in range(8)]
    _first_cast = [True]

    def cast_weight(nm):
        src_, dst = ws_map[nm]
        rows = src_.shape[0]
        nchunk = 4 if rows * src_.shape[1] >= HID * HID else 1
        rr = rows // nchunk
        for j in range(nchunk):
            ew = gate if _first_cast[0] else ()
            _first_cast[0] = False
            P.op("gpsimd", lambda e, src_=src_, dst=dst, j=j, rr=rr: e.dma_start(
                out=dst[j * rr:(j + 1) * rr, :], in_=src_[j * rr:(j + 1) * rr, :]),
                writes=[b_ws[nm]], wmode=("part" if nchunk > 1 else "full"),
                dma="qc_" + nm, extra_waits=ew)

    for nm in ("w1", "w2", "h1", "h2", "w3"):
        cast_weight(nm)

    for s in range(NSPAN):
        for cp in range(4):
            par2 = cp % 2
            for half2 in range(2):
                c = 2 * cp + half2
                for j in range(4):
                    P.op("tensor", lambda e, j=j, c=c, half2=half2, par2=par2: e.transpose(
                        psT[par2][:, half2 * 512 + j * 128:half2 * 512 + (j + 1) * 128],
                        ln0_tok[j][:, c * 128:(c + 1) * 128], ident_bf[:, :]),
                        reads=[b_ln0tok[j], b_ident], writes=[bT[par2]], wmode="part")
            P.op("vector", lambda e, cp=cp, par2=par2: e.tensor_copy(
                ln0_fm[:, cp * 1024:(cp + 1) * 1024], psT[par2][:, :]),
                reads=[bT[par2]], writes=[b_ln0fm[2 * cp], b_ln0fm[2 * cp + 1]],
                wmode="part")
        P.seal(b_ln0fm)
        qbanks = [(psG[0], bG[0]), (psG[1], bG[1]), (psF[0], bF[0]), (psF[1], bF[1])]

        def proj_fm(wname, dst, b_dst, kind):
            for oc in range(NE):
                pg = oc % 2
                psq, bq = qbanks[oc % 4]
                for k in range(NE):
                    P.op("tensor", lambda e, wname=wname, oc=oc, k=k, psq=psq: e.matmul(
                        psq[:, 0:512],
                        wT[wname][:, k * 1024 + oc * 128:k * 1024 + oc * 128 + 128],
                        ln0_fm[:, k * 512:(k + 1) * 512],
                        start=(k == 0), stop=(k == NE - 1)),
                        reads=[b_wT[wname][k], b_ln0fm[k]], writes=[bq],
                        wmode=("full" if k == NE - 1 else "mid"))
                if kind == "phi":
                    P.op("scalar", lambda e, pg=pg, oc=oc, psq=psq: e.activation(
                        eluR[pg][:, :], psq[:, 0:512], AF.Relu),
                        reads=[bq], writes=[b_eluR[pg]])
                    P.op("vector", lambda e, pg=pg, psq=psq: e.tensor_sub(
                        eluM[pg][:, :], psq[:, 0:512], eluR[pg][:, :]),
                        reads=[bq, b_eluR[pg]], writes=[b_eluM[pg]])
                    P.op("scalar", lambda e, pg=pg: e.activation(
                        eluE[pg][:, :], eluM[pg][:, :], AF.Exp),
                        reads=[b_eluM[pg]], writes=[b_eluE[pg]])
                    P.op("vector", lambda e, pg=pg, oc=oc, dst=dst: e.tensor_tensor(
                        dst[:, oc * 512:(oc + 1) * 512], eluR[pg][:, :],
                        eluE[pg][:, :], ALU.add),
                        reads=[b_eluR[pg], b_eluE[pg]], writes=[b_dst[oc]])
                else:
                    P.op("vector", lambda e, pg=pg, oc=oc, dst=dst, psq=psq: e.tensor_copy(
                        dst[:, oc * 512:(oc + 1) * 512], psq[:, 0:512]),
                        reads=[bq], writes=[b_dst[oc]])

        proj_fm("wk", phik_fm, b_phik, "phi")
        P.seal(b_phik)
        proj_fm("wq", phiq_fm, b_phiq, "phi")
        P.seal(b_phiq)

        def v_proj(tl):
            for osp in range(2):
                if tl == 0:
                    psq, bq = qbanks[osp % 4 if False else (tl * 2 + osp) % 4]
                else:
                    psq, bq = psG[(tl * 2 + osp) % 2], bG[(tl * 2 + osp) % 2]
                for k in range(NE):
                    P.op("tensor", lambda e, tl=tl, osp=osp, k=k, psq=psq: e.matmul(
                        psq[:, 0:512],
                        ln0_fm[:, k * 512 + tl * 128:k * 512 + tl * 128 + 128],
                        wT["wv"][:, k * 1024 + osp * 512:k * 1024 + osp * 512 + 512],
                        start=(k == 0), stop=(k == NE - 1)),
                        reads=[b_ln0fm[k], b_wT["wv"][k]], writes=[bq],
                        wmode=("full" if k == NE - 1 else "mid"))
                P.op("vector", lambda e, tl=tl, osp=osp, psq=psq: e.tensor_copy(
                    vT_sb[:, tl * 1024 + osp * 512:tl * 1024 + osp * 512 + 512],
                    psq[:, 0:512]),
                    reads=[bq], writes=[b_vT[tl]], wmode="part")

        v_proj(0)

        def y_transposes(nl):
            parx = nl % 2
            for c in range(NE):
                P.op("tensor", lambda e, c=c, parx=parx: e.transpose(
                    psT[parx][:, c * 128:(c + 1) * 128],
                    y_tok[parx][:, c * 128:(c + 1) * 128], ident_bf[:, :]),
                    reads=[b_ytok[parx], b_ident], writes=[bT[parx]], wmode="part")
            for c in range(NE):
                P.op("vector", lambda e, c=c, parx=parx, nl=nl: e.tensor_copy(
                    y_fm[:, c * 512 + nl * 128:c * 512 + nl * 128 + 128],
                    psT[parx][:, c * 128:(c + 1) * 128]),
                    reads=[bT[parx]], writes=[b_yfm[c]], wmode="part")

        for nl in range(4):
            par = nl % 2
            def emit_AT(hp):
                for hi in range(2):
                    fb = bF[hi]
                    pb = 64 * hi
                    P.op("tensor", lambda e, hp=hp, hi=hi, nl=nl, pb=pb: e.matmul(
                        psF[hi][:, 0:128],
                        phik_fm[pb:pb + 64, hp * 512 + nl * 128:hp * 512 + nl * 128 + 128],
                        phiq_fm[pb:pb + 64, hp * 512 + nl * 128:hp * 512 + nl * 128 + 128],
                        start=True, stop=True),
                        reads=[b_phik[hp], b_phiq[hp]], writes=[fb])
                    P.op("vector", lambda e, hp=hp, hi=hi: e.tensor_tensor(
                        ATm[hp % 2][:, hi * 128:(hi + 1) * 128],
                        psF[hi][:, 0:128], mask_f[:, :], ALU.mult),
                        reads=[fb, b_mask], writes=[b_ATm[hp % 2]], wmode="part")

            emit_AT(0)
            for hp in range(8):
                fn_ = bF[2 + hp % 2]
                if hp + 1 < 8:
                    emit_AT(hp + 1)
                for hi in range(2):
                    h = 2 * hp + hi
                    pb = 64 * hi
                    bs = hi * 65
                    P.op("tensor", lambda e, hp=hp, hi=hi, h=h, nl=nl, bs=bs: e.matmul(
                        psF[2 + hp % 2][:, bs:bs + 64],
                        ATm[hp % 2][:, hi * 128:(hi + 1) * 128],
                        vT_sb[:, nl * 1024 + h * 64:nl * 1024 + h * 64 + 64],
                        start=(hi == 0), stop=False),
                        reads=[b_ATm[hp % 2], b_vT[nl]], writes=[fn_], wmode="mid")
                    P.op("tensor", lambda e, hp=hp, hi=hi, bs=bs: e.matmul(
                        psF[2 + hp % 2][:, bs + 64:bs + 65],
                        ATm[hp % 2][:, hi * 128:(hi + 1) * 128],
                        ones_bf[:, 0:1], start=False, stop=False),
                        reads=[b_ATm[hp % 2], b_ones], writes=[fn_], wmode="mid")
                    P.op("tensor", lambda e, hp=hp, hi=hi, nl=nl, pb=pb, bs=bs: e.matmul(
                        psF[2 + hp % 2][:, bs:bs + 65],
                        phiq_fm[pb:pb + 64, hp * 512 + nl * 128:hp * 512 + nl * 128 + 128],
                        Sbf[pb:pb + 64, hp * 65:hp * 65 + 65],
                        start=False, stop=(hi == 1)),
                        reads=[b_phiq[hp], b_Sbf[hp]], writes=[fn_],
                        wmode=("full" if hi == 1 else "mid"))
                P.op("scalar", lambda e, hp=hp: e.activation(
                    nd_sb[:, hp * 130:(hp + 1) * 130], psF[2 + hp % 2][:, 0:130],
                    AF.Copy),
                    reads=[fn_], writes=[b_ndsb[hp]])
            for c in range(NE):
                P.op("tensor", lambda e, c=c, nl=nl, par=par: e.transpose(
                    psT[par][:, c * 128:(c + 1) * 128],
                    phik_fm[:, c * 512 + nl * 128:c * 512 + nl * 128 + 128],
                    ident_bf[:, :]),
                    reads=[b_phik[c], b_ident], writes=[bT[par]], wmode="part")
            P.op("vector", lambda e, par=par: e.tensor_copy(
                phikT[par][:, :], psT[par][:, :]),
                reads=[bT[par]], writes=[b_phikT[par]])
            for hp in range(8):
                gb = bG[hp % 2]
                for hi in range(2):
                    h = 2 * hp + hi
                    pb = 64 * hi
                    P.op("tensor", lambda e, h=h, hp=hp, hi=hi, nl=nl, pb=pb, par=par: e.matmul(
                        psG[hp % 2][pb:pb + 64, 0:64],
                        phikT[par][:, h * 64:h * 64 + 64],
                        vT_sb[:, nl * 1024 + h * 64:nl * 1024 + h * 64 + 64],
                        start=True, stop=False),
                        reads=[b_phikT[par], b_vT[nl]], writes=[gb], wmode="mid")
                    P.op("tensor", lambda e, h=h, hp=hp, hi=hi, pb=pb, par=par: e.matmul(
                        psG[hp % 2][pb:pb + 64, 64:65],
                        phikT[par][:, h * 64:h * 64 + 64],
                        ones_bf[:, 0:1], start=False, stop=True),
                        reads=[b_phikT[par], b_ones], writes=[gb],
                        wmode="part")
                P.op("vector", lambda e, hp=hp: e.tensor_add(
                    S32[:, hp * 65:hp * 65 + 65], S32[:, hp * 65:hp * 65 + 65],
                    psG[hp % 2][:, 0:65]),
                    reads=[b_S32[hp], gb], writes=[b_S32[hp]])
                P.op("scalar", lambda e, hp=hp: e.activation(
                    Sbf[:, hp * 65:hp * 65 + 65], S32[:, hp * 65:hp * 65 + 65],
                    AF.Copy),
                    reads=[b_S32[hp]], writes=[b_Sbf[hp]])
            if nl + 1 < 4:
                v_proj(nl + 1)
            if nl > 0:
                y_transposes(nl - 1)
            dsrc = bass.AP(nd_sb, 64, [[8 * 130, 128], [130, 8], [65, 2]])
            P.op("scalar", lambda e, dsrc=dsrc: e.activation(
                den_sb[:, 0:16], dsrc, AF.Copy, bias=ATEPS),
                reads=b_ndsb, writes=[b_den])
            P.op("vector", lambda e: e.reciprocal(rec_sb[:, 0:16], den_sb[:, 0:16]),
                 reads=[b_den], writes=[b_rec])
            for h in range(16):
                hp, hi = h // 2, h % 2
                P.op("scalar", lambda e, h=h, hp=hp, hi=hi, par=par: e.activation(
                    y_tok[par][:, h * 64:h * 64 + 64],
                    nd_sb[:, hp * 130 + hi * 65:hp * 130 + hi * 65 + 64], AF.Copy,
                    scale=rec_sb[:, h:h + 1]),
                    reads=[b_ndsb[hp], b_rec], writes=[b_ytok[par]], wmode="part")
        y_transposes(3)
        P.seal(b_yfm)
        if s + 1 < NSPAN:
            ln0_span(s + 1)

        for tl in range(4):
            tt = 4 * s + tl
            sl = tt % 8
            for osp in range(2):
                psq, bq = qbanks[(tl * 2 + osp) % 4]
                for k in range(NE):
                    P.op("tensor", lambda e, tl=tl, osp=osp, k=k, psq=psq: e.matmul(
                        psq[:, 0:512],
                        y_fm[:, k * 512 + tl * 128:k * 512 + tl * 128 + 128],
                        wT["wp"][:, k * 1024 + osp * 512:k * 1024 + osp * 512 + 512],
                        start=(k == 0), stop=(k == NE - 1)),
                        reads=[b_yfm[k], b_wT["wp"][k]], writes=[bq],
                        wmode=("full" if k == NE - 1 else "mid"))
                P.op("vector", lambda e, tl=tl, osp=osp, psq=psq: e.tensor_tensor(
                    ao_tok[tl % 2][:, osp * 512:(osp + 1) * 512],
                    psq[:, 0:512], BPt[:, osp * 512:(osp + 1) * 512], ALU.add),
                    reads=[bq, b_BP], writes=[b_ao[tl % 2]], wmode="part")
            ln_stats_apply(ao_tok[tl % 2], b_ao[tl % 2], 4 + (tl % 2),
                           b_stats[4 + (tl % 2)], None, None, tl % 2)
            i = 4 + (tl % 2)
            P.op("vector", lambda e, tl=tl, i=i: e.tensor_scalar(
                scrT[:, :], ao_tok[tl % 2][:, :], st_r[:, i:i + 1],
                st_nm[:, i:i + 1], ALU.mult, ALU.add),
                reads=[b_ao[tl % 2], b_stats[i]], writes=[b_scrT])
            P.op("vector", lambda e, sl=sl: e.tensor_add(
                xslot[sl][:, :], xslot[sl][:, :], scrT[:, :]),
                reads=[b_xslot[sl], b_scrT], writes=[b_xslot[sl]])
            P.op("scalar", lambda e, tt=tt, sl=sl: e.dma_start(
                out=x2_d[tt * 128:(tt + 1) * 128, :], in_=xslot[sl][:, :]),
                reads=[b_xslot[sl]], writes=[b_x2d[tt]], dma=f"qx2s{sl}")
        if 4 * (s + 2) < NT:
            for tt in range(4 * (s + 2), 4 * (s + 2) + 4):
                load_x(tt)

    for tt in range(NT):
        _qs = P.dsem(f"qx2s{tt % 8}")
        b_x2d[tt].pw = [(_qs, _qs.n)]

    P.claim(b_hfm, sum([b_wT[nm] for nm in ("wq", "wk", "wv", "wp")], []))
    P.claim(b_fo, [b_eluM[0], b_eluM[1], b_eluE[0], b_eluE[1]])
    P.claim(b_ln2fm + b_ln2tok + b_w12b + [b_scrS[0], b_scrS[1]],
            b_ln0fm + b_phiq + b_phik + [b_vT[i] for i in range(4)] +
            b_yfm + b_phikT + b_ytok + b_ATm + b_ao + b_ln0tok +
            [b_eluR[0], b_eluR[1], b_eluM[0], b_eluM[1], b_eluE[0], b_eluE[1]])
    P.claim(b_htb + b_outb, b_xslot)

    x2ld_total = P.dsem("qx2s")

    for sl in range(NSLAB):
        ts0 = sl * 8
        for tl in range(8):
            tt = ts0 + tl
            slx = tl % 2
            P.op("sync", lambda e, tt=tt, slx=slx: e.dma_start(
                out=xslot[slx][:, :], in_=x2_d[tt * 128:(tt + 1) * 128, :]),
                reads=[b_x2d[tt]], writes=[b_xslot[slx]], dma=f"qx2l{slx}")
            ln_stats_apply(xslot[slx], b_xslot[slx], tl % 8, b_stats[tl % 8],
                           ln2_tok[tl], b_ln2tok[tl], tl % 2)
        for c in range(NE):
            par = c % 2
            for j in range(8):
                P.op("tensor", lambda e, c=c, j=j, par=par: e.transpose(
                    psT[par][:, j * 128:(j + 1) * 128],
                    ln2_tok[j][:, c * 128:(c + 1) * 128], ident_bf[:, :]),
                    reads=[b_ln2tok[j], b_ident], writes=[bT[par]], wmode="part")
            P.op("vector", lambda e, c=c, par=par: e.tensor_copy(
                ln2_fm[:, c * 1024:(c + 1) * 1024], psT[par][:, :]),
                reads=[bT[par]], writes=[b_ln2fm[c]])
        P.seal(b_ln2fm)

        def load_w12(osp):
            wb = osp % 2
            P.op("sync", lambda e, osp=osp, wb=wb: e.dma_start(
                out=w12b[wb][:, 0:4096].rearrange("p (k o) -> p k o", k=NE),
                in_=w1_s[:, osp * 512:(osp + 1) * 512]
                .rearrange("(k p) o -> p k o", p=128)),
                reads=[b_ws["w1"]], writes=[b_w12b[wb]], dma=f"qw12{wb}")
            P.op("sync", lambda e, osp=osp, wb=wb: e.dma_start(
                out=w12b[wb][:, 4096:8192].rearrange("p (k o) -> p k o", k=NE),
                in_=w2_s[:, osp * 512:(osp + 1) * 512]
                .rearrange("(k p) o -> p k o", p=128)),
                reads=[b_ws["w2"]], writes=[b_w12b[wb]], wmode="part",
                dma=f"qw12{wb}")

        load_w12(0)
        load_w12(1)
        for osp in range(8):
            wb = osp % 2
            for oc4 in range(4):
                oc = osp * 4 + oc4
                for tc in range(2):
                    pa = 2 * ((oc4 * 2 + tc) % 2)
                    for k in range(NE):
                        P.op("tensor", lambda e, oc4=oc4, tc=tc, k=k, pa=pa, wb=wb: e.matmul(
                            psF[pa][:, 0:512],
                            w12b[wb][:, k * 512 + oc4 * 128:k * 512 + oc4 * 128 + 128],
                            ln2_fm[:, k * 1024 + tc * 512:k * 1024 + tc * 512 + 512],
                            start=(k == 0), stop=(k == NE - 1)),
                            reads=[b_w12b[wb], b_ln2fm[k]], writes=[bF[pa]],
                            wmode=("full" if k == NE - 1 else "mid"))
                    for k in range(NE):
                        P.op("tensor", lambda e, oc4=oc4, tc=tc, k=k, pa=pa, wb=wb: e.matmul(
                            psF[pa + 1][:, 0:512],
                            w12b[wb][:, 4096 + k * 512 + oc4 * 128:4096 + k * 512 + oc4 * 128 + 128],
                            ln2_fm[:, k * 1024 + tc * 512:k * 1024 + tc * 512 + 512],
                            start=(k == 0), stop=(k == NE - 1)),
                            reads=[b_w12b[wb], b_ln2fm[k]], writes=[bF[pa + 1]],
                            wmode=("full" if k == NE - 1 else "mid"))
                    sp = (oc4 * 2 + tc) % 2
                    P.op("scalar", lambda e, oc=oc, pa=pa, sp=sp: e.activation(
                        scrS[sp][:, :], psF[pa][:, 0:512], AF.Silu,
                        scale=s1c[:, oc:oc + 1]),
                        reads=[bF[pa], b_s1c], writes=[b_scrS[sp]])
                    P.op("vector", lambda e, oc=oc, tc=tc, pa=pa, sp=sp: e.scalar_tensor_tensor(
                        h_fm[:, oc * 1024 + tc * 512:oc * 1024 + tc * 512 + 512],
                        scrS[sp][:, :], s2c[:, oc:oc + 1], psF[pa + 1][:, 0:512],
                        ALU.mult, ALU.mult),
                        reads=[b_scrS[sp], b_s2c, bF[pa + 1]], writes=[b_hfm[oc]],
                        wmode="part")
            if osp + 2 < 8:
                load_w12(osp + 2)
        P.seal(b_hfm)

        def hidden_stage(w_s, b_wsrc, scol, b_scol, src_bufs, src_t, dst, b_dst):
            def load_span(osp):
                wb2 = osp % 2
                P.op("sync", lambda e, osp=osp, wb2=wb2, w_s=w_s: e.dma_start(
                    out=htb[wb2][:, :].rearrange("p (k o) -> p k o", k=NHID),
                    in_=w_s[:, osp * 256:(osp + 1) * 256]
                    .rearrange("(k p) o -> p k o", p=128)),
                    reads=[b_wsrc], writes=[b_htb[wb2]], dma=f"qht{wb2}")

            load_span(0)
            load_span(1)
            for osp in range(16):
                wb2 = osp % 2
                for oc2 in range(2):
                    oc = osp * 2 + oc2
                    for tc in range(2):
                        pa = (oc2 * 2 + tc) % 4
                        for k in range(NHID):
                            P.op("tensor", lambda e, oc2=oc2, tc=tc, k=k, pa=pa, wb2=wb2: e.matmul(
                                psF[pa][:, 0:512],
                                htb[wb2][:, k * 256 + oc2 * 128:k * 256 + oc2 * 128 + 128],
                                src_t[:, k * 1024 + tc * 512:k * 1024 + tc * 512 + 512],
                                start=(k == 0), stop=(k == NHID - 1)),
                                reads=[b_htb[wb2], src_bufs[k]], writes=[bF[pa]],
                                wmode=("full" if k == NHID - 1 else "mid"))
                        P.op("scalar", lambda e, oc=oc, tc=tc, pa=pa: e.activation(
                            dst[:, oc * 1024 + tc * 512:oc * 1024 + tc * 512 + 512],
                            psF[pa][:, 0:512], AF.Gelu, scale=scol[:, oc:oc + 1]),
                            reads=[bF[pa], b_scol], writes=[b_dst[oc]], wmode="part")
                if osp + 2 < 16:
                    load_span(osp + 2)

        P.claim(b_g1, b_ln2fm + b_ln2tok + b_w12b)
        P.claim(b_htb, b_xslot + b_outb)
        hidden_stage(h1_s, b_ws["h1"], h1c, b_h1c, b_hfm, h_fm, g1_fm, b_g1)
        P.seal(b_g1)
        P.claim(b_g2, b_hfm)
        hidden_stage(h2_s, b_ws["h2"], h2c, b_h2c, b_g1, g1_fm, g2_fm, b_g2)
        P.seal(b_g2)

        P.seal(b_g1)
        P.claim(b_w3T, b_g1)
        for hw3 in range(2):
            P.op("sync", lambda e, hw3=hw3: e.dma_start(
                out=w3T_sb[:, hw3 * 16384:(hw3 + 1) * 16384]
                .rearrange("p (k o) -> p k o", k=NHID // 2),
                in_=w3_s[hw3 * 2048:(hw3 + 1) * 2048, :]
                .rearrange("(k p) o -> p k o", p=128)),
                reads=[b_ws["w3"]], writes=b_w3T[hw3 * 16:(hw3 + 1) * 16],
                dma=f"qw3{hw3}")
        P.claim([b_xslot[0], b_xslot[1]], b_htb)
        P.claim(b_outb, b_htb)
        for half in range(2):
            loop = ([(osp, tl) for osp in range(4) for tl in range(4)] if half == 0
                    else [(osp, tl) for tl in range(4) for osp in range(4)])
            for idx, (osp, tl) in enumerate(loop):
                    gt = half * 4 + tl
                    pa = idx % 4
                    for k in range(NHID):
                        P.op("tensor", lambda e, gt=gt, osp=osp, k=k, pa=pa: e.matmul(
                            psF[pa][:, 0:256],
                            g2_fm[:, k * 1024 + gt * 128:k * 1024 + gt * 128 + 128],
                            w3T_sb[:, k * 1024 + osp * 256:k * 1024 + osp * 256 + 256],
                            start=(k == 0), stop=(k == NHID - 1)),
                            reads=[b_g2[k], b_w3T[k]], writes=[bF[pa]],
                            wmode=("full" if k == NHID - 1 else "mid"))
                    P.op("vector", lambda e, tl=tl, osp=osp, pa=pa: e.tensor_tensor(
                        fo_t[tl][:, osp * 256:(osp + 1) * 256], psF[pa][:, 0:256],
                        S3t[:, osp * 256:(osp + 1) * 256], ALU.mult),
                        reads=[bF[pa], b_S3], writes=[b_fo[tl]], wmode="part")
            for tl in range(4):
                gt = half * 4 + tl
                tt = ts0 + gt
                slx = tl % 2
                P.op("sync", lambda e, tt=tt, slx=slx: e.dma_start(
                    out=xslot[slx][:, :], in_=x2_d[tt * 128:(tt + 1) * 128, :]),
                    reads=[b_x2d[tt]], writes=[b_xslot[slx]], dma=f"qx2l{slx}")
                ln_stats_apply(fo_t[tl], b_fo[tl], tl % 8, b_stats[tl % 8],
                               None, None, tl % 2)
                i = tl % 8
                P.op("vector", lambda e, tl=tl, i=i: e.tensor_scalar(
                    scrT[:, :], fo_t[tl][:, :], st_r[:, i:i + 1],
                    st_nm[:, i:i + 1], ALU.mult, ALU.add),
                    reads=[b_fo[tl], b_stats[i]], writes=[b_scrT])
                ob = tl % 2
                P.op("vector", lambda e, slx=slx, ob=ob: e.tensor_add(
                    outb[ob][:, :], xslot[slx][:, :], scrT[:, :]),
                    reads=[b_xslot[slx], b_scrT], writes=[b_outb[ob]])
                P.op("gpsimd", lambda e, tt=tt, ob=ob: e.dma_start(
                    out=y_d[tt * 128:(tt + 1) * 128, :], in_=outb[ob][:, :]),
                    reads=[b_outb[ob]], writes=[b_yd[tt]], dma=f"qy{ob}")

        if sl + 1 < NSLAB:
            P.claim(b_hfm, b_g2)
            P.claim(b_ln2fm + b_ln2tok + b_w12b, b_w3T)
            P.claim([b_xslot[0], b_xslot[1]], b_htb + b_outb)

    P.op("gpsimd", lambda e: e.nop(),
         extra_waits=[(P.dsem("qy0"), P.dsem("qy0").n),
                      (P.dsem("qy1"), P.dsem("qy1").n)])

    with ExitStack() as sstk:
        for s in P.sems():
            s.h = sstk.enter_context(nc.semaphore(s.name))
        with nc.Block() as block:
            P.replay(block)
    return nc


_NC_CACHE = []


def _get_nc():
    if not _NC_CACHE:
        _NC_CACHE.append(build_nc())
    return _NC_CACHE[0]


def kernel(L0_attn_x, L0_attn_Wq, L0_attn_Wk, L0_attn_Wv, L0_attn_Wp,
           L0_attn_bp, L0_ffn_w1_w, L0_ffn_w1_s, L0_ffn_w2_w, L0_ffn_w2_s,
           L0_ffn_h1_w, L0_ffn_h1_s, L0_ffn_h2_w, L0_ffn_h2_s, L0_ffn_w3_w,
           L0_ffn_w3_s):
    x = np.ascontiguousarray(np.asarray(L0_attn_x, dtype=np.float32))
    common = {
        "wq": np.ascontiguousarray(np.asarray(L0_attn_Wq, np.float32).T),
        "wk": np.ascontiguousarray(np.asarray(L0_attn_Wk, np.float32).T),
        "wv": np.ascontiguousarray(np.asarray(L0_attn_Wv, np.float32).T),
        "wp": np.ascontiguousarray(np.asarray(L0_attn_Wp, np.float32).T),
        "bp": np.ascontiguousarray(np.asarray(L0_attn_bp, np.float32)),
        "w1": np.ascontiguousarray(np.asarray(L0_ffn_w1_w, np.float32).T),
        "w2": np.ascontiguousarray(np.asarray(L0_ffn_w2_w, np.float32).T),
        "h1": np.ascontiguousarray(np.asarray(L0_ffn_h1_w, np.float32).T),
        "h2": np.ascontiguousarray(np.asarray(L0_ffn_h2_w, np.float32).T),
        "w3": np.ascontiguousarray(np.asarray(L0_ffn_w3_w, np.float32).T),
        "s1": np.ascontiguousarray(np.asarray(L0_ffn_w1_s, np.float32)),
        "s2": np.ascontiguousarray(np.asarray(L0_ffn_w2_s, np.float32)),
        "sh1": np.ascontiguousarray(np.asarray(L0_ffn_h1_s, np.float32)),
        "sh2": np.ascontiguousarray(np.asarray(L0_ffn_h2_s, np.float32)),
        "s3": np.ascontiguousarray(np.asarray(L0_ffn_w3_s, np.float32)),
    }
    nc = _get_nc()
    in_maps = [dict(common, x=np.ascontiguousarray(x[b])) for b in range(8)]
    res = run_bass_kernel_spmd(nc, in_maps, core_ids=list(range(8)))
    return np.stack([res.results[b]["y"] for b in range(8)], axis=0)
